# version 12
# baseline (speedup 1.0000x reference)
import numpy as np
import ml_dtypes
from contextlib import ExitStack

import concourse.bass as bass
import concourse.mybir as mybir
from concourse.bass_utils import run_bass_kernel_spmd

F32 = mybir.dt.float32
BF16 = mybir.dt.bfloat16
ALU = mybir.AluOpType
ACT = mybir.ActivationFunctionType

D = 1024
H = 16
HD = 64
DCH = 8
TKV = 2304
NT = 18
TQ = 2048
E, NS, HID = 16, 16, 256
ES = 256
LN_EPS = 1e-5
MASKVAL = -10000.0

STAGE = 99
DBG = ()


def qws(c):
    return min(max(0, 128 * c - 128), TKV - 384)


def tset(c):
    return [t for t in (c - 1, c, c + 1) if 1 <= t <= 16]


class Sem:
    def __init__(self, stack, nc, name):
        self.h = stack.enter_context(nc.semaphore(name))
        self.n = 0

    def wait(self, eng, n=None):
        eng.wait_ge(self.h, self.n if n is None else n)


class Ring:
    def __init__(self, nslots):
        self.rel = [[] for _ in range(nslots)]

    def acquire(self, eng, slot):
        for sem, val in self.rel[slot]:
            eng.wait_ge(sem.h, val)

    def waits(self, slot):
        return tuple((sem.h, val) for sem, val in self.rel[slot])

    def release(self, slot, sem):
        self.rel[slot] = [(sem, sem.n)]

    def release_add(self, slot, sem):
        self.rel[slot].append((sem, sem.n))


class SBat:

    def __init__(self):
        self.jobs = []

    def add(self, fn):
        self.jobs.append(fn)

    def flush(self, eS):
        if not self.jobs:
            return
        jobs = tuple(self.jobs)
        self.jobs = []

        def _(s, jobs=jobs):
            for fn in jobs:
                fn(s)
        eS(_)


def build_program(stage=STAGE, dbg=DBG):
    nc = bass.Bass("TRN2", target_bir_lowering=False)
    st = ExitStack()

    def din(name, shape, dt=F32):
        return nc.dram_tensor(name, shape, dt, kind="ExternalInput").ap()

    xp = din("xp", [TKV, D])
    wq = din("wq", [128, DCH * 3 * D])
    wo = din("wo", [128, DCH * D])
    muh = din("muh", [128, DCH * ES])
    w1h = din("w1h", [128, E * DCH * HID])
    w2h = din("w2h", [128, E * 2 * D])
    b1r = din("b1r", [1, E * HID])
    b2r = din("b2r", [1, E * D])
    bor = din("bor", [1, D])
    gbc = din("gbc", [128, D])
    bbc = din("bbc", [128, D])
    scbc = din("scbc", [128, 1])
    cosr = din("cosr", [128, TKV])
    sinr = din("sinr", [128, TKV])
    maskh = din("maskh", [128, NT * 384])
    identb = din("identb", [128, 128], BF16)
    permb = din("permb", [128, 128], BF16)
    onesc = din("onesc", [128, 1], BF16)
    onesr = din("onesr", [1, 128], BF16)
    onesrf = din("onesrf", [1, 128])

    out = nc.dram_tensor("out", [TQ, D], F32, kind="ExternalOutput").ap()
    x2sp = nc.dram_tensor("x2sp", [TQ, D], F32).ap()
    ccin = nc.dram_tensor("ccin", [128, 2050], F32)
    ccout = nc.dram_tensor("ccout", [128, 2050], F32)

    dbg_shapes = {
        "xln": [TKV, D], "qT": [128, DCH * TKV], "kT": [128, DCH * TKV],
        "vtm": [128, NT * D], "attn": [128, DCH * TQ], "x2": [TQ, D],
        "xn": [128, 16 * 1056], "ee": [128, 16 * ES], "comb": [128, 16 * ES],
        "usb": [128, 2050], "slin": [128, 2 * D], "slout": [128, 2 * D],
    }
    dbg_out = {k: nc.dram_tensor("dbg_" + k, dbg_shapes[k], F32, kind="ExternalOutput").ap()
               for k in dbg}

    def sb(name, shape, dt):
        return st.enter_context(nc.sbuf_tensor(name, shape, dt))

    ident_sb = sb("ident_sb", [128, 128], BF16)
    perm_sb = sb("perm_sb", [128, 128], BF16)
    onesc_sb = sb("onesc_sb", [128, 1], BF16)
    onesr_sb = sb("onesr_sb", [1, 128], BF16)
    onesrf_sb = sb("onesrf_sb", [1, 128], F32)
    pA = ExitStack()
    qT = pA.enter_context(nc.sbuf_tensor("qT", [128, DCH * TKV], BF16))
    kT = pA.enter_context(nc.sbuf_tensor("kT", [128, DCH * TKV], BF16))
    vtm = pA.enter_context(nc.sbuf_tensor("vtm", [128, NT * 1040], BF16))

    names = ["t2s", "t2v", "s2t", "v2t", "s2v", "v2s", "v2y", "cc", "v2g", "g2t", "sacc",
             "dcs", "dcg", "ccd", "ddb",
             "dlx0", "dlx1", "dxr0", "dxr1", "dsp0", "dsp1",
             "dxc0", "dxc1", "dxc2", "dxc3", "dsg0", "dsg1",
             "dw20", "dw21", "dw22", "dw23",
             "dx60", "dx61", "dso0", "dso1"]
    sm = {n: Sem(st, nc, "sm_" + n) for n in names}
    t2s, t2v, s2t, v2t = sm["t2s"], sm["t2v"], sm["s2t"], sm["v2t"]
    s2v, v2s, v2y, ccS = sm["s2v"], sm["v2s"], sm["v2y"], sm["cc"]
    dcs, dcg, ccd, ddb = sm["dcs"], sm["dcg"], sm["ccd"], sm["ddb"]
    v2g, g2t, sacc = sm["v2g"], sm["g2t"], sm["sacc"]
    dlx = [sm["dlx0"], sm["dlx1"]]
    dxr = [sm["dxr0"], sm["dxr1"]]
    dsp = [sm["dsp0"], sm["dsp1"]]
    dxc = [sm["dxc0"], sm["dxc1"], sm["dxc2"], sm["dxc3"]]
    dw2 = [sm["dw20"], sm["dw21"], sm["dw22"], sm["dw23"]]
    dsg = [sm["dsg0"], sm["dsg1"]]
    dx6 = [sm["dx60"], sm["dx61"]]
    dso = [sm["dso0"], sm["dso1"]]

    def dma(eng, sem, out_ap, in_ap):
        eng.dma_start(out=out_ap, in_=in_ap).then_inc(sem.h, 16)
        sem.n += 16

    def fence(blk):
        vals = [(s.h, s.n) for s in sm.values() if s.n > 0]

        def _f(eng):
            for hh, nn in vals:
                eng.wait_ge(hh, nn)
        blk.tensor(_f)
        blk.vector(_f)
        blk.scalar(_f)
        blk.gpsimd(_f)
        blk.sync(_f)

    p1 = ExitStack()
    wq_sb = p1.enter_context(nc.sbuf_tensor("wq_sb", [128, DCH * 3 * D], BF16))
    cos_sb = p1.enter_context(nc.sbuf_tensor("cos_sb", [128, TKV], BF16))
    sin_sb = p1.enter_context(nc.sbuf_tensor("sin_sb", [128, TKV], BF16))
    xtile = p1.enter_context(nc.sbuf_tensor("xtile", [128, 2 * D], F32))
    xln = p1.enter_context(nc.sbuf_tensor("xln", [128, 4 * D], BF16))
    xlnT = p1.enter_context(nc.sbuf_tensor("xlnT", [128, DCH * 512], BF16))
    stat6 = p1.enter_context(nc.sbuf_tensor("stat6", [128, 48], F32))
    mstat = p1.enter_context(nc.sbuf_tensor("mstat", [128, 8], F32))
    scol = p1.enter_context(nc.sbuf_tensor("scol", [128, 4], F32))
    bcol = p1.enter_context(nc.sbuf_tensor("bcol", [128, 4], F32))
    qraw = p1.enter_context(nc.sbuf_tensor("qraw", [128, 2 * 512], BF16))
    qpm = p1.enter_context(nc.sbuf_tensor("qpm", [128, 2 * 512], BF16))

    xring = Ring(2)
    with nc.Block() as blk:
        eT, eV, eS, eG, eY = blk.tensor, blk.vector, blk.scalar, blk.gpsimd, blk.sync

        def _(g):
            dma(g, dcg, wq_sb[:], wq)
            dma(g, dcg, cos_sb[:], cosr)
            dma(g, dcg, sin_sb[:], sinr)
        eG(_)
        const_dcg = dcg.n

        def _(sy):
            dma(sy, dcs, ident_sb[:], identb)
            dma(sy, dcs, perm_sb[:], permb)
            dma(sy, dcs, onesc_sb[:], onesc)
            dma(sy, dcs, onesr_sb[:], onesr)
            dma(sy, dcs, onesrf_sb[:], onesrf)
        eY(_)
        const_dcs = dcs.n

        ps_tp = [p1.enter_context(nc.psum_tensor(f"ps_tp{i}", [128, 512], BF16)) for i in range(2)]
        ps_qk = [p1.enter_context(nc.psum_tensor(f"ps_qk{i}", [128, 512], F32)) for i in range(3)]
        ps_pm = [p1.enter_context(nc.psum_tensor(f"ps_pm{i}", [128, 512], F32)) for i in range(2)]

        tp_ring, qk_ring, pm_ring = Ring(2), Ring(3), Ring(2)
        qraw_ring, qpm_ring = Ring(2), Ring(2)
        xln_guard = [0]

        STS = [(0, 4), (4, 4), (8, 4), (12, 4), (16, 2)]
        for si, (t0, ntl) in enumerate(STS):
            W = 128 * ntl
            sbat = SBat()
            for h2 in range(0, ntl, 2):
                sub = list(range(h2, min(h2 + 2, ntl)))
                for i in sub:
                    t = t0 + i
                    xs = t % 2

                    def _(sy, t=t, xs=xs):
                        xring.acquire(sy, xs)
                        dma(sy, dlx[xs], xtile[:, xs * D:(xs + 1) * D],
                            xp[128 * t:128 * (t + 1), :])
                    eY(_)

                    def _(v, i=i, xs=xs):
                        dlx[xs].wait(v)
                        v.bn_stats(out=stat6[:, 12 * i:12 * i + 6],
                                   in_=xtile[:, xs * D:xs * D + 512])
                        v.bn_stats(out=stat6[:, 12 * i + 6:12 * i + 12],
                                   in_=xtile[:, xs * D + 512:xs * D + 1024])
                    eV(_)

                def _(v, sub=tuple(sub), h2=h2):
                    v.drain()
                    for i in sub:
                        v.bn_aggr(out=mstat[:, 2 * i:2 * i + 2],
                                  in_=stat6[:, 12 * i:12 * i + 12])
                    v.drain()
                    nn = len(sub)
                    va = mstat.ap().rearrange("p (t two) -> p t two", two=2)
                    v.tensor_scalar(out=scol[:, h2:h2 + nn], in0=va[:, h2:h2 + nn, 1],
                                    scalar1=LN_EPS, scalar2=None, op0=ALU.add).then_inc(v2s.h, 1); v2s.n += 1
                eV(_)

                vw = v2s.n
                s2v.n += 1

                def _j(s, h2=h2, nn=len(sub), vw=vw):
                    s.wait_ge(v2s.h, vw)
                    s.activation(out=scol[:, h2:h2 + nn], in_=scol[:, h2:h2 + nn],
                                 func=ACT.Sqrt).then_inc(s2v.h, 1)
                sbat.add(_j)

                def _(v, sub=tuple(sub), h2=h2):
                    s2v.wait(v)
                    nn = len(sub)
                    v.reciprocal(out=scol[:, h2:h2 + nn], in_=scol[:, h2:h2 + nn])
                    va = mstat.ap().rearrange("p (t two) -> p t two", two=2)
                    v.tensor_scalar(out=bcol[:, h2:h2 + nn], in0=va[:, h2:h2 + nn, 0],
                                    scalar1=-1.0, scalar2=None, op0=ALU.mult)
                    v.drain()
                    v.nop().then_inc(v2s.h, 1); v2s.n += 1
                eV(_)

                for i in sub:
                    xs = (t0 + i) % 2

                    def _(v, i=i, xs=xs, xg=xln_guard[0]):
                        if xg:
                            v.wait_ge(ddb.h, xg)
                        v.tensor_scalar(out=xln[:, i * D:(i + 1) * D],
                                        in0=xtile[:, xs * D:(xs + 1) * D],
                                        scalar1=bcol[:, i:i + 1], scalar2=scol[:, i:i + 1],
                                        op0=ALU.add, op1=ALU.mult).then_inc(s2t.h, 1)
                        s2t.n += 1
                    eV(_)
                    xring.release(xs, s2t)
                    if "xln" in dbg_out:
                        t = t0 + i

                        def _(g, t=t, i=i):
                            s2t.wait(g)
                            ddb.wait(g)
                            dma(g, ddb, dbg_out["xln"][128 * t:128 * (t + 1), :],
                                xln[:, i * D:(i + 1) * D])
                        eG(_)

            for c in range(DCH):
                slot = c % 2

                def _(t_, c=c, ntl=ntl, slot=slot):
                    if c == 0:
                        s2t.wait(t_)
                        t_.wait_ge(dcs.h, const_dcs)
                    tp_ring.acquire(t_, slot)
                    for i in range(ntl):
                        mi = t_.transpose(ps_tp[slot][:, 128 * i:128 * (i + 1)],
                                          xln[:, i * D + 128 * c: i * D + 128 * (c + 1)],
                                          ident_sb[:])
                    mi.then_inc(t2v.h, 1); t2v.n += 1
                eT(_)

                def _(v, c=c, W=W, slot=slot):
                    t2v.wait(v)
                    v.tensor_copy(out=xlnT[:, 512 * c:512 * c + W],
                                  in_=ps_tp[slot][:, 0:W]).then_inc(v2t.h, 1)
                    v2t.n += 1
                eV(_)
                tp_ring.release(slot, v2t)
            xlnT_done = v2t.n

            qkv_val = {}

            def emit_qkj(j, W=W):
                slot = j % 3

                def _(t_, j=j, W=W, slot=slot):
                    if j == 0:
                        t_.wait_ge(v2t.h, xlnT_done)
                        t_.wait_ge(dcg.h, const_dcg)
                    qk_ring.acquire(t_, slot)
                    for c in range(DCH):
                        mi = t_.matmul(ps_qk[slot][:, 0:W],
                                       wq_sb[:, 3072 * c + 128 * j: 3072 * c + 128 * (j + 1)],
                                       xlnT[:, 512 * c:512 * c + W],
                                       start=(c == 0), stop=(c == DCH - 1))
                    mi.then_inc(t2s.h, 1); t2s.n += 1
                eT(_)
                qkv_val[j] = t2s.n

            def emit_tail(j, t0=t0, W=W):
                slot = j % 3
                ss = j % 2

                qw = qkv_val[j]
                rw = qraw_ring.waits(ss)
                s2t.n += 1

                def _j(s, W=W, slot=slot, ss=ss, qw=qw, rw=rw):
                    s.wait_ge(t2s.h, qw)
                    for hh, vv in rw:
                        s.wait_ge(hh, vv)
                    s.copy(out=qraw[:, 512 * ss:512 * ss + W],
                           in_=ps_qk[slot][:, 0:W]).then_inc(s2t.h, 1)
                sbat.add(_j)
                ev_val = s2t.n
                qk_ring.release(slot, s2t)

                def _(t_, W=W, ss=ss, ev=ev_val):
                    t_.wait_ge(s2t.h, ev)
                    pm_ring.acquire(t_, ss)
                    t_.matmul(ps_pm[ss][:, 0:W], perm_sb[:],
                              qraw[:, 512 * ss:512 * ss + W],
                              start=True, stop=True).then_inc(t2v.h, 1)
                    t2v.n += 1
                eT(_)
                pm_val = t2v.n

                def _(v, t0=t0, W=W, ss=ss, pv=pm_val):
                    v.wait_ge(t2v.h, pv)
                    qpm_ring.acquire(v, ss)
                    v.tensor_tensor(out=qraw[:, 512 * ss:512 * ss + W],
                                    in0=qraw[:, 512 * ss:512 * ss + W],
                                    in1=cos_sb[:, 128 * t0:128 * t0 + W], op=ALU.mult)
                    v.tensor_tensor(out=qpm[:, 512 * ss:512 * ss + W],
                                    in0=ps_pm[ss][:, 0:W],
                                    in1=sin_sb[:, 128 * t0:128 * t0 + W],
                                    op=ALU.mult).then_inc(v2g.h, 1)
                    v2g.n += 1
                eV(_)
                pm_ring.release(ss, v2g)

                def _(g, j=j, t0=t0, W=W, ss=ss):
                    v2g.wait(g)
                    dest = qT if j < 8 else kT
                    jj = j % 8
                    c0 = TKV * jj + 128 * t0
                    g.tensor_tensor(out=dest[:, c0:c0 + W],
                                    in0=qraw[:, 512 * ss:512 * ss + W],
                                    in1=qpm[:, 512 * ss:512 * ss + W],
                                    op=ALU.add).then_inc(g2t.h, 1)
                    g2t.n += 1
                eG(_)
                qraw_ring.release(ss, g2t)
                qpm_ring.release(ss, g2t)

            for j in range(16):
                emit_qkj(j)
                if j >= 1:
                    emit_tail(j - 1)
            emit_tail(15)
            sbat.flush(eS)

            if "xln" in dbg_out:
                xln_guard[0] = ddb.n
            vmm_val = {}

            def emit_vmm(i, hf):
                slot = (2 * i + hf) % 3

                def _(t_, i=i, hf=hf, slot=slot):
                    qk_ring.acquire(t_, slot)
                    for c in range(DCH):
                        mi = t_.matmul(ps_qk[slot][:, 0:512],
                                       xlnT[:, 512 * c + 128 * i: 512 * c + 128 * (i + 1)],
                                       wq_sb[:, 3072 * c + 2048 + 512 * hf: 3072 * c + 2048 + 512 * (hf + 1)],
                                       start=(c == 0), stop=(c == DCH - 1))
                    mi.then_inc(t2v.h, 1); t2v.n += 1
                eT(_)
                vmm_val[(i, hf)] = t2v.n

            def emit_vevac(i, hf, t0=t0):
                t = t0 + i
                slot = (2 * i + hf) % 3

                def _(v, t=t, hf=hf, slot=slot, mv=vmm_val[(i, hf)]):
                    v.wait_ge(t2v.h, mv)
                    blkap = vtm.ap()[:, 1040 * t: 1040 * (t + 1)].rearrange(
                        "p (g w) -> p g w", w=65)
                    if hf == 0:
                        v.memset(blkap[:, :, 64:65], 1.0)
                    dst8 = blkap[:, 8 * hf:8 * (hf + 1), 0:64]
                    src8 = ps_qk[slot].ap()[:, 0:512].rearrange(
                        "p (g w) -> p g w", w=64)
                    v.tensor_copy(out=dst8, in_=src8).then_inc(v2t.h, 1)
                    v2t.n += 1
                eV(_)
                qk_ring.release(slot, v2t)

            vjobs = [(i, hf) for i in range(ntl) for hf in range(2)]
            for n_, (i, hf) in enumerate(vjobs):
                emit_vmm(i, hf)
                if n_ >= 1:
                    emit_vevac(*vjobs[n_ - 1])
            emit_vevac(*vjobs[-1])

        if any(k in dbg_out for k in ("qT", "kT", "vtm")):
            def _(g):
                v2t.wait(g)
                ddb.wait(g)
                for name, sbuf in (("qT", qT), ("kT", kT), ("vtm", vtm)):
                    if name in dbg_out:
                        dma(g, ddb, dbg_out[name], sbuf[:])
                g.wait_ge(ddb.h, ddb.n)
            eG(_)

    p1.close()
    if stage <= 2:
        pA.close()
        st.close()
        return nc, dbg_out

    wo_sb = pA.enter_context(nc.sbuf_tensor("wo_sb", [128, DCH * D], BF16))
    attn_T = pA.enter_context(nc.sbuf_tensor("attn_T", [128, DCH * TQ], BF16))
    p2 = ExitStack()
    mask_sb = p2.enter_context(nc.sbuf_tensor("mask_sb", [128, NT * 384], BF16))
    pb = p2.enter_context(nc.sbuf_tensor("pb", [128, 4 * 384], BF16))
    sums_sb = p2.enter_context(nc.sbuf_tensor("sums_sb", [16, TQ], F32))
    sums_st = p2.enter_context(nc.sbuf_tensor("sums_st", [1, TQ], F32))
    sums_bf = p2.enter_context(nc.sbuf_tensor("sums_bf", [16, TQ], BF16))
    rrow_bf = p2.enter_context(nc.sbuf_tensor("rrow_bf", [1, TQ], BF16))
    rb_sb = p2.enter_context(nc.sbuf_tensor("rb_sb", [128, 2 * TQ], BF16))

    with nc.Block() as blk:
        fence(blk)
        eT, eV, eS, eG, eY = blk.tensor, blk.vector, blk.scalar, blk.gpsimd, blk.sync

        def _(g):
            g.wait_ge(dcg.h, dcg.n)
            dma(g, dcg, mask_sb[:], maskh)
            dma(g, dcg, wo_sb[:], wo)
        eG(_)
        p2_loads = dcg.n

        at_ps = p2.enter_context(nc.psum_tensor("at_ps", [128, TQ], F32))
        p2s = ExitStack()
        scp = p2s.enter_context(nc.psum_tensor("scp", [128, 4 * 512], F32))

        sc_ring, pb_ring, at_ring = Ring(4), Ring(4), Ring(1)
        sums_stg_ring = Ring(1)

        pv_pairs = [(c, t) for c in range(NT) for t in tset(c)]
        pv_last = {}
        for c, t in pv_pairs:
            pv_last[(t - 1) // 4] = (c, t)
        p2bat = SBat()
        for h in range(H):
            jj, po = h // 2, 64 * (h % 2)
            fresh = set(range(4))
            qk_val = {}

            def emit_qk2(g, h=h, jj=jj, po=po):
                for c in (2 * g, 2 * g + 1):
                    slot = c % 4

                    def _(t_, c=c, slot=slot, h=h, jj=jj, po=po):
                        if h == 0 and c == 0:
                            t_.wait_ge(dcg.h, p2_loads)
                        sc_ring.acquire(t_, slot)
                        q0 = qws(c)
                        t_.matmul(scp[:, 512 * slot:512 * slot + 384],
                                  kT[po:po + 64, TKV * jj + 128 * c: TKV * jj + 128 * (c + 1)],
                                  qT[po:po + 64, TKV * jj + q0: TKV * jj + q0 + 384],
                                  start=True, stop=True,
                                  tile_position=(po, 0)).then_inc(t2s.h, 1)
                        t2s.n += 1
                    eT(_)
                qk_val[g] = t2s.n

            def emit_exp_pv(g, h=h, fresh=fresh):
                c0 = 2 * g
                s0 = c0 % 4

                qw = qk_val[g]
                rw = pb_ring.waits(s0) + pb_ring.waits(s0 + 1)
                s2t.n += 1

                def _j(s, s0=s0, qw=qw, rw=rw):
                    s.wait_ge(t2s.h, qw)
                    for hh, vv in rw:
                        s.wait_ge(hh, vv)
                    inap = scp.ap()[:, 512 * s0:512 * s0 + 1024].rearrange(
                        "p (two w) -> p two w", w=512)[:, :, 0:384]
                    outap = pb.ap()[:, 384 * s0:384 * (s0 + 2)].rearrange(
                        "p (two w) -> p two w", w=384)
                    s.activation(out=outap, in_=inap, func=ACT.Exp,
                                 scale=0.125, bias=0.0).then_inc(s2t.h, 1)
                p2bat.add(_j)
                exp_val = s2t.n
                sc_ring.release(s0, s2t)
                sc_ring.release(s0 + 1, s2t)

                def _(gq, g=g, s0=s0, ev=exp_val):
                    gq.wait_ge(s2t.h, ev)
                    gq.tensor_tensor(out=pb[:, 384 * s0:384 * (s0 + 2)],
                                     in0=pb[:, 384 * s0:384 * (s0 + 2)],
                                     in1=mask_sb[:, 384 * c0:384 * (c0 + 2)],
                                     op=ALU.mult).then_inc(g2t.h, 1)
                    g2t.n += 1
                eG(_)
                msk_val = g2t.n

                for c in (c0, c0 + 1):
                    pslot = c % 4

                    def _(t_, c=c, pslot=pslot, h=h, fresh=fresh, ev=msk_val):
                        t_.wait_ge(g2t.h, ev)
                        if c == 0:
                            at_ring.acquire(t_, 0)
                        q0 = qws(c)
                        for t in tset(c):
                            j0 = 128 * t - q0
                            bank = (t - 1) // 4
                            stf = bank in fresh
                            fresh.discard(bank)
                            last = pv_last[bank] == (c, t)
                            mi = t_.matmul(at_ps[0:65, 128 * (t - 1):128 * t],
                                           vtm[:, 1040 * c + 65 * h: 1040 * c + 65 * h + 65],
                                           pb[:, 384 * pslot + j0: 384 * pslot + j0 + 128],
                                           start=stf, stop=last, tile_position=(0, 0))
                        mi.then_inc(t2v.h, 1); t2v.n += 1
                    eT(_)
                    pb_ring.release(pslot, t2v)

            for g in range(9):
                emit_qk2(g)
                if g >= 1:
                    emit_exp_pv(g - 1)
            emit_exp_pv(8)

            hs = h % 2

            def _(v, h=h):
                t2v.wait(v)
                sums_stg_ring.acquire(v, 0)
                v.tensor_copy(out=sums_st[0:1, :],
                              in_=at_ps[64:65, 0:TQ]).then_inc(s2t.h, 1)
                s2t.n += 1
            eV(_)
            sums_done = s2t.n

            def _(sy, h=h, hs=hs):
                sy.wait_ge(s2t.h, sums_done)
                dma(sy, dxr[hs], sums_sb[h:h + 1, :], sums_st[0:1, :])
            eY(_)
            sums_stg_ring.release(0, dxr[hs])

            tw = t2v.n
            v2t.n += 1

            def _j(s, jj=jj, po=po, tw=tw):
                s.wait_ge(t2v.h, tw)
                s.copy(out=attn_T[po:po + 64, TQ * jj:TQ * (jj + 1)],
                       in_=at_ps[0:64, 0:TQ]).then_inc(v2t.h, 1)
            p2bat.add(_j)

            def _(t_, sd=sums_done):
                t_.wait_ge(s2t.h, sd)
            eT(_)
            at_ring.release(0, v2t)
            at_ring.release_add(0, s2t)
            p2bat.flush(eS)

        def _(v):
            v.wait_ge(dxr[0].h, dxr[0].n)
            v.wait_ge(dxr[1].h, dxr[1].n)
            v.reciprocal(out=sums_sb[:], in_=sums_sb[:])
            v.drain()
            v.tensor_copy(out=sums_bf[:], in_=sums_sb[:]).then_inc(v2t.h, 1)
            v2t.n += 1
        eV(_)
        recip_done_v = v2t.n

        rrow_ring = Ring(1)
        rb_ring = Ring(2)

        p2s.close()
        rb_ps = p2.enter_context(nc.psum_tensor("rb_ps", [128, TQ], F32))
        rbps_ring = Ring(1)
        p2bat2 = SBat()

        for h in range(H):
            jj, po = h // 2, 64 * (h % 2)

            def _(sy, h=h):
                if h == 0:
                    sy.wait_ge(v2t.h, recip_done_v)
                rrow_ring.acquire(sy, 0)
                dma(sy, dlx[0], rrow_bf[0:1, :], sums_bf[h:h + 1, :])
            eY(_)

            def _(t_, po=po):
                dlx[0].wait(t_)
                rbps_ring.acquire(t_, 0)
                for g in range(4):
                    mi = t_.matmul(rb_ps[po:po + 64, 512 * g:512 * (g + 1)],
                                   onesr_sb[0:1, 0:64],
                                   rrow_bf[0:1, 512 * g:512 * (g + 1)],
                                   start=True, stop=True,
                                   tile_position=(0, po))
                mi.then_inc(t2s.h, 1); t2s.n += 1
            eT(_)
            rrow_ring.release(0, t2s)

            hs = h % 2

            tw = t2s.n
            rw = rb_ring.waits(hs)
            s2v.n += 1

            def _j(s, po=po, hs=hs, tw=tw, rw=rw):
                s.wait_ge(t2s.h, tw)
                for hh, vv in rw:
                    s.wait_ge(hh, vv)
                s.copy(out=rb_sb[po:po + 64, TQ * hs:TQ * (hs + 1)],
                       in_=rb_ps[po:po + 64, :]).then_inc(s2v.h, 1)
            p2bat2.add(_j)
            rbps_ring.release(0, s2v)

            def _(v, jj=jj, po=po, hs=hs):
                s2v.wait(v)
                dest = attn_T[po:po + 64, TQ * jj:TQ * (jj + 1)]
                v.tensor_tensor(out=dest, in0=dest,
                                in1=rb_sb[po:po + 64, TQ * hs:TQ * (hs + 1)],
                                op=ALU.mult).then_inc(v2t.h, 1)
                v2t.n += 1
            eV(_)
            rb_ring.release(hs, v2t)
        p2bat2.flush(eS)

        if "attn" in dbg_out:
            def _(g):
                v2t.wait(g)
                ddb.wait(g)
                dma(g, ddb, dbg_out["attn"], attn_T[:])
                g.wait_ge(ddb.h, ddb.n)
            eG(_)

    p2.close()
    if stage <= 3:
        pA.close()
        st.close()
        return nc, dbg_out

    p25 = ExitStack()
    yln = p25.enter_context(nc.sbuf_tensor("yln", [128, 2 * D], F32))
    xrr = p25.enter_context(nc.sbuf_tensor("xrr", [128, 2 * D], F32))
    x2t = p25.enter_context(nc.sbuf_tensor("x2t", [128, 2 * D], F32))
    st6b = p25.enter_context(nc.sbuf_tensor("st6b", [128, 24], F32))
    msb = p25.enter_context(nc.sbuf_tensor("msb", [128, 4], F32))
    scl2 = p25.enter_context(nc.sbuf_tensor("scl2", [128, 2], F32))
    bcl2 = p25.enter_context(nc.sbuf_tensor("bcl2", [128, 2], F32))
    bor_sb = p25.enter_context(nc.sbuf_tensor("bor_sb", [1, D], BF16))

    spill_val = {}
    with nc.Block() as blk:
        fence(blk)
        eT, eV, eS, eG, eY = blk.tensor, blk.vector, blk.scalar, blk.gpsimd, blk.sync

        def _(g):
            g.wait_ge(dcg.h, dcg.n)
            dma(g, dcg, bor_sb[:], bor)
        eG(_)
        p25_loads = dcg.n

        y_ps = [p25.enter_context(nc.psum_tensor(f"y_ps{i}", [128, D], F32)) for i in range(2)]
        yps_ring, yln_ring = Ring(2), Ring(2)
        xrr_ring, x2_ring = Ring(2), Ring(2)

        p25bat = SBat()
        for tt in range(16):
            slot = tt % 2

            def _(sy, tt=tt, slot=slot):
                xrr_ring.acquire(sy, slot)
                dma(sy, dxr[slot], xrr[:, slot * D:(slot + 1) * D],
                    xp[128 * (tt + 1):128 * (tt + 2), :])
            eY(_)

            def _(t_, tt=tt, slot=slot):
                if tt == 0:
                    t_.wait_ge(dcg.h, p25_loads)
                yps_ring.acquire(t_, slot)
                for hf in range(2):
                    t_.matmul(y_ps[slot][:, 512 * hf:512 * (hf + 1)], onesr_sb[:],
                              bor_sb[0:1, 512 * hf:512 * (hf + 1)], start=True, stop=False)
                    for c in range(DCH):
                        mi = t_.matmul(y_ps[slot][:, 512 * hf:512 * (hf + 1)],
                                       attn_T[:, TQ * c + 128 * tt: TQ * c + 128 * (tt + 1)],
                                       wo_sb[:, 1024 * c + 512 * hf: 1024 * c + 512 * (hf + 1)],
                                       start=False, stop=(c == DCH - 1))
                mi.then_inc(t2s.h, 1); t2s.n += 1
            eT(_)

            def _(v, slot=slot):
                t2s.wait(v)
                v.bn_stats(out=st6b[:, 12 * slot:12 * slot + 6],
                           in_=y_ps[slot][:, 0:512])
                v.bn_stats(out=st6b[:, 12 * slot + 6:12 * slot + 12],
                           in_=y_ps[slot][:, 512:1024])
                v.drain()
                v.bn_aggr(out=msb[:, 2 * slot:2 * slot + 2],
                          in_=st6b[:, 12 * slot:12 * slot + 12])
                v.drain()
                v.tensor_scalar(out=scl2[:, slot:slot + 1],
                                in0=msb[:, 2 * slot + 1:2 * slot + 2],
                                scalar1=LN_EPS, scalar2=None, op0=ALU.add).then_inc(v2s.h, 1); v2s.n += 1
            eV(_)

            vw = v2s.n
            s2v.n += 1

            def _j(s, slot=slot, vw=vw):
                s.wait_ge(v2s.h, vw)
                s.activation(out=scl2[:, slot:slot + 1], in_=scl2[:, slot:slot + 1],
                             func=ACT.Sqrt).then_inc(s2v.h, 1)
            p25bat.add(_j)

            def _(v, slot=slot):
                s2v.wait(v)
                v.reciprocal(out=scl2[:, slot:slot + 1], in_=scl2[:, slot:slot + 1])
                v.tensor_scalar(out=bcl2[:, slot:slot + 1],
                                in0=msb[:, 2 * slot:2 * slot + 1],
                                scalar1=-1.0, scalar2=None, op0=ALU.mult)
                v.drain()
                yln_ring.acquire(v, slot)
                v.tensor_scalar(out=yln[:, slot * D:(slot + 1) * D],
                                in0=y_ps[slot][:, :],
                                scalar1=bcl2[:, slot:slot + 1],
                                scalar2=scl2[:, slot:slot + 1],
                                op0=ALU.add, op1=ALU.mult).then_inc(v2g.h, 1)
                v2g.n += 1
            eV(_)
            yps_ring.release(slot, v2g)

            def _(g, slot=slot):
                v2g.wait(g)
                dxr[slot].wait(g)
                x2_ring.acquire(g, slot)
                g.tensor_tensor(out=x2t[:, slot * D:(slot + 1) * D],
                                in0=yln[:, slot * D:(slot + 1) * D],
                                in1=xrr[:, slot * D:(slot + 1) * D],
                                op=ALU.add).then_inc(g2t.h, 1)
                g2t.n += 1
            eG(_)
            yln_ring.release(slot, g2t)
            xrr_ring.release(slot, g2t)

            def _(sy, tt=tt, slot=slot):
                g2t.wait(sy)
                dma(sy, dsp[slot], x2sp[128 * tt:128 * (tt + 1), :],
                    x2t[:, slot * D:(slot + 1) * D])
            eY(_)
            spill_val[tt] = (dsp[slot], dsp[slot].n)
            x2_ring.release(slot, dsp[slot])
            if "x2" in dbg_out:
                def _(g, tt=tt, slot=slot):
                    g2t.wait(g)
                    ddb.wait(g)
                    dma(g, ddb, dbg_out["x2"][128 * tt:128 * (tt + 1), :],
                        x2t[:, slot * D:(slot + 1) * D])
                eG(_)
                x2_ring.release_add(slot, ddb)

        p25bat.flush(eS)

        def _(sy):
            sy.wait_ge(dsp[0].h, dsp[0].n)
            sy.wait_ge(dsp[1].h, dsp[1].n)
        eY(_)

    p25.close()
    pA.close()
    if stage <= 4:
        st.close()
        return nc, dbg_out

    comb_sb = st.enter_context(nc.sbuf_tensor("comb_sb", [128, 16 * ES], BF16))
    slin = st.enter_context(nc.sbuf_tensor("slin", [128, 2 * D], BF16))
    w1_sb = st.enter_context(nc.sbuf_tensor("w1_sb", [128, E * DCH * HID], BF16))
    p3 = ExitStack()
    xn_sb = p3.enter_context(nc.sbuf_tensor("xn_sb", [128, 16 * 1056], BF16))
    E_sb = p3.enter_context(nc.sbuf_tensor("E_sb", [128, 16 * ES], BF16))
    gbc_sb = p3.enter_context(nc.sbuf_tensor("gbc_sb", [128, D], F32))
    bbc_sb = p3.enter_context(nc.sbuf_tensor("bbc_sb", [128, D], F32))
    scbc_sb = p3.enter_context(nc.sbuf_tensor("scbc_sb", [128, 1], F32))
    mu_sb = p3.enter_context(nc.sbuf_tensor("mu_sb", [128, DCH * ES], F32))
    mu2_sb = p3.enter_context(nc.sbuf_tensor("mu2_sb", [128, DCH * ES], BF16))
    nr_sb = p3.enter_context(nc.sbuf_tensor("nr_sb", [1, ES], F32))
    muU_sb = p3.enter_context(nc.sbuf_tensor("muU_sb", [128, DCH * ES], BF16))
    x2r = p3.enter_context(nc.sbuf_tensor("x2r", [128, 4 * D], F32))
    u2sb = p3.enter_context(nc.sbuf_tensor("u2sb", [128, 2 * D], BF16))
    xnT_sb = p3.enter_context(nc.sbuf_tensor("xnT_sb", [128, 2 * 1024], BF16))
    rs_sb = p3.enter_context(nc.sbuf_tensor("rs_sb", [128, 16], F32))
    st6c = p3.enter_context(nc.sbuf_tensor("st6c", [128, 48], F32))
    msc = p3.enter_context(nc.sbuf_tensor("msc", [128, 8], F32))
    s1c = p3.enter_context(nc.sbuf_tensor("s1c", [128, 4], F32))
    s2c = p3.enter_context(nc.sbuf_tensor("s2c", [128, 4], F32))
    b12 = p3.enter_context(nc.sbuf_tensor("b12", [128, 4], F32))
    sq_sb = p3.enter_context(nc.sbuf_tensor("sq_sb", [128, D], F32))
    fac = p3.enter_context(nc.sbuf_tensor("fac", [128, 16], F32))
    ccin_sb = p3.enter_context(nc.sbuf_tensor("ccin_sb", [128, 2050], F32))
    usb = ccin_sb

    with nc.Block() as blk:
        fence(blk)
        eT, eV, eS, eG, eY = blk.tensor, blk.vector, blk.scalar, blk.gpsimd, blk.sync

        def _(g):
            g.wait_ge(dcg.h, dcg.n)
            dma(g, dcg, w1_sb[:], w1h)
        eG(_)

        def _(sy):
            sy.wait_ge(dcs.h, dcs.n)
            dma(sy, dcs, gbc_sb[:], gbc)
            dma(sy, dcs, bbc_sb[:], bbc)
            dma(sy, dcs, scbc_sb[:], scbc)
            dma(sy, dcs, mu_sb[:], muh)
        eY(_)
        p3_loads = dcs.n

        p3s = ExitStack()
        ps_tb = [p3s.enter_context(nc.psum_tensor(f"ps_tb{i}", [128, 512], BF16)) for i in range(2)]
        lg_ps = [p3s.enter_context(nc.psum_tensor(f"lg_ps{i}", [128, 512], F32)) for i in range(2)]
        nb_ps = p3s.enter_context(nc.psum_tensor("nb_ps", [128, ES], F32))

        b4a = SBat()
        dw = p3_loads
        s2t.n += 1

        def _j(s, dw=dw):
            s.wait_ge(dcs.h, dw)
            s.activation(out=mu2_sb[:], in_=mu_sb[:], func=ACT.Square).then_inc(s2t.h, 1)
        b4a.add(_j)

        def _(t_):
            s2t.wait(t_)
            for c in range(DCH):
                mi = t_.matmul(nb_ps[0:1, :], onesc_sb[:], mu2_sb[:, ES * c:ES * (c + 1)],
                               start=(c == 0), stop=(c == DCH - 1))
            mi.then_inc(t2s.h, 1); t2s.n += 1
        eT(_)

        tw = t2s.n
        s2v.n += 1

        def _j(s, tw=tw):
            s.wait_ge(t2s.h, tw)
            s.activation(out=nr_sb[:], in_=nb_ps[0:1, :], func=ACT.Sqrt).then_inc(s2v.h, 1)
        b4a.add(_j)
        b4a.flush(eS)

        def _(v):
            s2v.wait(v)
            v.tensor_scalar(out=nr_sb[:], in0=nr_sb[:], scalar1=1e-12, scalar2=None,
                            op0=ALU.max)
            v.drain()
            v.reciprocal(out=nr_sb[:], in_=nr_sb[:])
            v.drain().then_inc(v2t.h, 1); v2t.n += 1
        eV(_)

        def _(t_):
            v2t.wait(t_)
            t_.matmul(nb_ps[:, :], onesrf_sb[:], nr_sb[0:1, :],
                      start=True, stop=True).then_inc(t2v.h, 1)
            t2v.n += 1
        eT(_)

        def _(v):
            t2v.wait(v)
            for c in range(DCH):
                mi = v.tensor_tensor(out=muU_sb[:, ES * c:ES * (c + 1)],
                                     in0=mu_sb[:, ES * c:ES * (c + 1)], in1=nb_ps[:, :],
                                     op=ALU.mult)
            mi.then_inc(v2t.h, 1); v2t.n += 1
        eV(_)
        muU_done = v2t.n

        tb_ring, lg_ring, xnT_ring = Ring(2), Ring(2), Ring(2)
        x2r_ring, u2_ring = Ring(4), Ring(2)
        for gg in range(4):
            tiles = list(range(4 * gg, 4 * gg + 4))
            b4s, b4e = SBat(), SBat()
            for i, tt in enumerate(tiles):
                def _(sy, tt=tt, i=i):
                    x2r_ring.acquire(sy, i)
                    sv = spill_val[tt]
                    sy.wait_ge(sv[0].h, sv[1])
                    dma(sy, dxc[i], x2r[:, i * D:(i + 1) * D],
                        x2sp[128 * tt:128 * (tt + 1), :])
                eY(_)

                def _(v, i=i):
                    dxc[i].wait(v)
                    v.bn_stats(out=st6c[:, 12 * i:12 * i + 6], in_=x2r[:, i * D:i * D + 512])
                    v.bn_stats(out=st6c[:, 12 * i + 6:12 * i + 12],
                               in_=x2r[:, i * D + 512:i * D + 1024])
                eV(_)

            def _(v):
                v.drain()
                for i in range(4):
                    v.bn_aggr(out=msc[:, 2 * i:2 * i + 2], in_=st6c[:, 12 * i:12 * i + 12])
                v.drain()
                va = msc.ap().rearrange("p (t two) -> p t two", two=2)
                v.tensor_scalar(out=s1c[:, 0:4], in0=va[:, 0:4, 1], scalar1=LN_EPS,
                                scalar2=None, op0=ALU.add).then_inc(v2s.h, 1); v2s.n += 1
            eV(_)

            vw = v2s.n
            s2v.n += 1

            def _j(s, vw=vw):
                s.wait_ge(v2s.h, vw)
                s.activation(out=s1c[:, 0:4], in_=s1c[:, 0:4], func=ACT.Sqrt).then_inc(s2v.h, 1)
            b4s.add(_j)

            def _(v):
                s2v.wait(v)
                v.reciprocal(out=s1c[:, 0:4], in_=s1c[:, 0:4])
                v.drain()
                va = msc.ap().rearrange("p (t two) -> p t two", two=2)
                v.tensor_tensor(out=s2c[:, 0:4], in0=s1c[:, 0:4], in1=s1c[:, 0:4],
                                op=ALU.mult)
                v.drain()
                v.tensor_tensor(out=s2c[:, 0:4], in0=s2c[:, 0:4], in1=va[:, 0:4, 1],
                                op=ALU.mult)
                v.drain()
                v.tensor_scalar(out=s2c[:, 0:4], in0=s2c[:, 0:4], scalar1=LN_EPS,
                                scalar2=None, op0=ALU.add).then_inc(v2s.h, 1); v2s.n += 1
            eV(_)

            vw = v2s.n
            s2v.n += 1

            def _j(s, vw=vw):
                s.wait_ge(v2s.h, vw)
                s.activation(out=s2c[:, 0:4], in_=s2c[:, 0:4], func=ACT.Sqrt).then_inc(s2v.h, 1)
            b4s.add(_j)

            def _(v):
                s2v.wait(v)
                v.reciprocal(out=s2c[:, 0:4], in_=s2c[:, 0:4])
                v.drain()
                v.tensor_tensor(out=s1c[:, 0:4], in0=s1c[:, 0:4], in1=s2c[:, 0:4],
                                op=ALU.mult)
                va = msc.ap().rearrange("p (t two) -> p t two", two=2)
                v.tensor_scalar(out=b12[:, 0:4], in0=va[:, 0:4, 0],
                                scalar1=-1.0, scalar2=None, op0=ALU.mult)
                v.drain()
                v.tensor_tensor(out=b12[:, 0:4], in0=b12[:, 0:4], in1=s1c[:, 0:4],
                                op=ALU.mult)
                v.drain()
                v.nop().then_inc(v2s.h, 1); v2s.n += 1
            eV(_)
            chain_done = v2s.n

            sq_w = {}
            for i, tt in enumerate(tiles):
                us = tt % 2

                ruw = u2_ring.waits(us)
                s2v.n += 1

                def _j(s, i=i, us=us, cd=chain_done, ruw=ruw):
                    s.wait_ge(v2s.h, cd)
                    for hh, vv in ruw:
                        s.wait_ge(hh, vv)
                    s.activation(out=u2sb[:, us * D:(us + 1) * D],
                                 in_=x2r[:, i * D:(i + 1) * D], func=ACT.Identity,
                                 scale=s1c[:, i:i + 1],
                                 bias=b12[:, i:i + 1]).then_inc(s2v.h, 1)
                b4s.add(_j)
                x2r_ring.release(i, s2v)
                app_val = s2v.n

                def _(v, tt=tt, us=us, av=app_val):
                    v.wait_ge(s2v.h, av)
                    c0 = 1056 * tt
                    v.tensor_tensor(out=xn_sb[:, c0:c0 + D],
                                    in0=u2sb[:, us * D:(us + 1) * D],
                                    in1=gbc_sb[:], op=ALU.mult)
                    v.memset(xn_sb[:, c0 + D:c0 + 1056], 1.0)
                    v.drain()
                    v.nop().then_inc(v2g.h, 1); v2g.n += 1
                eV(_)
                u2_ring.release(us, v2g)

                def _(g, tt=tt):
                    v2g.wait(g)
                    c0 = 1056 * tt
                    g.tensor_tensor(out=xn_sb[:, c0:c0 + D], in0=xn_sb[:, c0:c0 + D],
                                    in1=bbc_sb[:], op=ALU.add).then_inc(v2t.h, 1)
                    v2t.n += 1
                eG(_)
                sq_w[i] = v2t.n

            for i, tt in enumerate(tiles):
                pw = sacc.n
                sacc.n += 1

                def _j(s, tt=tt, vw=sq_w[i], pw=pw):
                    s.wait_ge(v2t.h, vw)
                    if pw:
                        s.wait_ge(sacc.h, pw)
                    c0 = 1056 * tt
                    s.activation(out=sq_sb[:, :], in_=xn_sb[:, c0:c0 + D],
                                 func=ACT.Square,
                                 accum_out=fac[:, tt:tt + 1]).then_inc(sacc.h, 1)
                b4s.add(_j)

            sw = sacc.n
            s2v.n += 1

            def _j(s, gg=gg, sw=sw):
                s.wait_ge(sacc.h, sw)
                s.activation(out=fac[:, 4 * gg:4 * gg + 4], in_=fac[:, 4 * gg:4 * gg + 4],
                             func=ACT.Sqrt).then_inc(s2v.h, 1)
            b4s.add(_j)
            b4s.flush(eS)

            def _(v, gg=gg):
                s2v.wait(v)
                v.tensor_scalar(out=fac[:, 4 * gg:4 * gg + 4], in0=fac[:, 4 * gg:4 * gg + 4],
                                scalar1=1e-12, scalar2=None, op0=ALU.max)
                v.drain()
                v.reciprocal(out=fac[:, 4 * gg:4 * gg + 4], in_=fac[:, 4 * gg:4 * gg + 4])
                v.drain()
                v.tensor_scalar(out=fac[:, 4 * gg:4 * gg + 4], in0=fac[:, 4 * gg:4 * gg + 4],
                                scalar1=scbc_sb[:, 0:1], scalar2=None,
                                op0=ALU.mult).then_inc(v2s.h, 1); v2s.n += 1
            eV(_)
            fac_done = v2s.n

            for i, tt in enumerate(tiles):
                xslot = tt % 2
                for half in range(2):
                    slot = (2 * tt + half) % 2

                    def _(t_, tt=tt, half=half, slot=slot):
                        v2t.wait(t_)
                        tb_ring.acquire(t_, slot)
                        for cc in range(4):
                            c = 4 * half + cc
                            mi = t_.transpose(ps_tb[slot][:, 128 * cc:128 * (cc + 1)],
                                              xn_sb[:, 1056 * tt + 128 * c:1056 * tt + 128 * (c + 1)],
                                              ident_sb[:])
                        mi.then_inc(t2v.h, 1); t2v.n += 1
                    eT(_)

                    def _(v, half=half, slot=slot, xslot=xslot):
                        t2v.wait(v)
                        if half == 0:
                            xnT_ring.acquire(v, xslot)
                        v.tensor_copy(
                            out=xnT_sb[:, 1024 * xslot + 512 * half:1024 * xslot + 512 * (half + 1)],
                            in_=ps_tb[slot][:, :]).then_inc(v2t.h, 1)
                        v2t.n += 1
                    eV(_)
                    tb_ring.release(slot, v2t)

                def _(t_, tt=tt, xslot=xslot):
                    v2t.wait(t_)
                    if tt == 0:
                        t_.wait_ge(v2t.h, muU_done)
                    slot = tt % 2
                    lg_ring.acquire(t_, slot)
                    for c in range(DCH):
                        mi = t_.matmul(lg_ps[slot][:, 0:ES],
                                       xnT_sb[:, 1024 * xslot + 128 * c:1024 * xslot + 128 * (c + 1)],
                                       muU_sb[:, ES * c:ES * (c + 1)],
                                       start=(c == 0), stop=(c == DCH - 1))
                    mi.then_inc(t2s.h, 1); t2s.n += 1
                eT(_)
                xnT_ring.release(xslot, t2s)

                tw = t2s.n
                s2v.n += 1

                def _j(s, tt=tt, fd=fac_done, tw=tw):
                    s.wait_ge(t2s.h, tw)
                    s.wait_ge(v2s.h, fd)
                    slot = tt % 2
                    s.activation(out=E_sb[:, ES * tt:ES * (tt + 1)],
                                 in_=lg_ps[slot][:, 0:ES],
                                 func=ACT.Exp, scale=fac[:, tt:tt + 1],
                                 accum_out=rs_sb[:, tt:tt + 1]).then_inc(s2v.h, 1)
                b4e.add(_j)
                lg_ring.release(tt % 2, s2v)
            b4e.flush(eS)

        def _(v):
            s2v.wait(v)
            v.reciprocal(out=rs_sb[:], in_=rs_sb[:])
            v.drain().then_inc(v2s.h, 1); v2s.n += 1
        eV(_)

        b4i = SBat()
        rw0 = v2s.n
        for tt in range(16):
            s2t.n += 1

            def _j(s, tt=tt, rw0=rw0):
                if tt == 0:
                    s.wait_ge(v2s.h, rw0)
                s.activation(out=comb_sb[:, ES * tt:ES * (tt + 1)],
                             in_=E_sb[:, ES * tt:ES * (tt + 1)], func=ACT.Identity,
                             scale=rs_sb[:, tt:tt + 1]).then_inc(s2t.h, 1)
            b4i.add(_j)
        b4i.flush(eS)
        comb_done = s2t.n

        p3s.close()
        u_ps = [p3.enter_context(nc.psum_tensor(f"u_ps{i}", [128, 512], F32)) for i in range(4)]
        ue_ps = p3.enter_context(nc.psum_tensor("ue_ps", [128, 512], F32))

        def _(t_):
            t_.wait_ge(s2t.h, comb_done)
            for half in range(2):
                for tt in range(16):
                    lhs = E_sb[:, ES * tt + 128 * half: ES * tt + 128 * (half + 1)]
                    stf = (tt == 0)
                    t_.matmul(u_ps[2 * half][:, :], lhs,
                              xn_sb[:, 1056 * tt:1056 * tt + 512],
                              start=stf, stop=(tt == 15))
                    t_.matmul(u_ps[2 * half + 1][:, :], lhs,
                              xn_sb[:, 1056 * tt + 512:1056 * tt + 1024],
                              start=stf, stop=(tt == 15))
                    mi = t_.matmul(ue_ps[:, half:half + 1], lhs,
                                   xn_sb[:, 1056 * tt + 1024:1056 * tt + 1025],
                                   start=stf, stop=(tt == 15))
            mi.then_inc(t2v.h, 1); t2v.n += 1
        eT(_)

        def _(v):
            t2v.wait(v)
            v.tensor_copy(out=ccin_sb[:, 0:512], in_=u_ps[0][:, :])
            v.tensor_copy(out=ccin_sb[:, 512:1024], in_=u_ps[1][:, :])
            v.tensor_copy(out=ccin_sb[:, 1024:1025], in_=ue_ps[:, 0:1])
            v.tensor_copy(out=ccin_sb[:, 1025:1537], in_=u_ps[2][:, :])
            v.tensor_copy(out=ccin_sb[:, 1537:2049], in_=u_ps[3][:, :])
            v.tensor_copy(out=ccin_sb[:, 2049:2050], in_=ue_ps[:, 1:2]).then_inc(v2t.h, 1); v2t.n += 1
        eV(_)

        def _(g):
            v2t.wait(g)
            dma(g, ccd, ccin.ap(), ccin_sb[:])
            g.wait_ge(ccd.h, ccd.n)
            g.collective_compute(
                "AllReduce", ALU.add,
                replica_groups=[[0, 1], [2, 3], [4, 5], [6, 7]],
                ins=[ccin.ap().opt()], outs=[ccout.ap().opt()],
            ).then_inc(ccS.h, 1)
            ccS.n += 1
            g.wait_ge(ccS.h, ccS.n)
            dma(g, ccd, usb[:], ccout.ap())
        eG(_)

        def _(v):
            ccd.wait(v)
            v.reciprocal(out=usb[:, 1024:1025], in_=usb[:, 1024:1025])
            v.reciprocal(out=usb[:, 2049:2050], in_=usb[:, 2049:2050])
            v.drain()
            v.tensor_scalar(out=slin[:, 0:D], in0=usb[:, 0:1024],
                            scalar1=usb[:, 1024:1025], scalar2=None, op0=ALU.mult)
            v.tensor_scalar(out=slin[:, D:2 * D], in0=usb[:, 1025:2049],
                            scalar1=usb[:, 2049:2050], scalar2=None, op0=ALU.mult).then_inc(v2t.h, 1); v2t.n += 1
        eV(_)

        if any(k in dbg_out for k in ("xn", "ee", "comb", "usb", "slin")):
            def _(g):
                g.wait_ge(v2t.h, v2t.n)
                g.wait_ge(s2t.h, comb_done)
                ddb.wait(g)
                for nm, sbuf in (("xn", xn_sb), ("ee", E_sb), ("comb", comb_sb),
                                 ("usb", usb), ("slin", slin)):
                    if nm in dbg_out:
                        dma(g, ddb, dbg_out[nm], sbuf[:])
                g.wait_ge(ddb.h, ddb.n)
            eG(_)

    p3.close()
    if stage <= 5:
        st.close()
        return nc, dbg_out

    slout = st.enter_context(nc.sbuf_tensor("slout", [128, 2 * D], BF16))
    p5 = ExitStack()
    w2_sb = p5.enter_context(nc.sbuf_tensor("w2_sb", [128, E * 2 * D], BF16))
    slinT = p5.enter_context(nc.sbuf_tensor("slinT", [128, DCH * ES], BF16))
    h_sb = p5.enter_context(nc.sbuf_tensor("h_sb", [16, 2 * HID], BF16))
    hT_sb = p5.enter_context(nc.sbuf_tensor("hT_sb", [128, 2 * 32], BF16))
    sostg = p5.enter_context(nc.sbuf_tensor("sostg", [16, 2 * D], BF16))
    b1_sb = p5.enter_context(nc.sbuf_tensor("b1_sb", [1, E * HID], BF16))
    b2_sb = p5.enter_context(nc.sbuf_tensor("b2_sb", [1, E * D], BF16))

    with nc.Block() as blk:
        fence(blk)
        eT, eV, eS, eG, eY = blk.tensor, blk.vector, blk.scalar, blk.gpsimd, blk.sync

        w2v = {}

        def _(g):
            g.wait_ge(dcg.h, dcg.n)
            dma(g, dcg, b1_sb[:], b1r)
            dma(g, dcg, b2_sb[:], b2r)
            for e in range(E):
                sem = dw2[e % 4]
                g.wait_ge(sem.h, sem.n)
                dma(g, sem, w2_sb[:, 2048 * e:2048 * (e + 1)],
                    w2h[:, 2048 * e:2048 * (e + 1)])
                w2v[e] = (sem, sem.n)
        eG(_)
        w2_done = dcg.n

        p5s = ExitStack()
        ps_t5 = [p5s.enter_context(nc.psum_tensor(f"ps_t5{i}", [128, 512], BF16)) for i in range(2)]

        t5_ring = Ring(2)
        for c in range(DCH):
            slot = c % 2

            def _(t_, c=c, slot=slot):
                if c == 0:
                    t_.wait_ge(v2t.h, v2t.n)
                t5_ring.acquire(t_, slot)
                for half in range(2):
                    mi = t_.transpose(ps_t5[slot][:, 128 * half:128 * (half + 1)],
                                      slin[:, D * half + 128 * c: D * half + 128 * (c + 1)],
                                      ident_sb[:])
                mi.then_inc(t2v.h, 1); t2v.n += 1
            eT(_)

            def _(v, c=c, slot=slot):
                t2v.wait(v)
                v.tensor_copy(out=slinT[:, ES * c:ES * (c + 1)],
                              in_=ps_t5[slot][:, 0:ES]).then_inc(v2t.h, 1)
                v2t.n += 1
            eV(_)
            t5_ring.release(slot, v2t)
        slinT_done = v2t.n

        p5s.close()
        h_ps = [p5.enter_context(nc.psum_tensor(f"h_ps{i}", [16, HID], F32)) for i in range(2)]
        hT_ps = [p5.enter_context(nc.psum_tensor(f"hT_ps{i}", [128, 32], BF16)) for i in range(2)]
        so_ps = [p5.enter_context(nc.psum_tensor(f"so_ps{i}", [16, D], F32)) for i in range(2)]

        hps_ring, hsb_ring, hTps_ring, hT_ring = Ring(2), Ring(2), Ring(2), Ring(2)
        sops_ring, stg_ring = Ring(2), Ring(2)
        p5bat = SBat()
        for e in range(E):
            es = e % 2

            def _(t_, e=e, es=es, w2d=w2_done):
                if e == 0:
                    t_.wait_ge(v2t.h, slinT_done)
                    t_.wait_ge(dcg.h, w2d)
                hps_ring.acquire(t_, es)
                t_.matmul(h_ps[es][:, :], onesr_sb[0:1, 0:16],
                          b1_sb[0:1, HID * e:HID * (e + 1)], start=True, stop=False)
                for c in range(DCH):
                    mi = t_.matmul(h_ps[es][:, :],
                                   slinT[:, ES * c + 16 * e: ES * c + 16 * (e + 1)],
                                   w1_sb[:, 2048 * e + HID * c: 2048 * e + HID * (c + 1)],
                                   start=False, stop=(c == DCH - 1))
                mi.then_inc(t2s.h, 1); t2s.n += 1
            eT(_)

            tw = t2s.n
            rw = hsb_ring.waits(es)
            s2t.n += 1

            def _j(s, es=es, tw=tw, rw=rw):
                s.wait_ge(t2s.h, tw)
                for hh, vv in rw:
                    s.wait_ge(hh, vv)
                s.activation(out=h_sb[:, HID * es:HID * (es + 1)], in_=h_ps[es][:, :],
                             func=ACT.Gelu).then_inc(s2t.h, 1)
            p5bat.add(_j)
            hps_ring.release(es, s2t)

            def _(t_, es=es):
                s2t.wait(t_)
                hTps_ring.acquire(t_, es)
                for c in range(2):
                    mi = t_.transpose(hT_ps[es][:, 16 * c:16 * (c + 1)],
                                      h_sb[0:16, HID * es + 128 * c: HID * es + 128 * (c + 1)],
                                      ident_sb[0:16, 0:16])
                mi.then_inc(t2v.h, 1); t2v.n += 1
            eT(_)

            def _(v, es=es):
                t2v.wait(v)
                hT_ring.acquire(v, es)
                v.tensor_copy(out=hT_sb[:, 32 * es:32 * (es + 1)],
                              in_=hT_ps[es][:, :]).then_inc(v2t.h, 1)
                v2t.n += 1
            eV(_)
            hsb_ring.release(es, v2t)
            hTps_ring.release(es, v2t)

            def _(t_, e=e, es=es, wv=w2v[e]):
                v2t.wait(t_)
                t_.wait_ge(wv[0].h, wv[1])
                sops_ring.acquire(t_, es)
                for dh in range(2):
                    t_.matmul(so_ps[es][:, 512 * dh:512 * (dh + 1)], onesr_sb[0:1, 0:16],
                              b2_sb[0:1, D * e + 512 * dh: D * e + 512 * (dh + 1)],
                              start=True, stop=False)
                    for c in range(2):
                        mi = t_.matmul(so_ps[es][:, 512 * dh:512 * (dh + 1)],
                                       hT_sb[:, 32 * es + 16 * c: 32 * es + 16 * (c + 1)],
                                       w2_sb[:, 2048 * e + 1024 * c + 512 * dh: 2048 * e + 1024 * c + 512 * (dh + 1)],
                                       start=False, stop=(c == 1))
                mi.then_inc(t2v.h, 1); t2v.n += 1
            eT(_)
            hT_ring.release(es, t2v)

            def _(v, es=es):
                t2v.wait(v)
                stg_ring.acquire(v, es)
                v.tensor_copy(out=sostg[:, D * es:D * (es + 1)],
                              in_=so_ps[es][:, :]).then_inc(v2y.h, 1)
                v2y.n += 1
            eV(_)
            sops_ring.release(es, v2y)

            def _(sy, e=e, es=es):
                v2y.wait(sy)
                half, row = e // 8, 16 * (e % 8)
                dma(sy, dsg[es], slout[row:row + 16, D * half:D * (half + 1)],
                    sostg[:, D * es:D * (es + 1)])
            eY(_)
            stg_ring.release(es, dsg[es])
        p5bat.flush(eS)

        if "slout" in dbg_out:
            def _(g):
                g.wait_ge(dsg[0].h, dsg[0].n)
                g.wait_ge(dsg[1].h, dsg[1].n)
                ddb.wait(g)
                dma(g, ddb, dbg_out["slout"], slout[:])
                g.wait_ge(ddb.h, ddb.n)
            eG(_)

    p5.close()

    p6 = ExitStack()
    combT = p6.enter_context(nc.sbuf_tensor("combT", [128, 2 * TQ], BF16))
    fln = p6.enter_context(nc.sbuf_tensor("fln", [128, 2 * D], F32))
    x2rr = p6.enter_context(nc.sbuf_tensor("x2rr", [128, 2 * D], F32))
    osb = p6.enter_context(nc.sbuf_tensor("osb", [128, 2 * D], F32))
    st6e = p6.enter_context(nc.sbuf_tensor("st6e", [128, 24], F32))
    mse = p6.enter_context(nc.sbuf_tensor("mse", [128, 4], F32))
    scl6 = p6.enter_context(nc.sbuf_tensor("scl6", [128, 2], F32))
    bcl6 = p6.enter_context(nc.sbuf_tensor("bcl6", [128, 2], F32))

    with nc.Block() as blk:
        fence(blk)
        eT, eV, eS, eG, eY = blk.tensor, blk.vector, blk.scalar, blk.gpsimd, blk.sync
        ps_t6 = [p6.enter_context(nc.psum_tensor(f"ps_t6{i}", [128, 512], BF16)) for i in range(2)]
        f_ps = [p6.enter_context(nc.psum_tensor(f"f_ps{i}", [128, D], F32)) for i in range(2)]

        t6_ring = Ring(2)
        for tt in range(16):
            slot = tt % 2

            def _(t_, tt=tt, slot=slot):
                t6_ring.acquire(t_, slot)
                for half in range(2):
                    mi = t_.transpose(ps_t6[slot][:, 128 * half:128 * (half + 1)],
                                      comb_sb[:, ES * tt + 128 * half: ES * tt + 128 * (half + 1)],
                                      ident_sb[:])
                mi.then_inc(t2v.h, 1); t2v.n += 1
            eT(_)

            def _(v, tt=tt, slot=slot):
                t2v.wait(v)
                for half in range(2):
                    mi = v.tensor_copy(out=combT[:, TQ * half + 128 * tt: TQ * half + 128 * (tt + 1)],
                                       in_=ps_t6[slot][:, 128 * half:128 * (half + 1)])
                mi.then_inc(v2t.h, 1); v2t.n += 1
            eV(_)
            t6_ring.release(slot, v2t)
        combT_done = v2t.n

        fps_ring, fln_ring = Ring(2), Ring(2)
        x2rr_ring, o_ring = Ring(2), Ring(2)
        p6bat = SBat()
        for tt in range(16):
            slot = tt % 2

            def _(sy, tt=tt, slot=slot):
                if tt == 0:
                    sy.wait_ge(dsp[0].h, dsp[0].n)
                    sy.wait_ge(dsp[1].h, dsp[1].n)
                x2rr_ring.acquire(sy, slot)
                dma(sy, dx6[slot], x2rr[:, slot * D:(slot + 1) * D],
                    x2sp[128 * tt:128 * (tt + 1), :])
            eY(_)

            def _(t_, tt=tt, slot=slot):
                if tt == 0:
                    t_.wait_ge(v2t.h, combT_done)
                    t_.wait_ge(dsg[0].h, dsg[0].n)
                    t_.wait_ge(dsg[1].h, dsg[1].n)
                fps_ring.acquire(t_, slot)
                for dh in range(2):
                    for half in range(2):
                        mi = t_.matmul(f_ps[slot][:, 512 * dh:512 * (dh + 1)],
                                       combT[:, TQ * half + 128 * tt: TQ * half + 128 * (tt + 1)],
                                       slout[:, D * half + 512 * dh: D * half + 512 * (dh + 1)],
                                       start=(half == 0), stop=(half == 1))
                mi.then_inc(t2s.h, 1); t2s.n += 1
            eT(_)

            def _(v, slot=slot):
                t2s.wait(v)
                v.bn_stats(out=st6e[:, 12 * slot:12 * slot + 6],
                           in_=f_ps[slot][:, 0:512])
                v.bn_stats(out=st6e[:, 12 * slot + 6:12 * slot + 12],
                           in_=f_ps[slot][:, 512:1024])
                v.drain()
                v.bn_aggr(out=mse[:, 2 * slot:2 * slot + 2],
                          in_=st6e[:, 12 * slot:12 * slot + 12])
                v.drain()
                v.tensor_scalar(out=scl6[:, slot:slot + 1],
                                in0=mse[:, 2 * slot + 1:2 * slot + 2],
                                scalar1=LN_EPS, scalar2=None, op0=ALU.add).then_inc(v2s.h, 1); v2s.n += 1
            eV(_)

            vw = v2s.n
            s2v.n += 1

            def _j(s, slot=slot, vw=vw):
                s.wait_ge(v2s.h, vw)
                s.activation(out=scl6[:, slot:slot + 1], in_=scl6[:, slot:slot + 1],
                             func=ACT.Sqrt).then_inc(s2v.h, 1)
            p6bat.add(_j)

            def _(v, slot=slot):
                s2v.wait(v)
                v.reciprocal(out=scl6[:, slot:slot + 1], in_=scl6[:, slot:slot + 1])
                v.tensor_scalar(out=bcl6[:, slot:slot + 1],
                                in0=mse[:, 2 * slot:2 * slot + 1],
                                scalar1=-1.0, scalar2=None, op0=ALU.mult)
                v.drain()
                fln_ring.acquire(v, slot)
                v.tensor_scalar(out=fln[:, slot * D:(slot + 1) * D],
                                in0=f_ps[slot][:, :],
                                scalar1=bcl6[:, slot:slot + 1],
                                scalar2=scl6[:, slot:slot + 1],
                                op0=ALU.add, op1=ALU.mult).then_inc(v2g.h, 1)
                v2g.n += 1
            eV(_)
            fps_ring.release(slot, v2g)

            def _(g, slot=slot):
                v2g.wait(g)
                dx6[slot].wait(g)
                o_ring.acquire(g, slot)
                g.tensor_tensor(out=osb[:, slot * D:(slot + 1) * D],
                                in0=fln[:, slot * D:(slot + 1) * D],
                                in1=x2rr[:, slot * D:(slot + 1) * D],
                                op=ALU.add).then_inc(g2t.h, 1)
                g2t.n += 1
            eG(_)
            fln_ring.release(slot, g2t)
            x2rr_ring.release(slot, g2t)

            def _(sy, tt=tt, slot=slot):
                g2t.wait(sy)
                dma(sy, dso[slot], out[128 * tt:128 * (tt + 1), :],
                    osb[:, slot * D:(slot + 1) * D])
            eY(_)
            o_ring.release(slot, dso[slot])
        p6bat.flush(eS)

        def _(sy):
            sy.wait_ge(dso[0].h, dso[0].n)
            sy.wait_ge(dso[1].h, dso[1].n)
            sy.wait_ge(ddb.h, ddb.n)
        eY(_)

    p6.close()
    st.close()
    return nc, dbg_out


def make_tables(h):
    g0 = 2048 * h - 128
    pos = (g0 + np.arange(TKV)).astype(np.float64)
    invf = 1.0 / (10000.0 ** (np.arange(0, HD, 2, dtype=np.float64) / HD))
    hdmod = np.arange(128) % 64
    fi = hdmod % 32
    theta = pos[None, :] * invf[fi][:, None]
    cosr = np.cos(theta).astype(np.float32)
    sign = np.where(hdmod < 32, -1.0, 1.0)
    sinr = (np.sin(theta) * sign[:, None]).astype(np.float32)
    return cosr, sinr


def make_mask(h):
    m = np.zeros((NT, 128, 384), np.float32)
    for c in range(NT):
        q0 = qws(c)
        tk = 128 * c + np.arange(128)
        tq = q0 + np.arange(384)
        d = tq[None, :] - tk[:, None]
        valid = (d >= -128) & (d <= 127)
        posk = (2048 * h - 128) + tk
        valid &= ((posk >= 0) & (posk < 4096))[:, None]
        valid &= ((tq >= 128) & (tq < 2176))[None, :]
        m[c][valid] = 1.0
    return np.ascontiguousarray(m.transpose(1, 0, 2).reshape(128, NT * 384))


def chunk_rows(a):
    R, C = a.shape
    n = R // 128
    return np.ascontiguousarray(a.reshape(n, 128, C).transpose(1, 0, 2).reshape(128, n * C))


def make_inputs(inputs):
    x = np.asarray(inputs["L0_attn_x"], np.float32)
    w_qkv = np.asarray(inputs["L0_attn_w_qkv"], np.float32)
    w_out = np.asarray(inputs["L0_attn_w_out"], np.float32)
    b_out = np.asarray(inputs["L0_attn_b_out"], np.float32)
    gam = np.asarray(inputs["L0_ffn_ln_gamma"], np.float32)
    bet = np.asarray(inputs["L0_ffn_ln_beta"], np.float32)
    mu = np.asarray(inputs["L0_ffn_mu"], np.float32)
    scale = np.asarray(inputs["L0_ffn_scale"], np.float32)
    w1 = np.asarray(inputs["L0_ffn_w1"], np.float32)
    w2 = np.asarray(inputs["L0_ffn_w2"], np.float32)
    b1 = np.asarray(inputs["L0_ffn_b1"], np.float32)
    b2 = np.asarray(inputs["L0_ffn_b2"], np.float32)

    wq_l = chunk_rows(np.ascontiguousarray(w_qkv.T))
    wo_l = chunk_rows(np.ascontiguousarray(w_out.T))
    muh_l = chunk_rows(np.ascontiguousarray(mu.reshape(D, ES)))
    w1_l = chunk_rows(w1.reshape(E * D, HID))
    w2_l = chunk_rows(w2.reshape(E * 2 * 128, D))
    b1_l = np.ascontiguousarray(b1.reshape(1, E * HID))
    b2_l = np.ascontiguousarray(b2.reshape(1, E * D))
    bor_l = np.ascontiguousarray(b_out.reshape(1, D))
    gbc_l = np.ascontiguousarray(np.broadcast_to(gam[None, :], (128, D)))
    bbc_l = np.ascontiguousarray(np.broadcast_to(bet[None, :], (128, D)))
    scbc_l = np.ascontiguousarray(np.broadcast_to(scale.reshape(1, 1), (128, 1)))
    identb = np.eye(128, dtype=ml_dtypes.bfloat16)
    pm = np.zeros((128, 128), dtype=ml_dtypes.bfloat16)
    for mcol in range(128):
        src = mcol + 32 if (mcol % 64) < 32 else mcol - 32
        pm[src, mcol] = 1
    onesc_l = np.ones((128, 1), dtype=ml_dtypes.bfloat16)
    onesr_l = np.ones((1, 128), dtype=ml_dtypes.bfloat16)

    in_maps = []
    for c in range(8):
        b, h = c // 2, c % 2
        g0 = 2048 * h - 128
        xpad = np.zeros((TKV, D), np.float32)
        lo = max(0, -g0)
        hi = TKV - max(0, (g0 + TKV) - 4096)
        xpad[lo:hi] = x[b, g0 + lo: g0 + hi]
        cosr_l, sinr_l = make_tables(h)
        in_maps.append({
            "xp": xpad, "wq": wq_l, "wo": wo_l, "muh": muh_l,
            "w1h": w1_l, "w2h": w2_l, "b1r": b1_l, "b2r": b2_l, "bor": bor_l,
            "gbc": gbc_l, "bbc": bbc_l, "scbc": scbc_l,
            "cosr": cosr_l, "sinr": sinr_l, "maskh": make_mask(h),
            "identb": identb, "permb": pm, "onesc": onesc_l, "onesr": onesr_l,
            "onesrf": np.ones((1, 128), np.float32),
        })
    return in_maps


_NC_CACHE = {}


def kernel(**inputs):
    key = (STAGE, tuple(DBG))
    if key not in _NC_CACHE:
        _NC_CACHE[key] = build_program()
    nc, _ = _NC_CACHE[key]
    in_maps = make_inputs(inputs)
    res = run_bass_kernel_spmd(nc, in_maps, core_ids=list(range(8)))
    outp = np.empty((4, 4096, D), np.float32)
    for c in range(8):
        b, h = c // 2, c % 2
        outp[b, 2048 * h:2048 * (h + 1), :] = res.results[c]["out"]
    return outp


# version 13
# speedup vs baseline: 1.0374x; 1.0374x over previous
import numpy as np
import ml_dtypes
from contextlib import ExitStack

import concourse.bass as bass
import concourse.mybir as mybir
from concourse.bass_utils import run_bass_kernel_spmd

F32 = mybir.dt.float32
BF16 = mybir.dt.bfloat16
ALU = mybir.AluOpType
ACT = mybir.ActivationFunctionType

D = 1024
H = 16
HD = 64
DCH = 8
TKV = 2304
NT = 18
TQ = 2048
E, NS, HID = 16, 16, 256
ES = 256
LN_EPS = 1e-5
MASKVAL = -10000.0

STAGE = 99
DBG = ()


def qws(c):
    return min(max(0, 128 * c - 128), TKV - 384)


def tset(c):
    return [t for t in (c - 1, c, c + 1) if 1 <= t <= 16]


class Sem:
    def __init__(self, stack, nc, name):
        self.h = stack.enter_context(nc.semaphore(name))
        self.n = 0

    def wait(self, eng, n=None):
        eng.wait_ge(self.h, self.n if n is None else n)


class Ring:
    def __init__(self, nslots):
        self.rel = [[] for _ in range(nslots)]

    def acquire(self, eng, slot):
        for sem, val in self.rel[slot]:
            eng.wait_ge(sem.h, val)

    def waits(self, slot):
        return tuple((sem.h, val) for sem, val in self.rel[slot])

    def release(self, slot, sem):
        self.rel[slot] = [(sem, sem.n)]

    def release_add(self, slot, sem):
        self.rel[slot].append((sem, sem.n))


class SBat:

    def __init__(self):
        self.jobs = []

    def add(self, fn):
        self.jobs.append(fn)

    def flush(self, eS):
        if not self.jobs:
            return
        jobs = tuple(self.jobs)
        self.jobs = []

        def _(s, jobs=jobs):
            for fn in jobs:
                fn(s)
        eS(_)


def build_program(stage=STAGE, dbg=DBG):
    nc = bass.Bass("TRN2", target_bir_lowering=False)
    st = ExitStack()

    def din(name, shape, dt=F32):
        return nc.dram_tensor(name, shape, dt, kind="ExternalInput").ap()

    xp = din("xp", [TKV, D])
    wq = din("wq", [128, DCH * 3 * D])
    wo = din("wo", [128, DCH * D])
    muh = din("muh", [128, DCH * ES])
    w1h = din("w1h", [128, E * DCH * HID])
    w2h = din("w2h", [128, E * 2 * D])
    b1r = din("b1r", [1, E * HID])
    b2r = din("b2r", [1, E * D])
    bor = din("bor", [1, D])
    gbc = din("gbc", [128, D])
    bbc = din("bbc", [128, D])
    scbc = din("scbc", [128, 1])
    cosr = din("cosr", [128, TKV])
    sinr = din("sinr", [128, TKV])
    maskh = din("maskh", [128, NT * 384])
    identb = din("identb", [128, 128], BF16)
    permb = din("permb", [128, 128], BF16)
    onesc = din("onesc", [128, 1], BF16)
    onesr = din("onesr", [1, 128], BF16)
    onesrf = din("onesrf", [1, 128])

    out = nc.dram_tensor("out", [TQ, D], F32, kind="ExternalOutput").ap()
    x2sp = nc.dram_tensor("x2sp", [TQ, D], F32).ap()
    ccin = nc.dram_tensor("ccin", [128, 2050], F32)
    ccout = nc.dram_tensor("ccout", [128, 2050], F32)

    dbg_shapes = {
        "xln": [TKV, D], "qT": [128, DCH * TKV], "kT": [128, DCH * TKV],
        "vtm": [128, NT * D], "attn": [128, DCH * TQ], "x2": [TQ, D],
        "xn": [128, 16 * 1056], "ee": [128, 16 * ES], "comb": [128, 16 * ES],
        "usb": [128, 2050], "slin": [128, 2 * D], "slout": [128, 2 * D],
    }
    dbg_out = {k: nc.dram_tensor("dbg_" + k, dbg_shapes[k], F32, kind="ExternalOutput").ap()
               for k in dbg}

    def sb(name, shape, dt):
        return st.enter_context(nc.sbuf_tensor(name, shape, dt))

    ident_sb = sb("ident_sb", [128, 128], BF16)
    perm_sb = sb("perm_sb", [128, 128], BF16)
    onesc_sb = sb("onesc_sb", [128, 1], BF16)
    onesr_sb = sb("onesr_sb", [1, 128], BF16)
    onesrf_sb = sb("onesrf_sb", [1, 128], F32)
    pA = ExitStack()
    qT = pA.enter_context(nc.sbuf_tensor("qT", [128, DCH * TKV], BF16))
    kT = pA.enter_context(nc.sbuf_tensor("kT", [128, DCH * TKV], BF16))
    vtm = pA.enter_context(nc.sbuf_tensor("vtm", [128, NT * 1040], BF16))

    names = ["t2s", "t2v", "s2t", "v2t", "s2v", "v2s", "v2y", "cc", "v2g", "g2t", "sacc",
             "dcs", "dcg", "ccd", "ddb",
             "dlx0", "dlx1", "dxr0", "dxr1", "dsp0", "dsp1",
             "dxc0", "dxc1", "dxc2", "dxc3", "dsg0", "dsg1",
             "dw20", "dw21", "dw22", "dw23",
             "dx60", "dx61", "dso0", "dso1"]
    sm = {n: Sem(st, nc, "sm_" + n) for n in names}
    t2s, t2v, s2t, v2t = sm["t2s"], sm["t2v"], sm["s2t"], sm["v2t"]
    s2v, v2s, v2y, ccS = sm["s2v"], sm["v2s"], sm["v2y"], sm["cc"]
    dcs, dcg, ccd, ddb = sm["dcs"], sm["dcg"], sm["ccd"], sm["ddb"]
    v2g, g2t, sacc = sm["v2g"], sm["g2t"], sm["sacc"]
    dlx = [sm["dlx0"], sm["dlx1"]]
    dxr = [sm["dxr0"], sm["dxr1"]]
    dsp = [sm["dsp0"], sm["dsp1"]]
    dxc = [sm["dxc0"], sm["dxc1"], sm["dxc2"], sm["dxc3"]]
    dw2 = [sm["dw20"], sm["dw21"], sm["dw22"], sm["dw23"]]
    dsg = [sm["dsg0"], sm["dsg1"]]
    dx6 = [sm["dx60"], sm["dx61"]]
    dso = [sm["dso0"], sm["dso1"]]

    def dma(eng, sem, out_ap, in_ap):
        eng.dma_start(out=out_ap, in_=in_ap).then_inc(sem.h, 16)
        sem.n += 16

    def fence(blk):
        vals = [(s.h, s.n) for s in sm.values() if s.n > 0]

        def _f(eng):
            for hh, nn in vals:
                eng.wait_ge(hh, nn)
        blk.tensor(_f)
        blk.vector(_f)
        blk.scalar(_f)
        blk.gpsimd(_f)
        blk.sync(_f)

    p1 = ExitStack()
    wq_sb = p1.enter_context(nc.sbuf_tensor("wq_sb", [128, DCH * 3 * D], BF16))
    cos_sb = p1.enter_context(nc.sbuf_tensor("cos_sb", [128, TKV], BF16))
    sin_sb = p1.enter_context(nc.sbuf_tensor("sin_sb", [128, TKV], BF16))
    xtile = p1.enter_context(nc.sbuf_tensor("xtile", [128, 2 * D], F32))
    xln = p1.enter_context(nc.sbuf_tensor("xln", [128, 4 * D], BF16))
    xlnT = p1.enter_context(nc.sbuf_tensor("xlnT", [128, DCH * 512], BF16))
    stat6 = p1.enter_context(nc.sbuf_tensor("stat6", [128, 48], F32))
    mstat = p1.enter_context(nc.sbuf_tensor("mstat", [128, 8], F32))
    scol = p1.enter_context(nc.sbuf_tensor("scol", [128, 4], F32))
    bcol = p1.enter_context(nc.sbuf_tensor("bcol", [128, 4], F32))
    qraw = p1.enter_context(nc.sbuf_tensor("qraw", [128, 2 * 512], BF16))
    qpm = p1.enter_context(nc.sbuf_tensor("qpm", [128, 2 * 512], BF16))

    xring = Ring(2)
    with nc.Block() as blk:
        eT, eV, eS, eG, eY = blk.tensor, blk.vector, blk.scalar, blk.gpsimd, blk.sync

        def _(g):
            dma(g, dcg, wq_sb[:], wq)
            dma(g, dcg, cos_sb[:], cosr)
            dma(g, dcg, sin_sb[:], sinr)
        eG(_)
        const_dcg = dcg.n

        def _(sy):
            dma(sy, dcs, ident_sb[:], identb)
            dma(sy, dcs, perm_sb[:], permb)
            dma(sy, dcs, onesc_sb[:], onesc)
            dma(sy, dcs, onesr_sb[:], onesr)
            dma(sy, dcs, onesrf_sb[:], onesrf)
        eY(_)
        const_dcs = dcs.n

        ps_tp = [p1.enter_context(nc.psum_tensor(f"ps_tp{i}", [128, 512], BF16)) for i in range(2)]
        ps_qk = [p1.enter_context(nc.psum_tensor(f"ps_qk{i}", [128, 512], F32)) for i in range(3)]
        ps_pm = [p1.enter_context(nc.psum_tensor(f"ps_pm{i}", [128, 512], F32)) for i in range(2)]

        tp_ring, qk_ring, pm_ring = Ring(2), Ring(3), Ring(2)
        qraw_ring, qpm_ring = Ring(2), Ring(2)
        xln_guard = [0]

        STS = [(0, 4), (4, 4), (8, 4), (12, 4), (16, 2)]
        for si, (t0, ntl) in enumerate(STS):
            W = 128 * ntl
            sbat = SBat()
            for h2 in range(0, ntl, 2):
                sub = list(range(h2, min(h2 + 2, ntl)))
                for i in sub:
                    t = t0 + i
                    xs = t % 2

                    def _(sy, t=t, xs=xs):
                        xring.acquire(sy, xs)
                        dma(sy, dlx[xs], xtile[:, xs * D:(xs + 1) * D],
                            xp[128 * t:128 * (t + 1), :])
                    eY(_)

                    def _(v, i=i, xs=xs):
                        dlx[xs].wait(v)
                        v.bn_stats(out=stat6[:, 12 * i:12 * i + 6],
                                   in_=xtile[:, xs * D:xs * D + 512])
                        v.bn_stats(out=stat6[:, 12 * i + 6:12 * i + 12],
                                   in_=xtile[:, xs * D + 512:xs * D + 1024])
                    eV(_)

                def _(v, sub=tuple(sub), h2=h2):
                    v.drain()
                    for i in sub:
                        v.bn_aggr(out=mstat[:, 2 * i:2 * i + 2],
                                  in_=stat6[:, 12 * i:12 * i + 12])
                    v.drain()
                    nn = len(sub)
                    va = mstat.ap().rearrange("p (t two) -> p t two", two=2)
                    v.tensor_scalar(out=scol[:, h2:h2 + nn], in0=va[:, h2:h2 + nn, 1],
                                    scalar1=LN_EPS, scalar2=None, op0=ALU.add).then_inc(v2s.h, 1); v2s.n += 1
                eV(_)

                vw = v2s.n
                s2v.n += 1

                def _j(s, h2=h2, nn=len(sub), vw=vw):
                    s.wait_ge(v2s.h, vw)
                    s.activation(out=scol[:, h2:h2 + nn], in_=scol[:, h2:h2 + nn],
                                 func=ACT.Sqrt).then_inc(s2v.h, 1)
                sbat.add(_j)

                def _(v, sub=tuple(sub), h2=h2):
                    s2v.wait(v)
                    nn = len(sub)
                    v.reciprocal(out=scol[:, h2:h2 + nn], in_=scol[:, h2:h2 + nn])
                    va = mstat.ap().rearrange("p (t two) -> p t two", two=2)
                    v.tensor_scalar(out=bcol[:, h2:h2 + nn], in0=va[:, h2:h2 + nn, 0],
                                    scalar1=-1.0, scalar2=None, op0=ALU.mult)
                    v.drain()
                    v.nop().then_inc(v2s.h, 1); v2s.n += 1
                eV(_)

                for i in sub:
                    xs = (t0 + i) % 2

                    def _(v, i=i, xs=xs, xg=xln_guard[0]):
                        if xg:
                            v.wait_ge(ddb.h, xg)
                        v.tensor_scalar(out=xln[:, i * D:(i + 1) * D],
                                        in0=xtile[:, xs * D:(xs + 1) * D],
                                        scalar1=bcol[:, i:i + 1], scalar2=scol[:, i:i + 1],
                                        op0=ALU.add, op1=ALU.mult).then_inc(s2t.h, 1)
                        s2t.n += 1
                    eV(_)
                    xring.release(xs, s2t)
                    if "xln" in dbg_out:
                        t = t0 + i

                        def _(g, t=t, i=i):
                            s2t.wait(g)
                            ddb.wait(g)
                            dma(g, ddb, dbg_out["xln"][128 * t:128 * (t + 1), :],
                                xln[:, i * D:(i + 1) * D])
                        eG(_)

            for c in range(DCH):
                slot = c % 2

                def _(t_, c=c, ntl=ntl, slot=slot):
                    if c == 0:
                        s2t.wait(t_)
                        t_.wait_ge(dcs.h, const_dcs)
                    tp_ring.acquire(t_, slot)
                    for i in range(ntl):
                        mi = t_.transpose(ps_tp[slot][:, 128 * i:128 * (i + 1)],
                                          xln[:, i * D + 128 * c: i * D + 128 * (c + 1)],
                                          ident_sb[:])
                    mi.then_inc(t2v.h, 1); t2v.n += 1
                eT(_)

                def _(v, c=c, W=W, slot=slot):
                    t2v.wait(v)
                    v.tensor_copy(out=xlnT[:, 512 * c:512 * c + W],
                                  in_=ps_tp[slot][:, 0:W]).then_inc(v2t.h, 1)
                    v2t.n += 1
                eV(_)
                tp_ring.release(slot, v2t)
            xlnT_done = v2t.n

            qkv_val = {}

            def emit_qkj(j, W=W):
                slot = j % 3

                def _(t_, j=j, W=W, slot=slot):
                    if j == 0:
                        t_.wait_ge(v2t.h, xlnT_done)
                        t_.wait_ge(dcg.h, const_dcg)
                    qk_ring.acquire(t_, slot)
                    for c in range(DCH):
                        mi = t_.matmul(ps_qk[slot][:, 0:W],
                                       wq_sb[:, 3072 * c + 128 * j: 3072 * c + 128 * (j + 1)],
                                       xlnT[:, 512 * c:512 * c + W],
                                       start=(c == 0), stop=(c == DCH - 1))
                    mi.then_inc(t2s.h, 1); t2s.n += 1
                eT(_)
                qkv_val[j] = t2s.n

            def emit_tail(j, t0=t0, W=W):
                slot = j % 3
                ss = j % 2

                qw = qkv_val[j]
                rw = qraw_ring.waits(ss)
                s2t.n += 1

                def _j(s, W=W, slot=slot, ss=ss, qw=qw, rw=rw):
                    s.wait_ge(t2s.h, qw)
                    for hh, vv in rw:
                        s.wait_ge(hh, vv)
                    s.copy(out=qraw[:, 512 * ss:512 * ss + W],
                           in_=ps_qk[slot][:, 0:W]).then_inc(s2t.h, 1)
                sbat.add(_j)
                ev_val = s2t.n
                qk_ring.release(slot, s2t)

                def _(t_, W=W, ss=ss, ev=ev_val):
                    t_.wait_ge(s2t.h, ev)
                    pm_ring.acquire(t_, ss)
                    t_.matmul(ps_pm[ss][:, 0:W], perm_sb[:],
                              qraw[:, 512 * ss:512 * ss + W],
                              start=True, stop=True).then_inc(t2v.h, 1)
                    t2v.n += 1
                eT(_)
                pm_val = t2v.n

                def _(v, t0=t0, W=W, ss=ss, pv=pm_val):
                    v.wait_ge(t2v.h, pv)
                    qpm_ring.acquire(v, ss)
                    v.tensor_tensor(out=qraw[:, 512 * ss:512 * ss + W],
                                    in0=qraw[:, 512 * ss:512 * ss + W],
                                    in1=cos_sb[:, 128 * t0:128 * t0 + W], op=ALU.mult)
                    v.tensor_tensor(out=qpm[:, 512 * ss:512 * ss + W],
                                    in0=ps_pm[ss][:, 0:W],
                                    in1=sin_sb[:, 128 * t0:128 * t0 + W],
                                    op=ALU.mult).then_inc(v2g.h, 1)
                    v2g.n += 1
                eV(_)
                pm_ring.release(ss, v2g)

                def _(g, j=j, t0=t0, W=W, ss=ss):
                    v2g.wait(g)
                    dest = qT if j < 8 else kT
                    jj = j % 8
                    c0 = TKV * jj + 128 * t0
                    g.tensor_tensor(out=dest[:, c0:c0 + W],
                                    in0=qraw[:, 512 * ss:512 * ss + W],
                                    in1=qpm[:, 512 * ss:512 * ss + W],
                                    op=ALU.add).then_inc(g2t.h, 1)
                    g2t.n += 1
                eG(_)
                qraw_ring.release(ss, g2t)
                qpm_ring.release(ss, g2t)

            for j in range(16):
                emit_qkj(j)
                if j >= 1:
                    emit_tail(j - 1)
            emit_tail(15)
            sbat.flush(eS)

            if "xln" in dbg_out:
                xln_guard[0] = ddb.n
            vmm_val = {}

            def emit_vmm(i, hf):
                slot = (2 * i + hf) % 3

                def _(t_, i=i, hf=hf, slot=slot):
                    qk_ring.acquire(t_, slot)
                    for c in range(DCH):
                        mi = t_.matmul(ps_qk[slot][:, 0:512],
                                       xlnT[:, 512 * c + 128 * i: 512 * c + 128 * (i + 1)],
                                       wq_sb[:, 3072 * c + 2048 + 512 * hf: 3072 * c + 2048 + 512 * (hf + 1)],
                                       start=(c == 0), stop=(c == DCH - 1))
                    mi.then_inc(t2v.h, 1); t2v.n += 1
                eT(_)
                vmm_val[(i, hf)] = t2v.n

            def emit_vevac(i, hf, t0=t0):
                t = t0 + i
                slot = (2 * i + hf) % 3

                def _(v, t=t, hf=hf, slot=slot, mv=vmm_val[(i, hf)]):
                    v.wait_ge(t2v.h, mv)
                    blkap = vtm.ap()[:, 1040 * t: 1040 * (t + 1)].rearrange(
                        "p (g w) -> p g w", w=65)
                    if hf == 0:
                        v.memset(blkap[:, :, 64:65], 1.0)
                    dst8 = blkap[:, 8 * hf:8 * (hf + 1), 0:64]
                    src8 = ps_qk[slot].ap()[:, 0:512].rearrange(
                        "p (g w) -> p g w", w=64)
                    v.tensor_copy(out=dst8, in_=src8).then_inc(v2t.h, 1)
                    v2t.n += 1
                eV(_)
                qk_ring.release(slot, v2t)

            vjobs = [(i, hf) for i in range(ntl) for hf in range(2)]
            for n_, (i, hf) in enumerate(vjobs):
                emit_vmm(i, hf)
                if n_ >= 1:
                    emit_vevac(*vjobs[n_ - 1])
            emit_vevac(*vjobs[-1])

        if any(k in dbg_out for k in ("qT", "kT", "vtm")):
            def _(g):
                v2t.wait(g)
                ddb.wait(g)
                for name, sbuf in (("qT", qT), ("kT", kT), ("vtm", vtm)):
                    if name in dbg_out:
                        dma(g, ddb, dbg_out[name], sbuf[:])
                g.wait_ge(ddb.h, ddb.n)
            eG(_)

    p1.close()
    if stage <= 2:
        pA.close()
        st.close()
        return nc, dbg_out

    wo_sb = pA.enter_context(nc.sbuf_tensor("wo_sb", [128, DCH * D], BF16))
    attn_T = pA.enter_context(nc.sbuf_tensor("attn_T", [128, DCH * TQ], BF16))
    p2 = ExitStack()
    mask_sb = p2.enter_context(nc.sbuf_tensor("mask_sb", [128, NT * 384], BF16))
    pb = p2.enter_context(nc.sbuf_tensor("pb", [128, 4 * 384], BF16))
    sums_sb = p2.enter_context(nc.sbuf_tensor("sums_sb", [16, TQ], F32))
    sums_st = p2.enter_context(nc.sbuf_tensor("sums_st", [1, TQ], F32))
    sums_bf = p2.enter_context(nc.sbuf_tensor("sums_bf", [16, TQ], BF16))
    rrow_bf = p2.enter_context(nc.sbuf_tensor("rrow_bf", [1, TQ], BF16))
    rb_sb = p2.enter_context(nc.sbuf_tensor("rb_sb", [128, 2 * TQ], BF16))

    with nc.Block() as blk:
        fence(blk)
        eT, eV, eS, eG, eY = blk.tensor, blk.vector, blk.scalar, blk.gpsimd, blk.sync

        def _(g):
            g.wait_ge(dcg.h, dcg.n)
            dma(g, dcg, mask_sb[:], maskh)
            dma(g, dcg, wo_sb[:], wo)
        eG(_)
        p2_loads = dcg.n

        at_ps = p2.enter_context(nc.psum_tensor("at_ps", [128, TQ], F32))
        p2s = ExitStack()
        scp = p2s.enter_context(nc.psum_tensor("scp", [128, 4 * 512], F32))

        sc_ring, pb_ring, at_ring = Ring(4), Ring(4), Ring(1)
        sums_stg_ring = Ring(1)

        pv_pairs = [(c, t) for c in range(NT) for t in tset(c)]
        pv_last = {}
        for c, t in pv_pairs:
            pv_last[(t - 1) // 4] = (c, t)
        p2bat = SBat()
        for h in range(H):
            jj, po = h // 2, 64 * (h % 2)
            fresh = set(range(4))
            qk_val = {}

            def emit_qk2(g, h=h, jj=jj, po=po):
                for c in (2 * g, 2 * g + 1):
                    slot = c % 4

                    def _(t_, c=c, slot=slot, h=h, jj=jj, po=po):
                        if h == 0 and c == 0:
                            t_.wait_ge(dcg.h, p2_loads)
                        sc_ring.acquire(t_, slot)
                        q0 = qws(c)
                        t_.matmul(scp[:, 512 * slot:512 * slot + 384],
                                  kT[po:po + 64, TKV * jj + 128 * c: TKV * jj + 128 * (c + 1)],
                                  qT[po:po + 64, TKV * jj + q0: TKV * jj + q0 + 384],
                                  start=True, stop=True,
                                  tile_position=(po, 0)).then_inc(t2s.h, 1)
                        t2s.n += 1
                    eT(_)
                qk_val[g] = t2s.n

            def emit_exp_pv(g, h=h, fresh=fresh):
                c0 = 2 * g
                s0 = c0 % 4

                qw = qk_val[g]
                rw = pb_ring.waits(s0) + pb_ring.waits(s0 + 1)
                s2t.n += 1

                def _j(s, s0=s0, qw=qw, rw=rw):
                    s.wait_ge(t2s.h, qw)
                    for hh, vv in rw:
                        s.wait_ge(hh, vv)
                    inap = scp.ap()[:, 512 * s0:512 * s0 + 1024].rearrange(
                        "p (two w) -> p two w", w=512)[:, :, 0:384]
                    outap = pb.ap()[:, 384 * s0:384 * (s0 + 2)].rearrange(
                        "p (two w) -> p two w", w=384)
                    s.activation(out=outap, in_=inap, func=ACT.Exp,
                                 scale=0.125, bias=0.0).then_inc(s2t.h, 1)
                p2bat.add(_j)
                exp_val = s2t.n
                sc_ring.release(s0, s2t)
                sc_ring.release(s0 + 1, s2t)

                def _(gq, g=g, s0=s0, c0=c0, ev=exp_val):
                    gq.wait_ge(s2t.h, ev)
                    gq.tensor_tensor(out=pb[:, 384 * s0:384 * (s0 + 2)],
                                     in0=pb[:, 384 * s0:384 * (s0 + 2)],
                                     in1=mask_sb[:, 384 * c0:384 * (c0 + 2)],
                                     op=ALU.mult).then_inc(g2t.h, 1)
                    g2t.n += 1
                (eG if g % 2 else eV)(_)
                msk_val = g2t.n

                for c in (c0, c0 + 1):
                    pslot = c % 4

                    def _(t_, c=c, pslot=pslot, h=h, fresh=fresh, ev=msk_val):
                        t_.wait_ge(g2t.h, ev)
                        if c == 0:
                            at_ring.acquire(t_, 0)
                        q0 = qws(c)
                        for t in tset(c):
                            j0 = 128 * t - q0
                            bank = (t - 1) // 4
                            stf = bank in fresh
                            fresh.discard(bank)
                            last = pv_last[bank] == (c, t)
                            mi = t_.matmul(at_ps[0:65, 128 * (t - 1):128 * t],
                                           vtm[:, 1040 * c + 65 * h: 1040 * c + 65 * h + 65],
                                           pb[:, 384 * pslot + j0: 384 * pslot + j0 + 128],
                                           start=stf, stop=last, tile_position=(0, 0))
                        mi.then_inc(t2v.h, 1); t2v.n += 1
                    eT(_)
                    pb_ring.release(pslot, t2v)

            for g in range(9):
                emit_qk2(g)
                if g >= 1:
                    emit_exp_pv(g - 1)
            emit_exp_pv(8)

            hs = h % 2

            def _(v, h=h):
                t2v.wait(v)
                sums_stg_ring.acquire(v, 0)
                v.tensor_copy(out=sums_st[0:1, :],
                              in_=at_ps[64:65, 0:TQ]).then_inc(s2t.h, 1)
                s2t.n += 1
            eV(_)
            sums_done = s2t.n

            def _(sy, h=h, hs=hs):
                sy.wait_ge(s2t.h, sums_done)
                dma(sy, dxr[hs], sums_sb[h:h + 1, :], sums_st[0:1, :])
            eY(_)
            sums_stg_ring.release(0, dxr[hs])

            tw = t2v.n
            v2t.n += 1

            def _j(s, jj=jj, po=po, tw=tw):
                s.wait_ge(t2v.h, tw)
                s.copy(out=attn_T[po:po + 64, TQ * jj:TQ * (jj + 1)],
                       in_=at_ps[0:64, 0:TQ]).then_inc(v2t.h, 1)
            p2bat.add(_j)

            def _(t_, sd=sums_done):
                t_.wait_ge(s2t.h, sd)
            eT(_)
            at_ring.release(0, v2t)
            at_ring.release_add(0, s2t)
            p2bat.flush(eS)

        def _(v):
            v.wait_ge(dxr[0].h, dxr[0].n)
            v.wait_ge(dxr[1].h, dxr[1].n)
            v.reciprocal(out=sums_sb[:], in_=sums_sb[:])
            v.drain()
            v.tensor_copy(out=sums_bf[:], in_=sums_sb[:]).then_inc(v2t.h, 1)
            v2t.n += 1
        eV(_)
        recip_done_v = v2t.n

        rrow_ring = Ring(1)
        rb_ring = Ring(2)

        p2s.close()
        rb_ps = p2.enter_context(nc.psum_tensor("rb_ps", [128, TQ], F32))
        rbps_ring = Ring(1)
        p2bat2 = SBat()

        for h in range(H):
            jj, po = h // 2, 64 * (h % 2)

            def _(sy, h=h):
                if h == 0:
                    sy.wait_ge(v2t.h, recip_done_v)
                rrow_ring.acquire(sy, 0)
                dma(sy, dlx[0], rrow_bf[0:1, :], sums_bf[h:h + 1, :])
            eY(_)

            def _(t_, po=po):
                dlx[0].wait(t_)
                rbps_ring.acquire(t_, 0)
                for g in range(4):
                    mi = t_.matmul(rb_ps[po:po + 64, 512 * g:512 * (g + 1)],
                                   onesr_sb[0:1, 0:64],
                                   rrow_bf[0:1, 512 * g:512 * (g + 1)],
                                   start=True, stop=True,
                                   tile_position=(0, po))
                mi.then_inc(t2s.h, 1); t2s.n += 1
            eT(_)
            rrow_ring.release(0, t2s)

            hs = h % 2

            tw = t2s.n
            rw = rb_ring.waits(hs)
            s2v.n += 1

            def _j(s, po=po, hs=hs, tw=tw, rw=rw):
                s.wait_ge(t2s.h, tw)
                for hh, vv in rw:
                    s.wait_ge(hh, vv)
                s.copy(out=rb_sb[po:po + 64, TQ * hs:TQ * (hs + 1)],
                       in_=rb_ps[po:po + 64, :]).then_inc(s2v.h, 1)
            p2bat2.add(_j)
            rbps_ring.release(0, s2v)

            def _(v, jj=jj, po=po, hs=hs):
                s2v.wait(v)
                dest = attn_T[po:po + 64, TQ * jj:TQ * (jj + 1)]
                v.tensor_tensor(out=dest, in0=dest,
                                in1=rb_sb[po:po + 64, TQ * hs:TQ * (hs + 1)],
                                op=ALU.mult).then_inc(v2t.h, 1)
                v2t.n += 1
            eV(_)
            rb_ring.release(hs, v2t)
        p2bat2.flush(eS)

        if "attn" in dbg_out:
            def _(g):
                v2t.wait(g)
                ddb.wait(g)
                dma(g, ddb, dbg_out["attn"], attn_T[:])
                g.wait_ge(ddb.h, ddb.n)
            eG(_)

    p2.close()
    if stage <= 3:
        pA.close()
        st.close()
        return nc, dbg_out

    p25 = ExitStack()
    yln = p25.enter_context(nc.sbuf_tensor("yln", [128, 2 * D], F32))
    xrr = p25.enter_context(nc.sbuf_tensor("xrr", [128, 2 * D], F32))
    x2t = p25.enter_context(nc.sbuf_tensor("x2t", [128, 2 * D], F32))
    st6b = p25.enter_context(nc.sbuf_tensor("st6b", [128, 24], F32))
    msb = p25.enter_context(nc.sbuf_tensor("msb", [128, 4], F32))
    scl2 = p25.enter_context(nc.sbuf_tensor("scl2", [128, 2], F32))
    bcl2 = p25.enter_context(nc.sbuf_tensor("bcl2", [128, 2], F32))
    bor_sb = p25.enter_context(nc.sbuf_tensor("bor_sb", [1, D], BF16))

    spill_val = {}
    with nc.Block() as blk:
        fence(blk)
        eT, eV, eS, eG, eY = blk.tensor, blk.vector, blk.scalar, blk.gpsimd, blk.sync

        def _(g):
            g.wait_ge(dcg.h, dcg.n)
            dma(g, dcg, bor_sb[:], bor)
        eG(_)
        p25_loads = dcg.n

        y_ps = [p25.enter_context(nc.psum_tensor(f"y_ps{i}", [128, D], F32)) for i in range(2)]
        yps_ring, yln_ring = Ring(2), Ring(2)
        xrr_ring, x2_ring = Ring(2), Ring(2)

        p25bat = SBat()
        for tt in range(16):
            slot = tt % 2

            def _(sy, tt=tt, slot=slot):
                xrr_ring.acquire(sy, slot)
                dma(sy, dxr[slot], xrr[:, slot * D:(slot + 1) * D],
                    xp[128 * (tt + 1):128 * (tt + 2), :])
            eY(_)

            def _(t_, tt=tt, slot=slot):
                if tt == 0:
                    t_.wait_ge(dcg.h, p25_loads)
                yps_ring.acquire(t_, slot)
                for hf in range(2):
                    t_.matmul(y_ps[slot][:, 512 * hf:512 * (hf + 1)], onesr_sb[:],
                              bor_sb[0:1, 512 * hf:512 * (hf + 1)], start=True, stop=False)
                    for c in range(DCH):
                        mi = t_.matmul(y_ps[slot][:, 512 * hf:512 * (hf + 1)],
                                       attn_T[:, TQ * c + 128 * tt: TQ * c + 128 * (tt + 1)],
                                       wo_sb[:, 1024 * c + 512 * hf: 1024 * c + 512 * (hf + 1)],
                                       start=False, stop=(c == DCH - 1))
                mi.then_inc(t2s.h, 1); t2s.n += 1
            eT(_)

            def _(v, slot=slot):
                t2s.wait(v)
                v.bn_stats(out=st6b[:, 12 * slot:12 * slot + 6],
                           in_=y_ps[slot][:, 0:512])
                v.bn_stats(out=st6b[:, 12 * slot + 6:12 * slot + 12],
                           in_=y_ps[slot][:, 512:1024])
                v.drain()
                v.bn_aggr(out=msb[:, 2 * slot:2 * slot + 2],
                          in_=st6b[:, 12 * slot:12 * slot + 12])
                v.drain()
                v.tensor_scalar(out=scl2[:, slot:slot + 1],
                                in0=msb[:, 2 * slot + 1:2 * slot + 2],
                                scalar1=LN_EPS, scalar2=None, op0=ALU.add).then_inc(v2s.h, 1); v2s.n += 1
            eV(_)

            vw = v2s.n
            s2v.n += 1

            def _j(s, slot=slot, vw=vw):
                s.wait_ge(v2s.h, vw)
                s.activation(out=scl2[:, slot:slot + 1], in_=scl2[:, slot:slot + 1],
                             func=ACT.Sqrt).then_inc(s2v.h, 1)
            p25bat.add(_j)

            def _(v, slot=slot):
                s2v.wait(v)
                v.reciprocal(out=scl2[:, slot:slot + 1], in_=scl2[:, slot:slot + 1])
                v.tensor_scalar(out=bcl2[:, slot:slot + 1],
                                in0=msb[:, 2 * slot:2 * slot + 1],
                                scalar1=-1.0, scalar2=None, op0=ALU.mult)
                v.drain()
                yln_ring.acquire(v, slot)
                v.tensor_scalar(out=yln[:, slot * D:(slot + 1) * D],
                                in0=y_ps[slot][:, :],
                                scalar1=bcl2[:, slot:slot + 1],
                                scalar2=scl2[:, slot:slot + 1],
                                op0=ALU.add, op1=ALU.mult).then_inc(v2g.h, 1)
                v2g.n += 1
            eV(_)
            yps_ring.release(slot, v2g)

            def _(g, slot=slot):
                v2g.wait(g)
                dxr[slot].wait(g)
                x2_ring.acquire(g, slot)
                g.tensor_tensor(out=x2t[:, slot * D:(slot + 1) * D],
                                in0=yln[:, slot * D:(slot + 1) * D],
                                in1=xrr[:, slot * D:(slot + 1) * D],
                                op=ALU.add).then_inc(g2t.h, 1)
                g2t.n += 1
            eG(_)
            yln_ring.release(slot, g2t)
            xrr_ring.release(slot, g2t)

            def _(sy, tt=tt, slot=slot):
                g2t.wait(sy)
                dma(sy, dsp[slot], x2sp[128 * tt:128 * (tt + 1), :],
                    x2t[:, slot * D:(slot + 1) * D])
            eY(_)
            spill_val[tt] = (dsp[slot], dsp[slot].n)
            x2_ring.release(slot, dsp[slot])
            if "x2" in dbg_out:
                def _(g, tt=tt, slot=slot):
                    g2t.wait(g)
                    ddb.wait(g)
                    dma(g, ddb, dbg_out["x2"][128 * tt:128 * (tt + 1), :],
                        x2t[:, slot * D:(slot + 1) * D])
                eG(_)
                x2_ring.release_add(slot, ddb)

        p25bat.flush(eS)

        def _(sy):
            sy.wait_ge(dsp[0].h, dsp[0].n)
            sy.wait_ge(dsp[1].h, dsp[1].n)
        eY(_)

    p25.close()
    pA.close()
    if stage <= 4:
        st.close()
        return nc, dbg_out

    comb_sb = st.enter_context(nc.sbuf_tensor("comb_sb", [128, 16 * ES], BF16))
    slin = st.enter_context(nc.sbuf_tensor("slin", [128, 2 * D], BF16))
    w1_sb = st.enter_context(nc.sbuf_tensor("w1_sb", [128, E * DCH * HID], BF16))
    p3 = ExitStack()
    xn_sb = p3.enter_context(nc.sbuf_tensor("xn_sb", [128, 16 * 1056], BF16))
    E_sb = p3.enter_context(nc.sbuf_tensor("E_sb", [128, 16 * ES], BF16))
    gbc_sb = p3.enter_context(nc.sbuf_tensor("gbc_sb", [128, D], F32))
    bbc_sb = p3.enter_context(nc.sbuf_tensor("bbc_sb", [128, D], F32))
    scbc_sb = p3.enter_context(nc.sbuf_tensor("scbc_sb", [128, 1], F32))
    mu_sb = p3.enter_context(nc.sbuf_tensor("mu_sb", [128, DCH * ES], F32))
    mu2_sb = p3.enter_context(nc.sbuf_tensor("mu2_sb", [128, DCH * ES], BF16))
    nr_sb = p3.enter_context(nc.sbuf_tensor("nr_sb", [1, ES], F32))
    muU_sb = p3.enter_context(nc.sbuf_tensor("muU_sb", [128, DCH * ES], BF16))
    x2r = p3.enter_context(nc.sbuf_tensor("x2r", [128, 4 * D], F32))
    u2sb = p3.enter_context(nc.sbuf_tensor("u2sb", [128, 2 * D], BF16))
    xnT_sb = p3.enter_context(nc.sbuf_tensor("xnT_sb", [128, 2 * 1024], BF16))
    rs_sb = p3.enter_context(nc.sbuf_tensor("rs_sb", [128, 16], F32))
    st6c = p3.enter_context(nc.sbuf_tensor("st6c", [128, 48], F32))
    msc = p3.enter_context(nc.sbuf_tensor("msc", [128, 8], F32))
    s1c = p3.enter_context(nc.sbuf_tensor("s1c", [128, 4], F32))
    s2c = p3.enter_context(nc.sbuf_tensor("s2c", [128, 4], F32))
    b12 = p3.enter_context(nc.sbuf_tensor("b12", [128, 4], F32))
    sq_sb = p3.enter_context(nc.sbuf_tensor("sq_sb", [128, D], F32))
    fac = p3.enter_context(nc.sbuf_tensor("fac", [128, 16], F32))
    ccin_sb = p3.enter_context(nc.sbuf_tensor("ccin_sb", [128, 2050], F32))
    usb = ccin_sb

    with nc.Block() as blk:
        fence(blk)
        eT, eV, eS, eG, eY = blk.tensor, blk.vector, blk.scalar, blk.gpsimd, blk.sync

        def _(g):
            g.wait_ge(dcg.h, dcg.n)
            dma(g, dcg, w1_sb[:], w1h)
        eG(_)

        def _(sy):
            sy.wait_ge(dcs.h, dcs.n)
            dma(sy, dcs, gbc_sb[:], gbc)
            dma(sy, dcs, bbc_sb[:], bbc)
            dma(sy, dcs, scbc_sb[:], scbc)
            dma(sy, dcs, mu_sb[:], muh)
        eY(_)
        p3_loads = dcs.n

        p3s = ExitStack()
        ps_tb = [p3s.enter_context(nc.psum_tensor(f"ps_tb{i}", [128, 512], BF16)) for i in range(2)]
        lg_ps = [p3s.enter_context(nc.psum_tensor(f"lg_ps{i}", [128, 512], F32)) for i in range(2)]
        nb_ps = p3s.enter_context(nc.psum_tensor("nb_ps", [128, ES], F32))

        b4a = SBat()
        dw = p3_loads
        s2t.n += 1

        def _j(s, dw=dw):
            s.wait_ge(dcs.h, dw)
            s.activation(out=mu2_sb[:], in_=mu_sb[:], func=ACT.Square).then_inc(s2t.h, 1)
        b4a.add(_j)

        def _(t_):
            s2t.wait(t_)
            for c in range(DCH):
                mi = t_.matmul(nb_ps[0:1, :], onesc_sb[:], mu2_sb[:, ES * c:ES * (c + 1)],
                               start=(c == 0), stop=(c == DCH - 1))
            mi.then_inc(t2s.h, 1); t2s.n += 1
        eT(_)

        tw = t2s.n
        s2v.n += 1

        def _j(s, tw=tw):
            s.wait_ge(t2s.h, tw)
            s.activation(out=nr_sb[:], in_=nb_ps[0:1, :], func=ACT.Sqrt).then_inc(s2v.h, 1)
        b4a.add(_j)
        b4a.flush(eS)

        def _(v):
            s2v.wait(v)
            v.tensor_scalar(out=nr_sb[:], in0=nr_sb[:], scalar1=1e-12, scalar2=None,
                            op0=ALU.max)
            v.drain()
            v.reciprocal(out=nr_sb[:], in_=nr_sb[:])
            v.drain().then_inc(v2t.h, 1); v2t.n += 1
        eV(_)

        def _(t_):
            v2t.wait(t_)
            t_.matmul(nb_ps[:, :], onesrf_sb[:], nr_sb[0:1, :],
                      start=True, stop=True).then_inc(t2v.h, 1)
            t2v.n += 1
        eT(_)

        def _(v):
            t2v.wait(v)
            for c in range(DCH):
                mi = v.tensor_tensor(out=muU_sb[:, ES * c:ES * (c + 1)],
                                     in0=mu_sb[:, ES * c:ES * (c + 1)], in1=nb_ps[:, :],
                                     op=ALU.mult)
            mi.then_inc(v2t.h, 1); v2t.n += 1
        eV(_)
        muU_done = v2t.n

        tb_ring, lg_ring, xnT_ring = Ring(2), Ring(2), Ring(2)
        x2r_ring, u2_ring = Ring(4), Ring(2)
        for gg in range(4):
            tiles = list(range(4 * gg, 4 * gg + 4))
            b4s, b4e = SBat(), SBat()
            for i, tt in enumerate(tiles):
                def _(sy, tt=tt, i=i):
                    x2r_ring.acquire(sy, i)
                    sv = spill_val[tt]
                    sy.wait_ge(sv[0].h, sv[1])
                    dma(sy, dxc[i], x2r[:, i * D:(i + 1) * D],
                        x2sp[128 * tt:128 * (tt + 1), :])
                eY(_)

                def _(v, i=i):
                    dxc[i].wait(v)
                    v.bn_stats(out=st6c[:, 12 * i:12 * i + 6], in_=x2r[:, i * D:i * D + 512])
                    v.bn_stats(out=st6c[:, 12 * i + 6:12 * i + 12],
                               in_=x2r[:, i * D + 512:i * D + 1024])
                eV(_)

            def _(v):
                v.drain()
                for i in range(4):
                    v.bn_aggr(out=msc[:, 2 * i:2 * i + 2], in_=st6c[:, 12 * i:12 * i + 12])
                v.drain()
                va = msc.ap().rearrange("p (t two) -> p t two", two=2)
                v.tensor_scalar(out=s1c[:, 0:4], in0=va[:, 0:4, 1], scalar1=LN_EPS,
                                scalar2=None, op0=ALU.add).then_inc(v2s.h, 1); v2s.n += 1
            eV(_)

            vw = v2s.n
            s2v.n += 1

            def _j(s, vw=vw):
                s.wait_ge(v2s.h, vw)
                s.activation(out=s1c[:, 0:4], in_=s1c[:, 0:4], func=ACT.Sqrt).then_inc(s2v.h, 1)
            b4s.add(_j)

            def _(v):
                s2v.wait(v)
                v.reciprocal(out=s1c[:, 0:4], in_=s1c[:, 0:4])
                v.drain()
                va = msc.ap().rearrange("p (t two) -> p t two", two=2)
                v.tensor_tensor(out=s2c[:, 0:4], in0=s1c[:, 0:4], in1=s1c[:, 0:4],
                                op=ALU.mult)
                v.drain()
                v.tensor_tensor(out=s2c[:, 0:4], in0=s2c[:, 0:4], in1=va[:, 0:4, 1],
                                op=ALU.mult)
                v.drain()
                v.tensor_scalar(out=s2c[:, 0:4], in0=s2c[:, 0:4], scalar1=LN_EPS,
                                scalar2=None, op0=ALU.add).then_inc(v2s.h, 1); v2s.n += 1
            eV(_)

            vw = v2s.n
            s2v.n += 1

            def _j(s, vw=vw):
                s.wait_ge(v2s.h, vw)
                s.activation(out=s2c[:, 0:4], in_=s2c[:, 0:4], func=ACT.Sqrt).then_inc(s2v.h, 1)
            b4s.add(_j)

            def _(v):
                s2v.wait(v)
                v.reciprocal(out=s2c[:, 0:4], in_=s2c[:, 0:4])
                v.drain()
                v.tensor_tensor(out=s1c[:, 0:4], in0=s1c[:, 0:4], in1=s2c[:, 0:4],
                                op=ALU.mult)
                va = msc.ap().rearrange("p (t two) -> p t two", two=2)
                v.tensor_scalar(out=b12[:, 0:4], in0=va[:, 0:4, 0],
                                scalar1=-1.0, scalar2=None, op0=ALU.mult)
                v.drain()
                v.tensor_tensor(out=b12[:, 0:4], in0=b12[:, 0:4], in1=s1c[:, 0:4],
                                op=ALU.mult)
                v.drain()
                v.nop().then_inc(v2s.h, 1); v2s.n += 1
            eV(_)
            chain_done = v2s.n

            sq_w = {}
            for i, tt in enumerate(tiles):
                us = tt % 2

                ruw = u2_ring.waits(us)
                s2v.n += 1

                def _j(s, i=i, us=us, cd=chain_done, ruw=ruw):
                    s.wait_ge(v2s.h, cd)
                    for hh, vv in ruw:
                        s.wait_ge(hh, vv)
                    s.activation(out=u2sb[:, us * D:(us + 1) * D],
                                 in_=x2r[:, i * D:(i + 1) * D], func=ACT.Identity,
                                 scale=s1c[:, i:i + 1],
                                 bias=b12[:, i:i + 1]).then_inc(s2v.h, 1)
                b4s.add(_j)
                x2r_ring.release(i, s2v)
                app_val = s2v.n

                def _(v, tt=tt, us=us, av=app_val):
                    v.wait_ge(s2v.h, av)
                    c0 = 1056 * tt
                    v.tensor_tensor(out=xn_sb[:, c0:c0 + D],
                                    in0=u2sb[:, us * D:(us + 1) * D],
                                    in1=gbc_sb[:], op=ALU.mult)
                    v.memset(xn_sb[:, c0 + D:c0 + 1056], 1.0)
                    v.drain()
                    v.nop().then_inc(v2g.h, 1); v2g.n += 1
                eV(_)
                u2_ring.release(us, v2g)

                def _(g, tt=tt):
                    v2g.wait(g)
                    c0 = 1056 * tt
                    g.tensor_tensor(out=xn_sb[:, c0:c0 + D], in0=xn_sb[:, c0:c0 + D],
                                    in1=bbc_sb[:], op=ALU.add).then_inc(v2t.h, 1)
                    v2t.n += 1
                eG(_)
                sq_w[i] = v2t.n

            for i, tt in enumerate(tiles):
                pw = sacc.n
                sacc.n += 1

                def _j(s, tt=tt, vw=sq_w[i], pw=pw):
                    s.wait_ge(v2t.h, vw)
                    if pw:
                        s.wait_ge(sacc.h, pw)
                    c0 = 1056 * tt
                    s.activation(out=sq_sb[:, :], in_=xn_sb[:, c0:c0 + D],
                                 func=ACT.Square,
                                 accum_out=fac[:, tt:tt + 1]).then_inc(sacc.h, 1)
                b4s.add(_j)

            sw = sacc.n
            s2v.n += 1

            def _j(s, gg=gg, sw=sw):
                s.wait_ge(sacc.h, sw)
                s.activation(out=fac[:, 4 * gg:4 * gg + 4], in_=fac[:, 4 * gg:4 * gg + 4],
                             func=ACT.Sqrt).then_inc(s2v.h, 1)
            b4s.add(_j)
            b4s.flush(eS)

            def _(v, gg=gg):
                s2v.wait(v)
                v.tensor_scalar(out=fac[:, 4 * gg:4 * gg + 4], in0=fac[:, 4 * gg:4 * gg + 4],
                                scalar1=1e-12, scalar2=None, op0=ALU.max)
                v.drain()
                v.reciprocal(out=fac[:, 4 * gg:4 * gg + 4], in_=fac[:, 4 * gg:4 * gg + 4])
                v.drain()
                v.tensor_scalar(out=fac[:, 4 * gg:4 * gg + 4], in0=fac[:, 4 * gg:4 * gg + 4],
                                scalar1=scbc_sb[:, 0:1], scalar2=None,
                                op0=ALU.mult).then_inc(v2s.h, 1); v2s.n += 1
            eV(_)
            fac_done = v2s.n

            for i, tt in enumerate(tiles):
                xslot = tt % 2
                for half in range(2):
                    slot = (2 * tt + half) % 2

                    def _(t_, tt=tt, half=half, slot=slot):
                        v2t.wait(t_)
                        tb_ring.acquire(t_, slot)
                        for cc in range(4):
                            c = 4 * half + cc
                            mi = t_.transpose(ps_tb[slot][:, 128 * cc:128 * (cc + 1)],
                                              xn_sb[:, 1056 * tt + 128 * c:1056 * tt + 128 * (c + 1)],
                                              ident_sb[:])
                        mi.then_inc(t2v.h, 1); t2v.n += 1
                    eT(_)

                    def _(v, half=half, slot=slot, xslot=xslot):
                        t2v.wait(v)
                        if half == 0:
                            xnT_ring.acquire(v, xslot)
                        v.tensor_copy(
                            out=xnT_sb[:, 1024 * xslot + 512 * half:1024 * xslot + 512 * (half + 1)],
                            in_=ps_tb[slot][:, :]).then_inc(v2t.h, 1)
                        v2t.n += 1
                    eV(_)
                    tb_ring.release(slot, v2t)

                def _(t_, tt=tt, xslot=xslot):
                    v2t.wait(t_)
                    if tt == 0:
                        t_.wait_ge(v2t.h, muU_done)
                    slot = tt % 2
                    lg_ring.acquire(t_, slot)
                    for c in range(DCH):
                        mi = t_.matmul(lg_ps[slot][:, 0:ES],
                                       xnT_sb[:, 1024 * xslot + 128 * c:1024 * xslot + 128 * (c + 1)],
                                       muU_sb[:, ES * c:ES * (c + 1)],
                                       start=(c == 0), stop=(c == DCH - 1))
                    mi.then_inc(t2s.h, 1); t2s.n += 1
                eT(_)
                xnT_ring.release(xslot, t2s)

                tw = t2s.n
                s2v.n += 1

                def _j(s, tt=tt, fd=fac_done, tw=tw):
                    s.wait_ge(t2s.h, tw)
                    s.wait_ge(v2s.h, fd)
                    slot = tt % 2
                    s.activation(out=E_sb[:, ES * tt:ES * (tt + 1)],
                                 in_=lg_ps[slot][:, 0:ES],
                                 func=ACT.Exp, scale=fac[:, tt:tt + 1],
                                 accum_out=rs_sb[:, tt:tt + 1]).then_inc(s2v.h, 1)
                b4e.add(_j)
                lg_ring.release(tt % 2, s2v)
            b4e.flush(eS)

        def _(v):
            s2v.wait(v)
            v.reciprocal(out=rs_sb[:], in_=rs_sb[:])
            v.drain().then_inc(v2s.h, 1); v2s.n += 1
        eV(_)

        b4i = SBat()
        rw0 = v2s.n
        for tt in range(16):
            s2t.n += 1

            def _j(s, tt=tt, rw0=rw0):
                if tt == 0:
                    s.wait_ge(v2s.h, rw0)
                s.activation(out=comb_sb[:, ES * tt:ES * (tt + 1)],
                             in_=E_sb[:, ES * tt:ES * (tt + 1)], func=ACT.Identity,
                             scale=rs_sb[:, tt:tt + 1]).then_inc(s2t.h, 1)
            b4i.add(_j)
        b4i.flush(eS)
        comb_done = s2t.n

        p3s.close()
        u_ps = [p3.enter_context(nc.psum_tensor(f"u_ps{i}", [128, 512], F32)) for i in range(4)]
        ue_ps = p3.enter_context(nc.psum_tensor("ue_ps", [128, 512], F32))

        def _(t_):
            t_.wait_ge(s2t.h, comb_done)
            for half in range(2):
                for tt in range(16):
                    lhs = E_sb[:, ES * tt + 128 * half: ES * tt + 128 * (half + 1)]
                    stf = (tt == 0)
                    t_.matmul(u_ps[2 * half][:, :], lhs,
                              xn_sb[:, 1056 * tt:1056 * tt + 512],
                              start=stf, stop=(tt == 15))
                    t_.matmul(u_ps[2 * half + 1][:, :], lhs,
                              xn_sb[:, 1056 * tt + 512:1056 * tt + 1024],
                              start=stf, stop=(tt == 15))
                    mi = t_.matmul(ue_ps[:, half:half + 1], lhs,
                                   xn_sb[:, 1056 * tt + 1024:1056 * tt + 1025],
                                   start=stf, stop=(tt == 15))
            mi.then_inc(t2v.h, 1); t2v.n += 1
        eT(_)

        def _(v):
            t2v.wait(v)
            v.tensor_copy(out=ccin_sb[:, 0:512], in_=u_ps[0][:, :])
            v.tensor_copy(out=ccin_sb[:, 512:1024], in_=u_ps[1][:, :])
            v.tensor_copy(out=ccin_sb[:, 1024:1025], in_=ue_ps[:, 0:1])
            v.tensor_copy(out=ccin_sb[:, 1025:1537], in_=u_ps[2][:, :])
            v.tensor_copy(out=ccin_sb[:, 1537:2049], in_=u_ps[3][:, :])
            v.tensor_copy(out=ccin_sb[:, 2049:2050], in_=ue_ps[:, 1:2]).then_inc(v2t.h, 1); v2t.n += 1
        eV(_)

        def _(g):
            v2t.wait(g)
            dma(g, ccd, ccin.ap(), ccin_sb[:])
            g.wait_ge(ccd.h, ccd.n)
            g.collective_compute(
                "AllReduce", ALU.add,
                replica_groups=[[0, 1], [2, 3], [4, 5], [6, 7]],
                ins=[ccin.ap().opt()], outs=[ccout.ap().opt()],
            ).then_inc(ccS.h, 1)
            ccS.n += 1
            g.wait_ge(ccS.h, ccS.n)
            dma(g, ccd, usb[:], ccout.ap())
        eG(_)

        def _(v):
            ccd.wait(v)
            v.reciprocal(out=usb[:, 1024:1025], in_=usb[:, 1024:1025])
            v.reciprocal(out=usb[:, 2049:2050], in_=usb[:, 2049:2050])
            v.drain()
            v.tensor_scalar(out=slin[:, 0:D], in0=usb[:, 0:1024],
                            scalar1=usb[:, 1024:1025], scalar2=None, op0=ALU.mult)
            v.tensor_scalar(out=slin[:, D:2 * D], in0=usb[:, 1025:2049],
                            scalar1=usb[:, 2049:2050], scalar2=None, op0=ALU.mult).then_inc(v2t.h, 1); v2t.n += 1
        eV(_)

        if any(k in dbg_out for k in ("xn", "ee", "comb", "usb", "slin")):
            def _(g):
                g.wait_ge(v2t.h, v2t.n)
                g.wait_ge(s2t.h, comb_done)
                ddb.wait(g)
                for nm, sbuf in (("xn", xn_sb), ("ee", E_sb), ("comb", comb_sb),
                                 ("usb", usb), ("slin", slin)):
                    if nm in dbg_out:
                        dma(g, ddb, dbg_out[nm], sbuf[:])
                g.wait_ge(ddb.h, ddb.n)
            eG(_)

    p3.close()
    if stage <= 5:
        st.close()
        return nc, dbg_out

    slout = st.enter_context(nc.sbuf_tensor("slout", [128, 2 * D], BF16))
    p5 = ExitStack()
    w2_sb = p5.enter_context(nc.sbuf_tensor("w2_sb", [128, E * 2 * D], BF16))
    slinT = p5.enter_context(nc.sbuf_tensor("slinT", [128, DCH * ES], BF16))
    h_sb = p5.enter_context(nc.sbuf_tensor("h_sb", [16, 2 * HID], BF16))
    hT_sb = p5.enter_context(nc.sbuf_tensor("hT_sb", [128, 2 * 32], BF16))
    sostg = p5.enter_context(nc.sbuf_tensor("sostg", [16, 2 * D], BF16))
    b1_sb = p5.enter_context(nc.sbuf_tensor("b1_sb", [1, E * HID], BF16))
    b2_sb = p5.enter_context(nc.sbuf_tensor("b2_sb", [1, E * D], BF16))

    with nc.Block() as blk:
        fence(blk)
        eT, eV, eS, eG, eY = blk.tensor, blk.vector, blk.scalar, blk.gpsimd, blk.sync

        w2v = {}

        def _(g):
            g.wait_ge(dcg.h, dcg.n)
            dma(g, dcg, b1_sb[:], b1r)
            dma(g, dcg, b2_sb[:], b2r)
            for e in range(E):
                sem = dw2[e % 4]
                g.wait_ge(sem.h, sem.n)
                dma(g, sem, w2_sb[:, 2048 * e:2048 * (e + 1)],
                    w2h[:, 2048 * e:2048 * (e + 1)])
                w2v[e] = (sem, sem.n)
        eG(_)
        w2_done = dcg.n

        p5s = ExitStack()
        ps_t5 = [p5s.enter_context(nc.psum_tensor(f"ps_t5{i}", [128, 512], BF16)) for i in range(2)]

        t5_ring = Ring(2)
        for c in range(DCH):
            slot = c % 2

            def _(t_, c=c, slot=slot):
                if c == 0:
                    t_.wait_ge(v2t.h, v2t.n)
                t5_ring.acquire(t_, slot)
                for half in range(2):
                    mi = t_.transpose(ps_t5[slot][:, 128 * half:128 * (half + 1)],
                                      slin[:, D * half + 128 * c: D * half + 128 * (c + 1)],
                                      ident_sb[:])
                mi.then_inc(t2v.h, 1); t2v.n += 1
            eT(_)

            def _(v, c=c, slot=slot):
                t2v.wait(v)
                v.tensor_copy(out=slinT[:, ES * c:ES * (c + 1)],
                              in_=ps_t5[slot][:, 0:ES]).then_inc(v2t.h, 1)
                v2t.n += 1
            eV(_)
            t5_ring.release(slot, v2t)
        slinT_done = v2t.n

        p5s.close()
        h_ps = [p5.enter_context(nc.psum_tensor(f"h_ps{i}", [16, HID], F32)) for i in range(2)]
        hT_ps = [p5.enter_context(nc.psum_tensor(f"hT_ps{i}", [128, 32], BF16)) for i in range(2)]
        so_ps = [p5.enter_context(nc.psum_tensor(f"so_ps{i}", [16, D], F32)) for i in range(2)]

        hps_ring, hsb_ring, hTps_ring, hT_ring = Ring(2), Ring(2), Ring(2), Ring(2)
        sops_ring, stg_ring = Ring(2), Ring(2)
        p5bat = SBat()
        for e in range(E):
            es = e % 2

            def _(t_, e=e, es=es, w2d=w2_done):
                if e == 0:
                    t_.wait_ge(v2t.h, slinT_done)
                    t_.wait_ge(dcg.h, w2d)
                hps_ring.acquire(t_, es)
                t_.matmul(h_ps[es][:, :], onesr_sb[0:1, 0:16],
                          b1_sb[0:1, HID * e:HID * (e + 1)], start=True, stop=False)
                for c in range(DCH):
                    mi = t_.matmul(h_ps[es][:, :],
                                   slinT[:, ES * c + 16 * e: ES * c + 16 * (e + 1)],
                                   w1_sb[:, 2048 * e + HID * c: 2048 * e + HID * (c + 1)],
                                   start=False, stop=(c == DCH - 1))
                mi.then_inc(t2s.h, 1); t2s.n += 1
            eT(_)

            tw = t2s.n
            rw = hsb_ring.waits(es)
            s2t.n += 1

            def _j(s, es=es, tw=tw, rw=rw):
                s.wait_ge(t2s.h, tw)
                for hh, vv in rw:
                    s.wait_ge(hh, vv)
                s.activation(out=h_sb[:, HID * es:HID * (es + 1)], in_=h_ps[es][:, :],
                             func=ACT.Gelu).then_inc(s2t.h, 1)
            p5bat.add(_j)
            hps_ring.release(es, s2t)

            def _(t_, es=es):
                s2t.wait(t_)
                hTps_ring.acquire(t_, es)
                for c in range(2):
                    mi = t_.transpose(hT_ps[es][:, 16 * c:16 * (c + 1)],
                                      h_sb[0:16, HID * es + 128 * c: HID * es + 128 * (c + 1)],
                                      ident_sb[0:16, 0:16])
                mi.then_inc(t2v.h, 1); t2v.n += 1
            eT(_)

            def _(v, es=es):
                t2v.wait(v)
                hT_ring.acquire(v, es)
                v.tensor_copy(out=hT_sb[:, 32 * es:32 * (es + 1)],
                              in_=hT_ps[es][:, :]).then_inc(v2t.h, 1)
                v2t.n += 1
            eV(_)
            hsb_ring.release(es, v2t)
            hTps_ring.release(es, v2t)

            def _(t_, e=e, es=es, wv=w2v[e]):
                v2t.wait(t_)
                t_.wait_ge(wv[0].h, wv[1])
                sops_ring.acquire(t_, es)
                for dh in range(2):
                    t_.matmul(so_ps[es][:, 512 * dh:512 * (dh + 1)], onesr_sb[0:1, 0:16],
                              b2_sb[0:1, D * e + 512 * dh: D * e + 512 * (dh + 1)],
                              start=True, stop=False)
                    for c in range(2):
                        mi = t_.matmul(so_ps[es][:, 512 * dh:512 * (dh + 1)],
                                       hT_sb[:, 32 * es + 16 * c: 32 * es + 16 * (c + 1)],
                                       w2_sb[:, 2048 * e + 1024 * c + 512 * dh: 2048 * e + 1024 * c + 512 * (dh + 1)],
                                       start=False, stop=(c == 1))
                mi.then_inc(t2v.h, 1); t2v.n += 1
            eT(_)
            hT_ring.release(es, t2v)

            def _(v, es=es):
                t2v.wait(v)
                stg_ring.acquire(v, es)
                v.tensor_copy(out=sostg[:, D * es:D * (es + 1)],
                              in_=so_ps[es][:, :]).then_inc(v2y.h, 1)
                v2y.n += 1
            eV(_)
            sops_ring.release(es, v2y)

            def _(sy, e=e, es=es):
                v2y.wait(sy)
                half, row = e // 8, 16 * (e % 8)
                dma(sy, dsg[es], slout[row:row + 16, D * half:D * (half + 1)],
                    sostg[:, D * es:D * (es + 1)])
            eY(_)
            stg_ring.release(es, dsg[es])
        p5bat.flush(eS)

        if "slout" in dbg_out:
            def _(g):
                g.wait_ge(dsg[0].h, dsg[0].n)
                g.wait_ge(dsg[1].h, dsg[1].n)
                ddb.wait(g)
                dma(g, ddb, dbg_out["slout"], slout[:])
                g.wait_ge(ddb.h, ddb.n)
            eG(_)

    p5.close()

    p6 = ExitStack()
    combT = p6.enter_context(nc.sbuf_tensor("combT", [128, 2 * TQ], BF16))
    fln = p6.enter_context(nc.sbuf_tensor("fln", [128, 2 * D], F32))
    x2rr = p6.enter_context(nc.sbuf_tensor("x2rr", [128, 2 * D], F32))
    osb = p6.enter_context(nc.sbuf_tensor("osb", [128, 2 * D], F32))
    st6e = p6.enter_context(nc.sbuf_tensor("st6e", [128, 24], F32))
    mse = p6.enter_context(nc.sbuf_tensor("mse", [128, 4], F32))
    scl6 = p6.enter_context(nc.sbuf_tensor("scl6", [128, 2], F32))
    bcl6 = p6.enter_context(nc.sbuf_tensor("bcl6", [128, 2], F32))

    with nc.Block() as blk:
        fence(blk)
        eT, eV, eS, eG, eY = blk.tensor, blk.vector, blk.scalar, blk.gpsimd, blk.sync
        ps_t6 = [p6.enter_context(nc.psum_tensor(f"ps_t6{i}", [128, 512], BF16)) for i in range(2)]
        f_ps = [p6.enter_context(nc.psum_tensor(f"f_ps{i}", [128, D], F32)) for i in range(2)]

        t6_ring = Ring(2)
        for tt in range(16):
            slot = tt % 2

            def _(t_, tt=tt, slot=slot):
                t6_ring.acquire(t_, slot)
                for half in range(2):
                    mi = t_.transpose(ps_t6[slot][:, 128 * half:128 * (half + 1)],
                                      comb_sb[:, ES * tt + 128 * half: ES * tt + 128 * (half + 1)],
                                      ident_sb[:])
                mi.then_inc(t2v.h, 1); t2v.n += 1
            eT(_)

            def _(v, tt=tt, slot=slot):
                t2v.wait(v)
                for half in range(2):
                    mi = v.tensor_copy(out=combT[:, TQ * half + 128 * tt: TQ * half + 128 * (tt + 1)],
                                       in_=ps_t6[slot][:, 128 * half:128 * (half + 1)])
                mi.then_inc(v2t.h, 1); v2t.n += 1
            eV(_)
            t6_ring.release(slot, v2t)
        combT_done = v2t.n

        fps_ring, fln_ring = Ring(2), Ring(2)
        x2rr_ring, o_ring = Ring(2), Ring(2)
        p6bat = SBat()
        for tt in range(16):
            slot = tt % 2

            def _(sy, tt=tt, slot=slot):
                if tt == 0:
                    sy.wait_ge(dsp[0].h, dsp[0].n)
                    sy.wait_ge(dsp[1].h, dsp[1].n)
                x2rr_ring.acquire(sy, slot)
                dma(sy, dx6[slot], x2rr[:, slot * D:(slot + 1) * D],
                    x2sp[128 * tt:128 * (tt + 1), :])
            eY(_)

            def _(t_, tt=tt, slot=slot):
                if tt == 0:
                    t_.wait_ge(v2t.h, combT_done)
                    t_.wait_ge(dsg[0].h, dsg[0].n)
                    t_.wait_ge(dsg[1].h, dsg[1].n)
                fps_ring.acquire(t_, slot)
                for dh in range(2):
                    for half in range(2):
                        mi = t_.matmul(f_ps[slot][:, 512 * dh:512 * (dh + 1)],
                                       combT[:, TQ * half + 128 * tt: TQ * half + 128 * (tt + 1)],
                                       slout[:, D * half + 512 * dh: D * half + 512 * (dh + 1)],
                                       start=(half == 0), stop=(half == 1))
                mi.then_inc(t2s.h, 1); t2s.n += 1
            eT(_)

            def _(v, slot=slot):
                t2s.wait(v)
                v.bn_stats(out=st6e[:, 12 * slot:12 * slot + 6],
                           in_=f_ps[slot][:, 0:512])
                v.bn_stats(out=st6e[:, 12 * slot + 6:12 * slot + 12],
                           in_=f_ps[slot][:, 512:1024])
                v.drain()
                v.bn_aggr(out=mse[:, 2 * slot:2 * slot + 2],
                          in_=st6e[:, 12 * slot:12 * slot + 12])
                v.drain()
                v.tensor_scalar(out=scl6[:, slot:slot + 1],
                                in0=mse[:, 2 * slot + 1:2 * slot + 2],
                                scalar1=LN_EPS, scalar2=None, op0=ALU.add).then_inc(v2s.h, 1); v2s.n += 1
            eV(_)

            vw = v2s.n
            s2v.n += 1

            def _j(s, slot=slot, vw=vw):
                s.wait_ge(v2s.h, vw)
                s.activation(out=scl6[:, slot:slot + 1], in_=scl6[:, slot:slot + 1],
                             func=ACT.Sqrt).then_inc(s2v.h, 1)
            p6bat.add(_j)

            def _(v, slot=slot):
                s2v.wait(v)
                v.reciprocal(out=scl6[:, slot:slot + 1], in_=scl6[:, slot:slot + 1])
                v.tensor_scalar(out=bcl6[:, slot:slot + 1],
                                in0=mse[:, 2 * slot:2 * slot + 1],
                                scalar1=-1.0, scalar2=None, op0=ALU.mult)
                v.drain()
                fln_ring.acquire(v, slot)
                v.tensor_scalar(out=fln[:, slot * D:(slot + 1) * D],
                                in0=f_ps[slot][:, :],
                                scalar1=bcl6[:, slot:slot + 1],
                                scalar2=scl6[:, slot:slot + 1],
                                op0=ALU.add, op1=ALU.mult).then_inc(v2g.h, 1)
                v2g.n += 1
            eV(_)
            fps_ring.release(slot, v2g)

            def _(g, slot=slot):
                v2g.wait(g)
                dx6[slot].wait(g)
                o_ring.acquire(g, slot)
                g.tensor_tensor(out=osb[:, slot * D:(slot + 1) * D],
                                in0=fln[:, slot * D:(slot + 1) * D],
                                in1=x2rr[:, slot * D:(slot + 1) * D],
                                op=ALU.add).then_inc(g2t.h, 1)
                g2t.n += 1
            eG(_)
            fln_ring.release(slot, g2t)
            x2rr_ring.release(slot, g2t)

            def _(sy, tt=tt, slot=slot):
                g2t.wait(sy)
                dma(sy, dso[slot], out[128 * tt:128 * (tt + 1), :],
                    osb[:, slot * D:(slot + 1) * D])
            eY(_)
            o_ring.release(slot, dso[slot])
        p6bat.flush(eS)

        def _(sy):
            sy.wait_ge(dso[0].h, dso[0].n)
            sy.wait_ge(dso[1].h, dso[1].n)
            sy.wait_ge(ddb.h, ddb.n)
        eY(_)

    p6.close()
    st.close()
    return nc, dbg_out


def make_tables(h):
    g0 = 2048 * h - 128
    pos = (g0 + np.arange(TKV)).astype(np.float64)
    invf = 1.0 / (10000.0 ** (np.arange(0, HD, 2, dtype=np.float64) / HD))
    hdmod = np.arange(128) % 64
    fi = hdmod % 32
    theta = pos[None, :] * invf[fi][:, None]
    cosr = np.cos(theta).astype(np.float32)
    sign = np.where(hdmod < 32, -1.0, 1.0)
    sinr = (np.sin(theta) * sign[:, None]).astype(np.float32)
    return cosr, sinr


def make_mask(h):
    m = np.zeros((NT, 128, 384), np.float32)
    for c in range(NT):
        q0 = qws(c)
        tk = 128 * c + np.arange(128)
        tq = q0 + np.arange(384)
        d = tq[None, :] - tk[:, None]
        valid = (d >= -128) & (d <= 127)
        posk = (2048 * h - 128) + tk
        valid &= ((posk >= 0) & (posk < 4096))[:, None]
        valid &= ((tq >= 128) & (tq < 2176))[None, :]
        m[c][valid] = 1.0
    return np.ascontiguousarray(m.transpose(1, 0, 2).reshape(128, NT * 384))


def chunk_rows(a):
    R, C = a.shape
    n = R // 128
    return np.ascontiguousarray(a.reshape(n, 128, C).transpose(1, 0, 2).reshape(128, n * C))


def make_inputs(inputs):
    x = np.asarray(inputs["L0_attn_x"], np.float32)
    w_qkv = np.asarray(inputs["L0_attn_w_qkv"], np.float32)
    w_out = np.asarray(inputs["L0_attn_w_out"], np.float32)
    b_out = np.asarray(inputs["L0_attn_b_out"], np.float32)
    gam = np.asarray(inputs["L0_ffn_ln_gamma"], np.float32)
    bet = np.asarray(inputs["L0_ffn_ln_beta"], np.float32)
    mu = np.asarray(inputs["L0_ffn_mu"], np.float32)
    scale = np.asarray(inputs["L0_ffn_scale"], np.float32)
    w1 = np.asarray(inputs["L0_ffn_w1"], np.float32)
    w2 = np.asarray(inputs["L0_ffn_w2"], np.float32)
    b1 = np.asarray(inputs["L0_ffn_b1"], np.float32)
    b2 = np.asarray(inputs["L0_ffn_b2"], np.float32)

    wq_l = chunk_rows(np.ascontiguousarray(w_qkv.T))
    wo_l = chunk_rows(np.ascontiguousarray(w_out.T))
    muh_l = chunk_rows(np.ascontiguousarray(mu.reshape(D, ES)))
    w1_l = chunk_rows(w1.reshape(E * D, HID))
    w2_l = chunk_rows(w2.reshape(E * 2 * 128, D))
    b1_l = np.ascontiguousarray(b1.reshape(1, E * HID))
    b2_l = np.ascontiguousarray(b2.reshape(1, E * D))
    bor_l = np.ascontiguousarray(b_out.reshape(1, D))
    gbc_l = np.ascontiguousarray(np.broadcast_to(gam[None, :], (128, D)))
    bbc_l = np.ascontiguousarray(np.broadcast_to(bet[None, :], (128, D)))
    scbc_l = np.ascontiguousarray(np.broadcast_to(scale.reshape(1, 1), (128, 1)))
    identb = np.eye(128, dtype=ml_dtypes.bfloat16)
    pm = np.zeros((128, 128), dtype=ml_dtypes.bfloat16)
    for mcol in range(128):
        src = mcol + 32 if (mcol % 64) < 32 else mcol - 32
        pm[src, mcol] = 1
    onesc_l = np.ones((128, 1), dtype=ml_dtypes.bfloat16)
    onesr_l = np.ones((1, 128), dtype=ml_dtypes.bfloat16)

    in_maps = []
    for c in range(8):
        b, h = c // 2, c % 2
        g0 = 2048 * h - 128
        xpad = np.zeros((TKV, D), np.float32)
        lo = max(0, -g0)
        hi = TKV - max(0, (g0 + TKV) - 4096)
        xpad[lo:hi] = x[b, g0 + lo: g0 + hi]
        cosr_l, sinr_l = make_tables(h)
        in_maps.append({
            "xp": xpad, "wq": wq_l, "wo": wo_l, "muh": muh_l,
            "w1h": w1_l, "w2h": w2_l, "b1r": b1_l, "b2r": b2_l, "bor": bor_l,
            "gbc": gbc_l, "bbc": bbc_l, "scbc": scbc_l,
            "cosr": cosr_l, "sinr": sinr_l, "maskh": make_mask(h),
            "identb": identb, "permb": pm, "onesc": onesc_l, "onesr": onesr_l,
            "onesrf": np.ones((1, 128), np.float32),
        })
    return in_maps


_NC_CACHE = {}


def kernel(**inputs):
    key = (STAGE, tuple(DBG))
    if key not in _NC_CACHE:
        _NC_CACHE[key] = build_program()
    nc, _ = _NC_CACHE[key]
    in_maps = make_inputs(inputs)
    res = run_bass_kernel_spmd(nc, in_maps, core_ids=list(range(8)))
    outp = np.empty((4, 4096, D), np.float32)
    for c in range(8):
        b, h = c // 2, c % 2
        outp[b, 2048 * h:2048 * (h + 1), :] = res.results[c]["out"]
    return outp


# version 14 (final)
# speedup vs baseline: 1.0419x; 1.0044x over previous
import numpy as np
import ml_dtypes
from contextlib import ExitStack

import concourse.bass as bass
import concourse.mybir as mybir
from concourse.bass_utils import run_bass_kernel_spmd

F32 = mybir.dt.float32
BF16 = mybir.dt.bfloat16
ALU = mybir.AluOpType
ACT = mybir.ActivationFunctionType

D = 1024
H = 16
HD = 64
DCH = 8
TKV = 2304
NT = 18
TQ = 2048
E, NS, HID = 16, 16, 256
ES = 256
LN_EPS = 1e-5
MASKVAL = -10000.0

STAGE = 99
DBG = ()


def qws(c):
    return min(max(0, 128 * c - 128), TKV - 384)


def tset(c):
    return [t for t in (c - 1, c, c + 1) if 1 <= t <= 16]


class Sem:
    def __init__(self, stack, nc, name):
        self.h = stack.enter_context(nc.semaphore(name))
        self.n = 0

    def wait(self, eng, n=None):
        eng.wait_ge(self.h, self.n if n is None else n)


class Ring:
    def __init__(self, nslots):
        self.rel = [[] for _ in range(nslots)]

    def acquire(self, eng, slot):
        for sem, val in self.rel[slot]:
            eng.wait_ge(sem.h, val)

    def waits(self, slot):
        return tuple((sem.h, val) for sem, val in self.rel[slot])

    def release(self, slot, sem):
        self.rel[slot] = [(sem, sem.n)]

    def release_add(self, slot, sem):
        self.rel[slot].append((sem, sem.n))


class SBat:

    def __init__(self):
        self.jobs = []

    def add(self, fn):
        self.jobs.append(fn)

    def flush(self, eS):
        if not self.jobs:
            return
        jobs = tuple(self.jobs)
        self.jobs = []

        def _(s, jobs=jobs):
            for fn in jobs:
                fn(s)
        eS(_)


def build_program(stage=STAGE, dbg=DBG):
    nc = bass.Bass("TRN2", target_bir_lowering=False)
    st = ExitStack()

    def din(name, shape, dt=F32):
        return nc.dram_tensor(name, shape, dt, kind="ExternalInput").ap()

    xp = din("xp", [TKV, D])
    wq = din("wq", [128, DCH * 3 * D])
    wo = din("wo", [128, DCH * D])
    muh = din("muh", [128, DCH * ES])
    w1h = din("w1h", [128, E * DCH * HID])
    w2h = din("w2h", [128, E * 2 * D])
    b1r = din("b1r", [1, E * HID])
    b2r = din("b2r", [1, E * D])
    bor = din("bor", [1, D])
    gbc = din("gbc", [128, D])
    bbc = din("bbc", [128, D])
    scbc = din("scbc", [128, 1])
    cosr = din("cosr", [128, TKV])
    sinr = din("sinr", [128, TKV])
    maskh = din("maskh", [128, NT * 384])
    identb = din("identb", [128, 128], BF16)
    permb = din("permb", [128, 128], BF16)
    onesc = din("onesc", [128, 1], BF16)
    onesr = din("onesr", [1, 128], BF16)
    onesrf = din("onesrf", [1, 128])

    out = nc.dram_tensor("out", [TQ, D], F32, kind="ExternalOutput").ap()
    x2sp = nc.dram_tensor("x2sp", [TQ, D], F32).ap()
    ccin = nc.dram_tensor("ccin", [128, 2050], F32)
    ccout = nc.dram_tensor("ccout", [128, 2050], F32)

    dbg_shapes = {
        "xln": [TKV, D], "qT": [128, DCH * TKV], "kT": [128, DCH * TKV],
        "vtm": [128, NT * D], "attn": [128, DCH * TQ], "x2": [TQ, D],
        "xn": [128, 16 * 1056], "ee": [128, 16 * ES], "comb": [128, 16 * ES],
        "usb": [128, 2050], "slin": [128, 2 * D], "slout": [128, 2 * D],
    }
    dbg_out = {k: nc.dram_tensor("dbg_" + k, dbg_shapes[k], F32, kind="ExternalOutput").ap()
               for k in dbg}

    def sb(name, shape, dt):
        return st.enter_context(nc.sbuf_tensor(name, shape, dt))

    ident_sb = sb("ident_sb", [128, 128], BF16)
    perm_sb = sb("perm_sb", [128, 128], BF16)
    onesc_sb = sb("onesc_sb", [128, 1], BF16)
    onesr_sb = sb("onesr_sb", [1, 128], BF16)
    onesrf_sb = sb("onesrf_sb", [1, 128], F32)
    pA = ExitStack()
    qT = pA.enter_context(nc.sbuf_tensor("qT", [128, DCH * TKV], BF16))
    kT = pA.enter_context(nc.sbuf_tensor("kT", [128, DCH * TKV], BF16))
    vtm = pA.enter_context(nc.sbuf_tensor("vtm", [128, NT * 1040], BF16))

    names = ["t2s", "t2v", "s2t", "v2t", "s2v", "v2s", "v2y", "cc", "v2g", "g2t", "sacc",
             "dcs", "dcg", "ccd", "ddb",
             "dlx0", "dlx1", "dxr0", "dxr1", "dsp0", "dsp1",
             "dxc0", "dxc1", "dxc2", "dxc3", "dsg0", "dsg1",
             "dw20", "dw21", "dw22", "dw23",
             "dx60", "dx61", "dso0", "dso1"]
    sm = {n: Sem(st, nc, "sm_" + n) for n in names}
    t2s, t2v, s2t, v2t = sm["t2s"], sm["t2v"], sm["s2t"], sm["v2t"]
    s2v, v2s, v2y, ccS = sm["s2v"], sm["v2s"], sm["v2y"], sm["cc"]
    dcs, dcg, ccd, ddb = sm["dcs"], sm["dcg"], sm["ccd"], sm["ddb"]
    v2g, g2t, sacc = sm["v2g"], sm["g2t"], sm["sacc"]
    dlx = [sm["dlx0"], sm["dlx1"]]
    dxr = [sm["dxr0"], sm["dxr1"]]
    dsp = [sm["dsp0"], sm["dsp1"]]
    dxc = [sm["dxc0"], sm["dxc1"], sm["dxc2"], sm["dxc3"]]
    dw2 = [sm["dw20"], sm["dw21"], sm["dw22"], sm["dw23"]]
    dsg = [sm["dsg0"], sm["dsg1"]]
    dx6 = [sm["dx60"], sm["dx61"]]
    dso = [sm["dso0"], sm["dso1"]]

    def dma(eng, sem, out_ap, in_ap):
        eng.dma_start(out=out_ap, in_=in_ap).then_inc(sem.h, 16)
        sem.n += 16

    def fence(blk):
        vals = [(s.h, s.n) for s in sm.values() if s.n > 0]

        def _f(eng):
            for hh, nn in vals:
                eng.wait_ge(hh, nn)
        blk.tensor(_f)
        blk.vector(_f)
        blk.scalar(_f)
        blk.gpsimd(_f)
        blk.sync(_f)

    p1 = ExitStack()
    wq_sb = p1.enter_context(nc.sbuf_tensor("wq_sb", [128, DCH * 3 * D], BF16))
    cos_sb = p1.enter_context(nc.sbuf_tensor("cos_sb", [128, TKV], BF16))
    sin_sb = p1.enter_context(nc.sbuf_tensor("sin_sb", [128, TKV], BF16))
    xtile = p1.enter_context(nc.sbuf_tensor("xtile", [128, 2 * D], F32))
    xln = p1.enter_context(nc.sbuf_tensor("xln", [128, 4 * D], BF16))
    xlnT = p1.enter_context(nc.sbuf_tensor("xlnT", [128, DCH * 512], BF16))
    stat6 = p1.enter_context(nc.sbuf_tensor("stat6", [128, 48], F32))
    mstat = p1.enter_context(nc.sbuf_tensor("mstat", [128, 8], F32))
    scol = p1.enter_context(nc.sbuf_tensor("scol", [128, 4], F32))
    bcol = p1.enter_context(nc.sbuf_tensor("bcol", [128, 4], F32))
    qraw = p1.enter_context(nc.sbuf_tensor("qraw", [128, 2 * 512], BF16))
    qpm = p1.enter_context(nc.sbuf_tensor("qpm", [128, 2 * 512], BF16))

    xring = Ring(2)
    with nc.Block() as blk:
        eT, eV, eS, eG, eY = blk.tensor, blk.vector, blk.scalar, blk.gpsimd, blk.sync

        def _(g):
            dma(g, dcg, wq_sb[:], wq)
            dma(g, dcg, cos_sb[:], cosr)
            dma(g, dcg, sin_sb[:], sinr)
        eG(_)
        const_dcg = dcg.n

        def _(sy):
            dma(sy, dcs, ident_sb[:], identb)
            dma(sy, dcs, perm_sb[:], permb)
            dma(sy, dcs, onesc_sb[:], onesc)
            dma(sy, dcs, onesr_sb[:], onesr)
            dma(sy, dcs, onesrf_sb[:], onesrf)
        eY(_)
        const_dcs = dcs.n

        ps_tp = [p1.enter_context(nc.psum_tensor(f"ps_tp{i}", [128, 512], BF16)) for i in range(2)]
        ps_qk = [p1.enter_context(nc.psum_tensor(f"ps_qk{i}", [128, 512], F32)) for i in range(3)]
        ps_pm = [p1.enter_context(nc.psum_tensor(f"ps_pm{i}", [128, 512], F32)) for i in range(2)]

        tp_ring, qk_ring, pm_ring = Ring(2), Ring(3), Ring(2)
        qraw_ring, qpm_ring = Ring(2), Ring(2)
        xln_guard = [0]

        STS = [(0, 4), (4, 4), (8, 4), (12, 4), (16, 2)]
        for si, (t0, ntl) in enumerate(STS):
            W = 128 * ntl
            sbat = SBat()
            for h2 in range(0, ntl, 2):
                sub = list(range(h2, min(h2 + 2, ntl)))
                for i in sub:
                    t = t0 + i
                    xs = t % 2

                    def _(sy, t=t, xs=xs):
                        xring.acquire(sy, xs)
                        dma(sy, dlx[xs], xtile[:, xs * D:(xs + 1) * D],
                            xp[128 * t:128 * (t + 1), :])
                    eY(_)

                    def _(v, i=i, xs=xs):
                        dlx[xs].wait(v)
                        v.bn_stats(out=stat6[:, 12 * i:12 * i + 6],
                                   in_=xtile[:, xs * D:xs * D + 512])
                        v.bn_stats(out=stat6[:, 12 * i + 6:12 * i + 12],
                                   in_=xtile[:, xs * D + 512:xs * D + 1024])
                    eV(_)

                def _(v, sub=tuple(sub), h2=h2):
                    v.drain()
                    for i in sub:
                        v.bn_aggr(out=mstat[:, 2 * i:2 * i + 2],
                                  in_=stat6[:, 12 * i:12 * i + 12])
                    v.drain()
                    nn = len(sub)
                    va = mstat.ap().rearrange("p (t two) -> p t two", two=2)
                    v.tensor_scalar(out=scol[:, h2:h2 + nn], in0=va[:, h2:h2 + nn, 1],
                                    scalar1=LN_EPS, scalar2=None, op0=ALU.add).then_inc(v2s.h, 1); v2s.n += 1
                eV(_)

                vw = v2s.n
                s2v.n += 1

                def _j(s, h2=h2, nn=len(sub), vw=vw):
                    s.wait_ge(v2s.h, vw)
                    s.activation(out=scol[:, h2:h2 + nn], in_=scol[:, h2:h2 + nn],
                                 func=ACT.Sqrt).then_inc(s2v.h, 1)
                sbat.add(_j)

                def _(v, sub=tuple(sub), h2=h2):
                    s2v.wait(v)
                    nn = len(sub)
                    v.reciprocal(out=scol[:, h2:h2 + nn], in_=scol[:, h2:h2 + nn])
                    va = mstat.ap().rearrange("p (t two) -> p t two", two=2)
                    v.tensor_scalar(out=bcol[:, h2:h2 + nn], in0=va[:, h2:h2 + nn, 0],
                                    scalar1=-1.0, scalar2=None, op0=ALU.mult)
                    v.drain()
                    v.nop().then_inc(v2s.h, 1); v2s.n += 1
                eV(_)

                for i in sub:
                    xs = (t0 + i) % 2

                    def _(v, i=i, xs=xs, xg=xln_guard[0]):
                        if xg:
                            v.wait_ge(ddb.h, xg)
                        v.tensor_scalar(out=xln[:, i * D:(i + 1) * D],
                                        in0=xtile[:, xs * D:(xs + 1) * D],
                                        scalar1=bcol[:, i:i + 1], scalar2=scol[:, i:i + 1],
                                        op0=ALU.add, op1=ALU.mult).then_inc(s2t.h, 1)
                        s2t.n += 1
                    eV(_)
                    xring.release(xs, s2t)
                    if "xln" in dbg_out:
                        t = t0 + i

                        def _(g, t=t, i=i):
                            s2t.wait(g)
                            ddb.wait(g)
                            dma(g, ddb, dbg_out["xln"][128 * t:128 * (t + 1), :],
                                xln[:, i * D:(i + 1) * D])
                        eG(_)

            for c in range(DCH):
                slot = c % 2

                def _(t_, c=c, ntl=ntl, slot=slot):
                    if c == 0:
                        s2t.wait(t_)
                        t_.wait_ge(dcs.h, const_dcs)
                    tp_ring.acquire(t_, slot)
                    for i in range(ntl):
                        mi = t_.transpose(ps_tp[slot][:, 128 * i:128 * (i + 1)],
                                          xln[:, i * D + 128 * c: i * D + 128 * (c + 1)],
                                          ident_sb[:])
                    mi.then_inc(t2v.h, 1); t2v.n += 1
                eT(_)

                def _(v, c=c, W=W, slot=slot):
                    t2v.wait(v)
                    v.tensor_copy(out=xlnT[:, 512 * c:512 * c + W],
                                  in_=ps_tp[slot][:, 0:W]).then_inc(v2t.h, 1)
                    v2t.n += 1
                eV(_)
                tp_ring.release(slot, v2t)
            xlnT_done = v2t.n

            qkv_val = {}

            def emit_qkj(j, W=W):
                slot = j % 3

                def _(t_, j=j, W=W, slot=slot):
                    if j == 0:
                        t_.wait_ge(v2t.h, xlnT_done)
                        t_.wait_ge(dcg.h, const_dcg)
                    qk_ring.acquire(t_, slot)
                    for c in range(DCH):
                        mi = t_.matmul(ps_qk[slot][:, 0:W],
                                       wq_sb[:, 3072 * c + 128 * j: 3072 * c + 128 * (j + 1)],
                                       xlnT[:, 512 * c:512 * c + W],
                                       start=(c == 0), stop=(c == DCH - 1))
                    mi.then_inc(t2s.h, 1); t2s.n += 1
                eT(_)
                qkv_val[j] = t2s.n

            def emit_tail(j, t0=t0, W=W):
                slot = j % 3
                ss = j % 2

                qw = qkv_val[j]
                rw = qraw_ring.waits(ss)
                s2t.n += 1

                def _j(s, W=W, slot=slot, ss=ss, qw=qw, rw=rw):
                    s.wait_ge(t2s.h, qw)
                    for hh, vv in rw:
                        s.wait_ge(hh, vv)
                    s.copy(out=qraw[:, 512 * ss:512 * ss + W],
                           in_=ps_qk[slot][:, 0:W]).then_inc(s2t.h, 1)
                sbat.add(_j)
                ev_val = s2t.n
                qk_ring.release(slot, s2t)

                def _(t_, W=W, ss=ss, ev=ev_val):
                    t_.wait_ge(s2t.h, ev)
                    pm_ring.acquire(t_, ss)
                    t_.matmul(ps_pm[ss][:, 0:W], perm_sb[:],
                              qraw[:, 512 * ss:512 * ss + W],
                              start=True, stop=True).then_inc(t2v.h, 1)
                    t2v.n += 1
                eT(_)
                pm_val = t2v.n

                def _(v, t0=t0, W=W, ss=ss, pv=pm_val):
                    v.wait_ge(t2v.h, pv)
                    qpm_ring.acquire(v, ss)
                    v.tensor_tensor(out=qraw[:, 512 * ss:512 * ss + W],
                                    in0=qraw[:, 512 * ss:512 * ss + W],
                                    in1=cos_sb[:, 128 * t0:128 * t0 + W], op=ALU.mult)
                    v.tensor_tensor(out=qpm[:, 512 * ss:512 * ss + W],
                                    in0=ps_pm[ss][:, 0:W],
                                    in1=sin_sb[:, 128 * t0:128 * t0 + W],
                                    op=ALU.mult).then_inc(v2g.h, 1)
                    v2g.n += 1
                eV(_)
                pm_ring.release(ss, v2g)

                def _(g, j=j, t0=t0, W=W, ss=ss):
                    v2g.wait(g)
                    dest = qT if j < 8 else kT
                    jj = j % 8
                    c0 = TKV * jj + 128 * t0
                    g.tensor_tensor(out=dest[:, c0:c0 + W],
                                    in0=qraw[:, 512 * ss:512 * ss + W],
                                    in1=qpm[:, 512 * ss:512 * ss + W],
                                    op=ALU.add).then_inc(g2t.h, 1)
                    g2t.n += 1
                eG(_)
                qraw_ring.release(ss, g2t)
                qpm_ring.release(ss, g2t)

            for j in range(16):
                emit_qkj(j)
                if j >= 1:
                    emit_tail(j - 1)
            emit_tail(15)
            sbat.flush(eS)

            if "xln" in dbg_out:
                xln_guard[0] = ddb.n
            vmm_val = {}

            def emit_vmm(i, hf):
                slot = (2 * i + hf) % 3

                def _(t_, i=i, hf=hf, slot=slot):
                    qk_ring.acquire(t_, slot)
                    for c in range(DCH):
                        mi = t_.matmul(ps_qk[slot][:, 0:512],
                                       xlnT[:, 512 * c + 128 * i: 512 * c + 128 * (i + 1)],
                                       wq_sb[:, 3072 * c + 2048 + 512 * hf: 3072 * c + 2048 + 512 * (hf + 1)],
                                       start=(c == 0), stop=(c == DCH - 1))
                    mi.then_inc(t2v.h, 1); t2v.n += 1
                eT(_)
                vmm_val[(i, hf)] = t2v.n

            def emit_vevac(i, hf, t0=t0):
                t = t0 + i
                slot = (2 * i + hf) % 3

                def _(v, t=t, hf=hf, slot=slot, mv=vmm_val[(i, hf)]):
                    v.wait_ge(t2v.h, mv)
                    blkap = vtm.ap()[:, 1040 * t: 1040 * (t + 1)].rearrange(
                        "p (g w) -> p g w", w=65)
                    if hf == 0:
                        v.memset(blkap[:, :, 64:65], 1.0)
                    dst8 = blkap[:, 8 * hf:8 * (hf + 1), 0:64]
                    src8 = ps_qk[slot].ap()[:, 0:512].rearrange(
                        "p (g w) -> p g w", w=64)
                    v.tensor_copy(out=dst8, in_=src8).then_inc(v2t.h, 1)
                    v2t.n += 1
                eV(_)
                qk_ring.release(slot, v2t)

            vjobs = [(i, hf) for i in range(ntl) for hf in range(2)]
            for n_, (i, hf) in enumerate(vjobs):
                emit_vmm(i, hf)
                if n_ >= 1:
                    emit_vevac(*vjobs[n_ - 1])
            emit_vevac(*vjobs[-1])

        if any(k in dbg_out for k in ("qT", "kT", "vtm")):
            def _(g):
                v2t.wait(g)
                ddb.wait(g)
                for name, sbuf in (("qT", qT), ("kT", kT), ("vtm", vtm)):
                    if name in dbg_out:
                        dma(g, ddb, dbg_out[name], sbuf[:])
                g.wait_ge(ddb.h, ddb.n)
            eG(_)

    p1.close()
    if stage <= 2:
        pA.close()
        st.close()
        return nc, dbg_out

    wo_sb = pA.enter_context(nc.sbuf_tensor("wo_sb", [128, DCH * D], BF16))
    attn_T = pA.enter_context(nc.sbuf_tensor("attn_T", [128, DCH * TQ], BF16))
    p2 = ExitStack()
    mask_sb = p2.enter_context(nc.sbuf_tensor("mask_sb", [128, NT * 384], BF16))
    pb = p2.enter_context(nc.sbuf_tensor("pb", [128, 4 * 384], BF16))
    sums_sb = p2.enter_context(nc.sbuf_tensor("sums_sb", [16, TQ], F32))
    sums_st = p2.enter_context(nc.sbuf_tensor("sums_st", [1, TQ], F32))
    sums_bf = p2.enter_context(nc.sbuf_tensor("sums_bf", [16, TQ], BF16))
    rrow_bf = p2.enter_context(nc.sbuf_tensor("rrow_bf", [1, TQ], BF16))
    rb_sb = p2.enter_context(nc.sbuf_tensor("rb_sb", [128, 2 * TQ], BF16))

    with nc.Block() as blk:
        fence(blk)
        eT, eV, eS, eG, eY = blk.tensor, blk.vector, blk.scalar, blk.gpsimd, blk.sync

        def _(g):
            g.wait_ge(dcg.h, dcg.n)
            dma(g, dcg, mask_sb[:], maskh)
            dma(g, dcg, wo_sb[:], wo)
        eG(_)
        p2_loads = dcg.n

        at_ps = p2.enter_context(nc.psum_tensor("at_ps", [128, TQ], F32))
        p2s = ExitStack()
        scp = p2s.enter_context(nc.psum_tensor("scp", [128, 4 * 512], F32))

        sc_ring, pb_ring, at_ring = Ring(4), Ring(4), Ring(1)
        sums_stg_ring = Ring(1)

        pv_pairs = [(c, t) for c in range(NT) for t in tset(c)]
        pv_last = {}
        for c, t in pv_pairs:
            pv_last[(t - 1) // 4] = (c, t)
        p2bat = SBat()
        for h in range(H):
            jj, po = h // 2, 64 * (h % 2)
            fresh = set(range(4))
            qk_val = {}

            def emit_qk2(g, h=h, jj=jj, po=po):
                for c in (2 * g, 2 * g + 1):
                    slot = c % 4

                    def _(t_, c=c, slot=slot, h=h, jj=jj, po=po):
                        if h == 0 and c == 0:
                            t_.wait_ge(dcg.h, p2_loads)
                        sc_ring.acquire(t_, slot)
                        q0 = qws(c)
                        t_.matmul(scp[:, 512 * slot:512 * slot + 384],
                                  kT[po:po + 64, TKV * jj + 128 * c: TKV * jj + 128 * (c + 1)],
                                  qT[po:po + 64, TKV * jj + q0: TKV * jj + q0 + 384],
                                  start=True, stop=True,
                                  tile_position=(po, 0)).then_inc(t2s.h, 1)
                        t2s.n += 1
                    eT(_)
                qk_val[g] = t2s.n

            def emit_exp_pv(g, h=h, fresh=fresh):
                c0 = 2 * g
                s0 = c0 % 4

                qw = qk_val[g]
                rw = pb_ring.waits(s0) + pb_ring.waits(s0 + 1)
                s2t.n += 1

                def _j(s, s0=s0, qw=qw, rw=rw):
                    s.wait_ge(t2s.h, qw)
                    for hh, vv in rw:
                        s.wait_ge(hh, vv)
                    inap = scp.ap()[:, 512 * s0:512 * s0 + 1024].rearrange(
                        "p (two w) -> p two w", w=512)[:, :, 0:384]
                    outap = pb.ap()[:, 384 * s0:384 * (s0 + 2)].rearrange(
                        "p (two w) -> p two w", w=384)
                    s.activation(out=outap, in_=inap, func=ACT.Exp,
                                 scale=0.125, bias=0.0).then_inc(s2t.h, 1)
                p2bat.add(_j)
                exp_val = s2t.n
                sc_ring.release(s0, s2t)
                sc_ring.release(s0 + 1, s2t)

                msem = g2t if g % 2 else v2y

                def _(gq, g=g, s0=s0, c0=c0, ev=exp_val, msem=msem):
                    gq.wait_ge(s2t.h, ev)
                    gq.tensor_tensor(out=pb[:, 384 * s0:384 * (s0 + 2)],
                                     in0=pb[:, 384 * s0:384 * (s0 + 2)],
                                     in1=mask_sb[:, 384 * c0:384 * (c0 + 2)],
                                     op=ALU.mult).then_inc(msem.h, 1)
                    msem.n += 1
                (eG if g % 2 else eV)(_)
                msk_sem, msk_val = msem, msem.n

                for c in (c0, c0 + 1):
                    pslot = c % 4

                    def _(t_, c=c, pslot=pslot, h=h, fresh=fresh,
                          mh=msk_sem.h, ev=msk_val):
                        t_.wait_ge(mh, ev)
                        if c == 0:
                            at_ring.acquire(t_, 0)
                        q0 = qws(c)
                        for t in tset(c):
                            j0 = 128 * t - q0
                            bank = (t - 1) // 4
                            stf = bank in fresh
                            fresh.discard(bank)
                            last = pv_last[bank] == (c, t)
                            mi = t_.matmul(at_ps[0:65, 128 * (t - 1):128 * t],
                                           vtm[:, 1040 * c + 65 * h: 1040 * c + 65 * h + 65],
                                           pb[:, 384 * pslot + j0: 384 * pslot + j0 + 128],
                                           start=stf, stop=last, tile_position=(0, 0))
                        mi.then_inc(t2v.h, 1); t2v.n += 1
                    eT(_)
                    pb_ring.release(pslot, t2v)

            for g in range(9):
                emit_qk2(g)
                if g >= 1:
                    emit_exp_pv(g - 1)
            emit_exp_pv(8)

            hs = h % 2

            def _(v, h=h):
                t2v.wait(v)
                sums_stg_ring.acquire(v, 0)
                v.tensor_copy(out=sums_st[0:1, :],
                              in_=at_ps[64:65, 0:TQ]).then_inc(s2t.h, 1)
                s2t.n += 1
            eV(_)
            sums_done = s2t.n

            def _(sy, h=h, hs=hs):
                sy.wait_ge(s2t.h, sums_done)
                dma(sy, dxr[hs], sums_sb[h:h + 1, :], sums_st[0:1, :])
            eY(_)
            sums_stg_ring.release(0, dxr[hs])

            tw = t2v.n
            v2t.n += 1

            def _j(s, jj=jj, po=po, tw=tw):
                s.wait_ge(t2v.h, tw)
                s.copy(out=attn_T[po:po + 64, TQ * jj:TQ * (jj + 1)],
                       in_=at_ps[0:64, 0:TQ]).then_inc(v2t.h, 1)
            p2bat.add(_j)

            def _(t_, sd=sums_done):
                t_.wait_ge(s2t.h, sd)
            eT(_)
            at_ring.release(0, v2t)
            at_ring.release_add(0, s2t)
            p2bat.flush(eS)

        def _(v):
            v.wait_ge(dxr[0].h, dxr[0].n)
            v.wait_ge(dxr[1].h, dxr[1].n)
            v.reciprocal(out=sums_sb[:], in_=sums_sb[:])
            v.drain()
            v.tensor_copy(out=sums_bf[:], in_=sums_sb[:]).then_inc(v2t.h, 1)
            v2t.n += 1
        eV(_)
        recip_done_v = v2t.n

        rrow_ring = Ring(1)
        rb_ring = Ring(2)

        p2s.close()
        rb_ps = p2.enter_context(nc.psum_tensor("rb_ps", [128, TQ], F32))
        rbps_ring = Ring(1)
        p2bat2 = SBat()

        for h in range(H):
            jj, po = h // 2, 64 * (h % 2)

            def _(sy, h=h):
                if h == 0:
                    sy.wait_ge(v2t.h, recip_done_v)
                rrow_ring.acquire(sy, 0)
                dma(sy, dlx[0], rrow_bf[0:1, :], sums_bf[h:h + 1, :])
            eY(_)

            def _(t_, po=po):
                dlx[0].wait(t_)
                rbps_ring.acquire(t_, 0)
                for g in range(4):
                    mi = t_.matmul(rb_ps[po:po + 64, 512 * g:512 * (g + 1)],
                                   onesr_sb[0:1, 0:64],
                                   rrow_bf[0:1, 512 * g:512 * (g + 1)],
                                   start=True, stop=True,
                                   tile_position=(0, po))
                mi.then_inc(t2s.h, 1); t2s.n += 1
            eT(_)
            rrow_ring.release(0, t2s)

            hs = h % 2

            tw = t2s.n
            rw = rb_ring.waits(hs)
            s2v.n += 1

            def _j(s, po=po, hs=hs, tw=tw, rw=rw):
                s.wait_ge(t2s.h, tw)
                for hh, vv in rw:
                    s.wait_ge(hh, vv)
                s.copy(out=rb_sb[po:po + 64, TQ * hs:TQ * (hs + 1)],
                       in_=rb_ps[po:po + 64, :]).then_inc(s2v.h, 1)
            p2bat2.add(_j)
            rbps_ring.release(0, s2v)

            def _(v, jj=jj, po=po, hs=hs):
                s2v.wait(v)
                dest = attn_T[po:po + 64, TQ * jj:TQ * (jj + 1)]
                v.tensor_tensor(out=dest, in0=dest,
                                in1=rb_sb[po:po + 64, TQ * hs:TQ * (hs + 1)],
                                op=ALU.mult).then_inc(v2t.h, 1)
                v2t.n += 1
            eV(_)
            rb_ring.release(hs, v2t)
        p2bat2.flush(eS)

        if "attn" in dbg_out:
            def _(g):
                v2t.wait(g)
                ddb.wait(g)
                dma(g, ddb, dbg_out["attn"], attn_T[:])
                g.wait_ge(ddb.h, ddb.n)
            eG(_)

    p2.close()
    if stage <= 3:
        pA.close()
        st.close()
        return nc, dbg_out

    p25 = ExitStack()
    yln = p25.enter_context(nc.sbuf_tensor("yln", [128, 2 * D], F32))
    xrr = p25.enter_context(nc.sbuf_tensor("xrr", [128, 2 * D], F32))
    x2t = p25.enter_context(nc.sbuf_tensor("x2t", [128, 2 * D], F32))
    st6b = p25.enter_context(nc.sbuf_tensor("st6b", [128, 24], F32))
    msb = p25.enter_context(nc.sbuf_tensor("msb", [128, 4], F32))
    scl2 = p25.enter_context(nc.sbuf_tensor("scl2", [128, 2], F32))
    bcl2 = p25.enter_context(nc.sbuf_tensor("bcl2", [128, 2], F32))
    bor_sb = p25.enter_context(nc.sbuf_tensor("bor_sb", [1, D], BF16))

    spill_val = {}
    with nc.Block() as blk:
        fence(blk)
        eT, eV, eS, eG, eY = blk.tensor, blk.vector, blk.scalar, blk.gpsimd, blk.sync

        def _(g):
            g.wait_ge(dcg.h, dcg.n)
            dma(g, dcg, bor_sb[:], bor)
        eG(_)
        p25_loads = dcg.n

        y_ps = [p25.enter_context(nc.psum_tensor(f"y_ps{i}", [128, D], F32)) for i in range(2)]
        yps_ring, yln_ring = Ring(2), Ring(2)
        xrr_ring, x2_ring = Ring(2), Ring(2)

        p25bat = SBat()
        for tt in range(16):
            slot = tt % 2

            def _(sy, tt=tt, slot=slot):
                xrr_ring.acquire(sy, slot)
                dma(sy, dxr[slot], xrr[:, slot * D:(slot + 1) * D],
                    xp[128 * (tt + 1):128 * (tt + 2), :])
            eY(_)

            def _(t_, tt=tt, slot=slot):
                if tt == 0:
                    t_.wait_ge(dcg.h, p25_loads)
                yps_ring.acquire(t_, slot)
                for hf in range(2):
                    t_.matmul(y_ps[slot][:, 512 * hf:512 * (hf + 1)], onesr_sb[:],
                              bor_sb[0:1, 512 * hf:512 * (hf + 1)], start=True, stop=False)
                    for c in range(DCH):
                        mi = t_.matmul(y_ps[slot][:, 512 * hf:512 * (hf + 1)],
                                       attn_T[:, TQ * c + 128 * tt: TQ * c + 128 * (tt + 1)],
                                       wo_sb[:, 1024 * c + 512 * hf: 1024 * c + 512 * (hf + 1)],
                                       start=False, stop=(c == DCH - 1))
                mi.then_inc(t2s.h, 1); t2s.n += 1
            eT(_)

            def _(v, slot=slot):
                t2s.wait(v)
                v.bn_stats(out=st6b[:, 12 * slot:12 * slot + 6],
                           in_=y_ps[slot][:, 0:512])
                v.bn_stats(out=st6b[:, 12 * slot + 6:12 * slot + 12],
                           in_=y_ps[slot][:, 512:1024])
                v.drain()
                v.bn_aggr(out=msb[:, 2 * slot:2 * slot + 2],
                          in_=st6b[:, 12 * slot:12 * slot + 12])
                v.drain()
                v.tensor_scalar(out=scl2[:, slot:slot + 1],
                                in0=msb[:, 2 * slot + 1:2 * slot + 2],
                                scalar1=LN_EPS, scalar2=None, op0=ALU.add).then_inc(v2s.h, 1); v2s.n += 1
            eV(_)

            vw = v2s.n
            s2v.n += 1

            def _j(s, slot=slot, vw=vw):
                s.wait_ge(v2s.h, vw)
                s.activation(out=scl2[:, slot:slot + 1], in_=scl2[:, slot:slot + 1],
                             func=ACT.Sqrt).then_inc(s2v.h, 1)
            p25bat.add(_j)

            def _(v, slot=slot):
                s2v.wait(v)
                v.reciprocal(out=scl2[:, slot:slot + 1], in_=scl2[:, slot:slot + 1])
                v.tensor_scalar(out=bcl2[:, slot:slot + 1],
                                in0=msb[:, 2 * slot:2 * slot + 1],
                                scalar1=-1.0, scalar2=None, op0=ALU.mult)
                v.drain()
                yln_ring.acquire(v, slot)
                v.tensor_scalar(out=yln[:, slot * D:(slot + 1) * D],
                                in0=y_ps[slot][:, :],
                                scalar1=bcl2[:, slot:slot + 1],
                                scalar2=scl2[:, slot:slot + 1],
                                op0=ALU.add, op1=ALU.mult).then_inc(v2g.h, 1)
                v2g.n += 1
            eV(_)
            yps_ring.release(slot, v2g)

            def _(g, slot=slot):
                v2g.wait(g)
                dxr[slot].wait(g)
                x2_ring.acquire(g, slot)
                g.tensor_tensor(out=x2t[:, slot * D:(slot + 1) * D],
                                in0=yln[:, slot * D:(slot + 1) * D],
                                in1=xrr[:, slot * D:(slot + 1) * D],
                                op=ALU.add).then_inc(g2t.h, 1)
                g2t.n += 1
            eG(_)
            yln_ring.release(slot, g2t)
            xrr_ring.release(slot, g2t)

            def _(sy, tt=tt, slot=slot):
                g2t.wait(sy)
                dma(sy, dsp[slot], x2sp[128 * tt:128 * (tt + 1), :],
                    x2t[:, slot * D:(slot + 1) * D])
            eY(_)
            spill_val[tt] = (dsp[slot], dsp[slot].n)
            x2_ring.release(slot, dsp[slot])
            if "x2" in dbg_out:
                def _(g, tt=tt, slot=slot):
                    g2t.wait(g)
                    ddb.wait(g)
                    dma(g, ddb, dbg_out["x2"][128 * tt:128 * (tt + 1), :],
                        x2t[:, slot * D:(slot + 1) * D])
                eG(_)
                x2_ring.release_add(slot, ddb)

        p25bat.flush(eS)

        def _(sy):
            sy.wait_ge(dsp[0].h, dsp[0].n)
            sy.wait_ge(dsp[1].h, dsp[1].n)
        eY(_)

    p25.close()
    pA.close()
    if stage <= 4:
        st.close()
        return nc, dbg_out

    comb_sb = st.enter_context(nc.sbuf_tensor("comb_sb", [128, 16 * ES], BF16))
    slin = st.enter_context(nc.sbuf_tensor("slin", [128, 2 * D], BF16))
    w1_sb = st.enter_context(nc.sbuf_tensor("w1_sb", [128, E * DCH * HID], BF16))
    p3 = ExitStack()
    xn_sb = p3.enter_context(nc.sbuf_tensor("xn_sb", [128, 16 * 1056], BF16))
    E_sb = p3.enter_context(nc.sbuf_tensor("E_sb", [128, 16 * ES], BF16))
    gbc_sb = p3.enter_context(nc.sbuf_tensor("gbc_sb", [128, D], F32))
    bbc_sb = p3.enter_context(nc.sbuf_tensor("bbc_sb", [128, D], F32))
    scbc_sb = p3.enter_context(nc.sbuf_tensor("scbc_sb", [128, 1], F32))
    mu_sb = p3.enter_context(nc.sbuf_tensor("mu_sb", [128, DCH * ES], F32))
    mu2_sb = p3.enter_context(nc.sbuf_tensor("mu2_sb", [128, DCH * ES], BF16))
    nr_sb = p3.enter_context(nc.sbuf_tensor("nr_sb", [1, ES], F32))
    muU_sb = p3.enter_context(nc.sbuf_tensor("muU_sb", [128, DCH * ES], BF16))
    x2r = p3.enter_context(nc.sbuf_tensor("x2r", [128, 4 * D], F32))
    u2sb = p3.enter_context(nc.sbuf_tensor("u2sb", [128, 2 * D], BF16))
    xnT_sb = p3.enter_context(nc.sbuf_tensor("xnT_sb", [128, 2 * 1024], BF16))
    rs_sb = p3.enter_context(nc.sbuf_tensor("rs_sb", [128, 16], F32))
    st6c = p3.enter_context(nc.sbuf_tensor("st6c", [128, 48], F32))
    msc = p3.enter_context(nc.sbuf_tensor("msc", [128, 8], F32))
    s1c = p3.enter_context(nc.sbuf_tensor("s1c", [128, 4], F32))
    s2c = p3.enter_context(nc.sbuf_tensor("s2c", [128, 4], F32))
    b12 = p3.enter_context(nc.sbuf_tensor("b12", [128, 4], F32))
    sq_sb = p3.enter_context(nc.sbuf_tensor("sq_sb", [128, D], F32))
    fac = p3.enter_context(nc.sbuf_tensor("fac", [128, 16], F32))
    ccin_sb = p3.enter_context(nc.sbuf_tensor("ccin_sb", [128, 2050], F32))
    usb = ccin_sb

    with nc.Block() as blk:
        fence(blk)
        eT, eV, eS, eG, eY = blk.tensor, blk.vector, blk.scalar, blk.gpsimd, blk.sync

        def _(g):
            g.wait_ge(dcg.h, dcg.n)
            dma(g, dcg, w1_sb[:], w1h)
        eG(_)

        def _(sy):
            sy.wait_ge(dcs.h, dcs.n)
            dma(sy, dcs, gbc_sb[:], gbc)
            dma(sy, dcs, bbc_sb[:], bbc)
            dma(sy, dcs, scbc_sb[:], scbc)
            dma(sy, dcs, mu_sb[:], muh)
        eY(_)
        p3_loads = dcs.n

        p3s = ExitStack()
        ps_tb = [p3s.enter_context(nc.psum_tensor(f"ps_tb{i}", [128, 512], BF16)) for i in range(2)]
        lg_ps = [p3s.enter_context(nc.psum_tensor(f"lg_ps{i}", [128, 512], F32)) for i in range(2)]
        nb_ps = p3s.enter_context(nc.psum_tensor("nb_ps", [128, ES], F32))

        b4a = SBat()
        dw = p3_loads
        s2t.n += 1

        def _j(s, dw=dw):
            s.wait_ge(dcs.h, dw)
            s.activation(out=mu2_sb[:], in_=mu_sb[:], func=ACT.Square).then_inc(s2t.h, 1)
        b4a.add(_j)

        def _(t_):
            s2t.wait(t_)
            for c in range(DCH):
                mi = t_.matmul(nb_ps[0:1, :], onesc_sb[:], mu2_sb[:, ES * c:ES * (c + 1)],
                               start=(c == 0), stop=(c == DCH - 1))
            mi.then_inc(t2s.h, 1); t2s.n += 1
        eT(_)

        tw = t2s.n
        s2v.n += 1

        def _j(s, tw=tw):
            s.wait_ge(t2s.h, tw)
            s.activation(out=nr_sb[:], in_=nb_ps[0:1, :], func=ACT.Sqrt).then_inc(s2v.h, 1)
        b4a.add(_j)
        b4a.flush(eS)

        def _(v):
            s2v.wait(v)
            v.tensor_scalar(out=nr_sb[:], in0=nr_sb[:], scalar1=1e-12, scalar2=None,
                            op0=ALU.max)
            v.drain()
            v.reciprocal(out=nr_sb[:], in_=nr_sb[:])
            v.drain().then_inc(v2t.h, 1); v2t.n += 1
        eV(_)

        def _(t_):
            v2t.wait(t_)
            t_.matmul(nb_ps[:, :], onesrf_sb[:], nr_sb[0:1, :],
                      start=True, stop=True).then_inc(t2v.h, 1)
            t2v.n += 1
        eT(_)

        def _(v):
            t2v.wait(v)
            for c in range(DCH):
                mi = v.tensor_tensor(out=muU_sb[:, ES * c:ES * (c + 1)],
                                     in0=mu_sb[:, ES * c:ES * (c + 1)], in1=nb_ps[:, :],
                                     op=ALU.mult)
            mi.then_inc(v2t.h, 1); v2t.n += 1
        eV(_)
        muU_done = v2t.n

        tb_ring, lg_ring, xnT_ring = Ring(2), Ring(2), Ring(2)
        x2r_ring, u2_ring = Ring(4), Ring(2)
        for gg in range(4):
            tiles = list(range(4 * gg, 4 * gg + 4))
            b4s, b4e = SBat(), SBat()
            for i, tt in enumerate(tiles):
                def _(sy, tt=tt, i=i):
                    x2r_ring.acquire(sy, i)
                    sv = spill_val[tt]
                    sy.wait_ge(sv[0].h, sv[1])
                    dma(sy, dxc[i], x2r[:, i * D:(i + 1) * D],
                        x2sp[128 * tt:128 * (tt + 1), :])
                eY(_)

                def _(v, i=i):
                    dxc[i].wait(v)
                    v.bn_stats(out=st6c[:, 12 * i:12 * i + 6], in_=x2r[:, i * D:i * D + 512])
                    v.bn_stats(out=st6c[:, 12 * i + 6:12 * i + 12],
                               in_=x2r[:, i * D + 512:i * D + 1024])
                eV(_)

            def _(v):
                v.drain()
                for i in range(4):
                    v.bn_aggr(out=msc[:, 2 * i:2 * i + 2], in_=st6c[:, 12 * i:12 * i + 12])
                v.drain()
                va = msc.ap().rearrange("p (t two) -> p t two", two=2)
                v.tensor_scalar(out=s1c[:, 0:4], in0=va[:, 0:4, 1], scalar1=LN_EPS,
                                scalar2=None, op0=ALU.add).then_inc(v2s.h, 1); v2s.n += 1
            eV(_)

            vw = v2s.n
            s2v.n += 1

            def _j(s, vw=vw):
                s.wait_ge(v2s.h, vw)
                s.activation(out=s1c[:, 0:4], in_=s1c[:, 0:4], func=ACT.Sqrt).then_inc(s2v.h, 1)
            b4s.add(_j)

            def _(v):
                s2v.wait(v)
                v.reciprocal(out=s1c[:, 0:4], in_=s1c[:, 0:4])
                v.drain()
                va = msc.ap().rearrange("p (t two) -> p t two", two=2)
                v.tensor_tensor(out=s2c[:, 0:4], in0=s1c[:, 0:4], in1=s1c[:, 0:4],
                                op=ALU.mult)
                v.drain()
                v.tensor_tensor(out=s2c[:, 0:4], in0=s2c[:, 0:4], in1=va[:, 0:4, 1],
                                op=ALU.mult)
                v.drain()
                v.tensor_scalar(out=s2c[:, 0:4], in0=s2c[:, 0:4], scalar1=LN_EPS,
                                scalar2=None, op0=ALU.add).then_inc(v2s.h, 1); v2s.n += 1
            eV(_)

            vw = v2s.n
            s2v.n += 1

            def _j(s, vw=vw):
                s.wait_ge(v2s.h, vw)
                s.activation(out=s2c[:, 0:4], in_=s2c[:, 0:4], func=ACT.Sqrt).then_inc(s2v.h, 1)
            b4s.add(_j)

            def _(v):
                s2v.wait(v)
                v.reciprocal(out=s2c[:, 0:4], in_=s2c[:, 0:4])
                v.drain()
                v.tensor_tensor(out=s1c[:, 0:4], in0=s1c[:, 0:4], in1=s2c[:, 0:4],
                                op=ALU.mult)
                va = msc.ap().rearrange("p (t two) -> p t two", two=2)
                v.tensor_scalar(out=b12[:, 0:4], in0=va[:, 0:4, 0],
                                scalar1=-1.0, scalar2=None, op0=ALU.mult)
                v.drain()
                v.tensor_tensor(out=b12[:, 0:4], in0=b12[:, 0:4], in1=s1c[:, 0:4],
                                op=ALU.mult)
                v.drain()
                v.nop().then_inc(v2s.h, 1); v2s.n += 1
            eV(_)
            chain_done = v2s.n

            sq_w = {}
            for i, tt in enumerate(tiles):
                us = tt % 2

                ruw = u2_ring.waits(us)
                s2v.n += 1

                def _j(s, i=i, us=us, cd=chain_done, ruw=ruw):
                    s.wait_ge(v2s.h, cd)
                    for hh, vv in ruw:
                        s.wait_ge(hh, vv)
                    s.activation(out=u2sb[:, us * D:(us + 1) * D],
                                 in_=x2r[:, i * D:(i + 1) * D], func=ACT.Identity,
                                 scale=s1c[:, i:i + 1],
                                 bias=b12[:, i:i + 1]).then_inc(s2v.h, 1)
                b4s.add(_j)
                x2r_ring.release(i, s2v)
                app_val = s2v.n

                def _(v, tt=tt, us=us, av=app_val):
                    v.wait_ge(s2v.h, av)
                    c0 = 1056 * tt
                    v.tensor_tensor(out=xn_sb[:, c0:c0 + D],
                                    in0=u2sb[:, us * D:(us + 1) * D],
                                    in1=gbc_sb[:], op=ALU.mult)
                    v.memset(xn_sb[:, c0 + D:c0 + 1056], 1.0)
                    v.drain()
                    v.nop().then_inc(v2g.h, 1); v2g.n += 1
                eV(_)
                u2_ring.release(us, v2g)

                def _(g, tt=tt):
                    v2g.wait(g)
                    c0 = 1056 * tt
                    g.tensor_tensor(out=xn_sb[:, c0:c0 + D], in0=xn_sb[:, c0:c0 + D],
                                    in1=bbc_sb[:], op=ALU.add).then_inc(v2t.h, 1)
                    v2t.n += 1
                eG(_)
                sq_w[i] = v2t.n

            for i, tt in enumerate(tiles):
                pw = sacc.n
                sacc.n += 1

                def _j(s, tt=tt, vw=sq_w[i], pw=pw):
                    s.wait_ge(v2t.h, vw)
                    if pw:
                        s.wait_ge(sacc.h, pw)
                    c0 = 1056 * tt
                    s.activation(out=sq_sb[:, :], in_=xn_sb[:, c0:c0 + D],
                                 func=ACT.Square,
                                 accum_out=fac[:, tt:tt + 1]).then_inc(sacc.h, 1)
                b4s.add(_j)

            sw = sacc.n
            s2v.n += 1

            def _j(s, gg=gg, sw=sw):
                s.wait_ge(sacc.h, sw)
                s.activation(out=fac[:, 4 * gg:4 * gg + 4], in_=fac[:, 4 * gg:4 * gg + 4],
                             func=ACT.Sqrt).then_inc(s2v.h, 1)
            b4s.add(_j)
            b4s.flush(eS)

            def _(v, gg=gg):
                s2v.wait(v)
                v.tensor_scalar(out=fac[:, 4 * gg:4 * gg + 4], in0=fac[:, 4 * gg:4 * gg + 4],
                                scalar1=1e-12, scalar2=None, op0=ALU.max)
                v.drain()
                v.reciprocal(out=fac[:, 4 * gg:4 * gg + 4], in_=fac[:, 4 * gg:4 * gg + 4])
                v.drain()
                v.tensor_scalar(out=fac[:, 4 * gg:4 * gg + 4], in0=fac[:, 4 * gg:4 * gg + 4],
                                scalar1=scbc_sb[:, 0:1], scalar2=None,
                                op0=ALU.mult).then_inc(v2s.h, 1); v2s.n += 1
            eV(_)
            fac_done = v2s.n

            for i, tt in enumerate(tiles):
                xslot = tt % 2
                for half in range(2):
                    slot = (2 * tt + half) % 2

                    def _(t_, tt=tt, half=half, slot=slot):
                        v2t.wait(t_)
                        tb_ring.acquire(t_, slot)
                        for cc in range(4):
                            c = 4 * half + cc
                            mi = t_.transpose(ps_tb[slot][:, 128 * cc:128 * (cc + 1)],
                                              xn_sb[:, 1056 * tt + 128 * c:1056 * tt + 128 * (c + 1)],
                                              ident_sb[:])
                        mi.then_inc(t2v.h, 1); t2v.n += 1
                    eT(_)

                    def _(v, half=half, slot=slot, xslot=xslot):
                        t2v.wait(v)
                        if half == 0:
                            xnT_ring.acquire(v, xslot)
                        v.tensor_copy(
                            out=xnT_sb[:, 1024 * xslot + 512 * half:1024 * xslot + 512 * (half + 1)],
                            in_=ps_tb[slot][:, :]).then_inc(v2t.h, 1)
                        v2t.n += 1
                    eV(_)
                    tb_ring.release(slot, v2t)

                def _(t_, tt=tt, xslot=xslot):
                    v2t.wait(t_)
                    if tt == 0:
                        t_.wait_ge(v2t.h, muU_done)
                    slot = tt % 2
                    lg_ring.acquire(t_, slot)
                    for c in range(DCH):
                        mi = t_.matmul(lg_ps[slot][:, 0:ES],
                                       xnT_sb[:, 1024 * xslot + 128 * c:1024 * xslot + 128 * (c + 1)],
                                       muU_sb[:, ES * c:ES * (c + 1)],
                                       start=(c == 0), stop=(c == DCH - 1))
                    mi.then_inc(t2s.h, 1); t2s.n += 1
                eT(_)
                xnT_ring.release(xslot, t2s)

                tw = t2s.n
                s2v.n += 1

                def _j(s, tt=tt, fd=fac_done, tw=tw):
                    s.wait_ge(t2s.h, tw)
                    s.wait_ge(v2s.h, fd)
                    slot = tt % 2
                    s.activation(out=E_sb[:, ES * tt:ES * (tt + 1)],
                                 in_=lg_ps[slot][:, 0:ES],
                                 func=ACT.Exp, scale=fac[:, tt:tt + 1],
                                 accum_out=rs_sb[:, tt:tt + 1]).then_inc(s2v.h, 1)
                b4e.add(_j)
                lg_ring.release(tt % 2, s2v)
            b4e.flush(eS)

        def _(v):
            s2v.wait(v)
            v.reciprocal(out=rs_sb[:], in_=rs_sb[:])
            v.drain().then_inc(v2s.h, 1); v2s.n += 1
        eV(_)

        b4i = SBat()
        rw0 = v2s.n
        for tt in range(16):
            s2t.n += 1

            def _j(s, tt=tt, rw0=rw0):
                if tt == 0:
                    s.wait_ge(v2s.h, rw0)
                s.activation(out=comb_sb[:, ES * tt:ES * (tt + 1)],
                             in_=E_sb[:, ES * tt:ES * (tt + 1)], func=ACT.Identity,
                             scale=rs_sb[:, tt:tt + 1]).then_inc(s2t.h, 1)
            b4i.add(_j)
        b4i.flush(eS)
        comb_done = s2t.n

        p3s.close()
        u_ps = [p3.enter_context(nc.psum_tensor(f"u_ps{i}", [128, 512], F32)) for i in range(4)]
        ue_ps = p3.enter_context(nc.psum_tensor("ue_ps", [128, 512], F32))

        def _(t_):
            t_.wait_ge(s2t.h, comb_done)
            for half in range(2):
                for tt in range(16):
                    lhs = E_sb[:, ES * tt + 128 * half: ES * tt + 128 * (half + 1)]
                    stf = (tt == 0)
                    t_.matmul(u_ps[2 * half][:, :], lhs,
                              xn_sb[:, 1056 * tt:1056 * tt + 512],
                              start=stf, stop=(tt == 15))
                    t_.matmul(u_ps[2 * half + 1][:, :], lhs,
                              xn_sb[:, 1056 * tt + 512:1056 * tt + 1024],
                              start=stf, stop=(tt == 15))
                    mi = t_.matmul(ue_ps[:, half:half + 1], lhs,
                                   xn_sb[:, 1056 * tt + 1024:1056 * tt + 1025],
                                   start=stf, stop=(tt == 15))
            mi.then_inc(t2v.h, 1); t2v.n += 1
        eT(_)

        def _(v):
            t2v.wait(v)
            v.tensor_copy(out=ccin_sb[:, 0:512], in_=u_ps[0][:, :])
            v.tensor_copy(out=ccin_sb[:, 512:1024], in_=u_ps[1][:, :])
            v.tensor_copy(out=ccin_sb[:, 1024:1025], in_=ue_ps[:, 0:1])
            v.tensor_copy(out=ccin_sb[:, 1025:1537], in_=u_ps[2][:, :])
            v.tensor_copy(out=ccin_sb[:, 1537:2049], in_=u_ps[3][:, :])
            v.tensor_copy(out=ccin_sb[:, 2049:2050], in_=ue_ps[:, 1:2]).then_inc(v2t.h, 1); v2t.n += 1
        eV(_)

        def _(g):
            v2t.wait(g)
            dma(g, ccd, ccin.ap(), ccin_sb[:])
            g.wait_ge(ccd.h, ccd.n)
            g.collective_compute(
                "AllReduce", ALU.add,
                replica_groups=[[0, 1], [2, 3], [4, 5], [6, 7]],
                ins=[ccin.ap().opt()], outs=[ccout.ap().opt()],
            ).then_inc(ccS.h, 1)
            ccS.n += 1
            g.wait_ge(ccS.h, ccS.n)
            dma(g, ccd, usb[:], ccout.ap())
        eG(_)

        def _(v):
            ccd.wait(v)
            v.reciprocal(out=usb[:, 1024:1025], in_=usb[:, 1024:1025])
            v.reciprocal(out=usb[:, 2049:2050], in_=usb[:, 2049:2050])
            v.drain()
            v.tensor_scalar(out=slin[:, 0:D], in0=usb[:, 0:1024],
                            scalar1=usb[:, 1024:1025], scalar2=None, op0=ALU.mult)
            v.tensor_scalar(out=slin[:, D:2 * D], in0=usb[:, 1025:2049],
                            scalar1=usb[:, 2049:2050], scalar2=None, op0=ALU.mult).then_inc(v2t.h, 1); v2t.n += 1
        eV(_)

        if any(k in dbg_out for k in ("xn", "ee", "comb", "usb", "slin")):
            def _(g):
                g.wait_ge(v2t.h, v2t.n)
                g.wait_ge(s2t.h, comb_done)
                ddb.wait(g)
                for nm, sbuf in (("xn", xn_sb), ("ee", E_sb), ("comb", comb_sb),
                                 ("usb", usb), ("slin", slin)):
                    if nm in dbg_out:
                        dma(g, ddb, dbg_out[nm], sbuf[:])
                g.wait_ge(ddb.h, ddb.n)
            eG(_)

    p3.close()
    if stage <= 5:
        st.close()
        return nc, dbg_out

    slout = st.enter_context(nc.sbuf_tensor("slout", [128, 2 * D], BF16))
    p5 = ExitStack()
    w2_sb = p5.enter_context(nc.sbuf_tensor("w2_sb", [128, E * 2 * D], BF16))
    slinT = p5.enter_context(nc.sbuf_tensor("slinT", [128, DCH * ES], BF16))
    h_sb = p5.enter_context(nc.sbuf_tensor("h_sb", [16, 2 * HID], BF16))
    hT_sb = p5.enter_context(nc.sbuf_tensor("hT_sb", [128, 2 * 32], BF16))
    sostg = p5.enter_context(nc.sbuf_tensor("sostg", [16, 2 * D], BF16))
    b1_sb = p5.enter_context(nc.sbuf_tensor("b1_sb", [1, E * HID], BF16))
    b2_sb = p5.enter_context(nc.sbuf_tensor("b2_sb", [1, E * D], BF16))

    with nc.Block() as blk:
        fence(blk)
        eT, eV, eS, eG, eY = blk.tensor, blk.vector, blk.scalar, blk.gpsimd, blk.sync

        w2v = {}

        def _(g):
            g.wait_ge(dcg.h, dcg.n)
            dma(g, dcg, b1_sb[:], b1r)
            dma(g, dcg, b2_sb[:], b2r)
            for e in range(E):
                sem = dw2[e % 4]
                g.wait_ge(sem.h, sem.n)
                dma(g, sem, w2_sb[:, 2048 * e:2048 * (e + 1)],
                    w2h[:, 2048 * e:2048 * (e + 1)])
                w2v[e] = (sem, sem.n)
        eG(_)
        w2_done = dcg.n

        p5s = ExitStack()
        ps_t5 = [p5s.enter_context(nc.psum_tensor(f"ps_t5{i}", [128, 512], BF16)) for i in range(2)]

        t5_ring = Ring(2)
        for c in range(DCH):
            slot = c % 2

            def _(t_, c=c, slot=slot):
                if c == 0:
                    t_.wait_ge(v2t.h, v2t.n)
                t5_ring.acquire(t_, slot)
                for half in range(2):
                    mi = t_.transpose(ps_t5[slot][:, 128 * half:128 * (half + 1)],
                                      slin[:, D * half + 128 * c: D * half + 128 * (c + 1)],
                                      ident_sb[:])
                mi.then_inc(t2v.h, 1); t2v.n += 1
            eT(_)

            def _(v, c=c, slot=slot):
                t2v.wait(v)
                v.tensor_copy(out=slinT[:, ES * c:ES * (c + 1)],
                              in_=ps_t5[slot][:, 0:ES]).then_inc(v2t.h, 1)
                v2t.n += 1
            eV(_)
            t5_ring.release(slot, v2t)
        slinT_done = v2t.n

        p5s.close()
        h_ps = [p5.enter_context(nc.psum_tensor(f"h_ps{i}", [16, HID], F32)) for i in range(2)]
        hT_ps = [p5.enter_context(nc.psum_tensor(f"hT_ps{i}", [128, 32], BF16)) for i in range(2)]
        so_ps = [p5.enter_context(nc.psum_tensor(f"so_ps{i}", [16, D], F32)) for i in range(2)]

        hps_ring, hsb_ring, hTps_ring, hT_ring = Ring(2), Ring(2), Ring(2), Ring(2)
        sops_ring, stg_ring = Ring(2), Ring(2)
        p5bat = SBat()
        for e in range(E):
            es = e % 2

            def _(t_, e=e, es=es, w2d=w2_done):
                if e == 0:
                    t_.wait_ge(v2t.h, slinT_done)
                    t_.wait_ge(dcg.h, w2d)
                hps_ring.acquire(t_, es)
                t_.matmul(h_ps[es][:, :], onesr_sb[0:1, 0:16],
                          b1_sb[0:1, HID * e:HID * (e + 1)], start=True, stop=False)
                for c in range(DCH):
                    mi = t_.matmul(h_ps[es][:, :],
                                   slinT[:, ES * c + 16 * e: ES * c + 16 * (e + 1)],
                                   w1_sb[:, 2048 * e + HID * c: 2048 * e + HID * (c + 1)],
                                   start=False, stop=(c == DCH - 1))
                mi.then_inc(t2s.h, 1); t2s.n += 1
            eT(_)

            tw = t2s.n
            rw = hsb_ring.waits(es)
            s2t.n += 1

            def _j(s, es=es, tw=tw, rw=rw):
                s.wait_ge(t2s.h, tw)
                for hh, vv in rw:
                    s.wait_ge(hh, vv)
                s.activation(out=h_sb[:, HID * es:HID * (es + 1)], in_=h_ps[es][:, :],
                             func=ACT.Gelu).then_inc(s2t.h, 1)
            p5bat.add(_j)
            hps_ring.release(es, s2t)

            def _(t_, es=es):
                s2t.wait(t_)
                hTps_ring.acquire(t_, es)
                for c in range(2):
                    mi = t_.transpose(hT_ps[es][:, 16 * c:16 * (c + 1)],
                                      h_sb[0:16, HID * es + 128 * c: HID * es + 128 * (c + 1)],
                                      ident_sb[0:16, 0:16])
                mi.then_inc(t2v.h, 1); t2v.n += 1
            eT(_)

            def _(v, es=es):
                t2v.wait(v)
                hT_ring.acquire(v, es)
                v.tensor_copy(out=hT_sb[:, 32 * es:32 * (es + 1)],
                              in_=hT_ps[es][:, :]).then_inc(v2t.h, 1)
                v2t.n += 1
            eV(_)
            hsb_ring.release(es, v2t)
            hTps_ring.release(es, v2t)

            def _(t_, e=e, es=es, wv=w2v[e]):
                v2t.wait(t_)
                t_.wait_ge(wv[0].h, wv[1])
                sops_ring.acquire(t_, es)
                for dh in range(2):
                    t_.matmul(so_ps[es][:, 512 * dh:512 * (dh + 1)], onesr_sb[0:1, 0:16],
                              b2_sb[0:1, D * e + 512 * dh: D * e + 512 * (dh + 1)],
                              start=True, stop=False)
                    for c in range(2):
                        mi = t_.matmul(so_ps[es][:, 512 * dh:512 * (dh + 1)],
                                       hT_sb[:, 32 * es + 16 * c: 32 * es + 16 * (c + 1)],
                                       w2_sb[:, 2048 * e + 1024 * c + 512 * dh: 2048 * e + 1024 * c + 512 * (dh + 1)],
                                       start=False, stop=(c == 1))
                mi.then_inc(t2v.h, 1); t2v.n += 1
            eT(_)
            hT_ring.release(es, t2v)

            def _(v, es=es):
                t2v.wait(v)
                stg_ring.acquire(v, es)
                v.tensor_copy(out=sostg[:, D * es:D * (es + 1)],
                              in_=so_ps[es][:, :]).then_inc(v2y.h, 1)
                v2y.n += 1
            eV(_)
            sops_ring.release(es, v2y)

            def _(sy, e=e, es=es):
                v2y.wait(sy)
                half, row = e // 8, 16 * (e % 8)
                dma(sy, dsg[es], slout[row:row + 16, D * half:D * (half + 1)],
                    sostg[:, D * es:D * (es + 1)])
            eY(_)
            stg_ring.release(es, dsg[es])
        p5bat.flush(eS)

        if "slout" in dbg_out:
            def _(g):
                g.wait_ge(dsg[0].h, dsg[0].n)
                g.wait_ge(dsg[1].h, dsg[1].n)
                ddb.wait(g)
                dma(g, ddb, dbg_out["slout"], slout[:])
                g.wait_ge(ddb.h, ddb.n)
            eG(_)

    p5.close()

    p6 = ExitStack()
    combT = p6.enter_context(nc.sbuf_tensor("combT", [128, 2 * TQ], BF16))
    fln = p6.enter_context(nc.sbuf_tensor("fln", [128, 2 * D], F32))
    x2rr = p6.enter_context(nc.sbuf_tensor("x2rr", [128, 2 * D], F32))
    osb = p6.enter_context(nc.sbuf_tensor("osb", [128, 2 * D], F32))
    st6e = p6.enter_context(nc.sbuf_tensor("st6e", [128, 24], F32))
    mse = p6.enter_context(nc.sbuf_tensor("mse", [128, 4], F32))
    scl6 = p6.enter_context(nc.sbuf_tensor("scl6", [128, 2], F32))
    bcl6 = p6.enter_context(nc.sbuf_tensor("bcl6", [128, 2], F32))

    with nc.Block() as blk:
        fence(blk)
        eT, eV, eS, eG, eY = blk.tensor, blk.vector, blk.scalar, blk.gpsimd, blk.sync
        ps_t6 = [p6.enter_context(nc.psum_tensor(f"ps_t6{i}", [128, 512], BF16)) for i in range(2)]
        f_ps = [p6.enter_context(nc.psum_tensor(f"f_ps{i}", [128, D], F32)) for i in range(2)]

        t6_ring = Ring(2)
        for tt in range(16):
            slot = tt % 2

            def _(t_, tt=tt, slot=slot):
                t6_ring.acquire(t_, slot)
                for half in range(2):
                    mi = t_.transpose(ps_t6[slot][:, 128 * half:128 * (half + 1)],
                                      comb_sb[:, ES * tt + 128 * half: ES * tt + 128 * (half + 1)],
                                      ident_sb[:])
                mi.then_inc(t2v.h, 1); t2v.n += 1
            eT(_)

            def _(v, tt=tt, slot=slot):
                t2v.wait(v)
                for half in range(2):
                    mi = v.tensor_copy(out=combT[:, TQ * half + 128 * tt: TQ * half + 128 * (tt + 1)],
                                       in_=ps_t6[slot][:, 128 * half:128 * (half + 1)])
                mi.then_inc(v2t.h, 1); v2t.n += 1
            eV(_)
            t6_ring.release(slot, v2t)
        combT_done = v2t.n

        fps_ring, fln_ring = Ring(2), Ring(2)
        x2rr_ring, o_ring = Ring(2), Ring(2)
        p6bat = SBat()
        for tt in range(16):
            slot = tt % 2

            def _(sy, tt=tt, slot=slot):
                if tt == 0:
                    sy.wait_ge(dsp[0].h, dsp[0].n)
                    sy.wait_ge(dsp[1].h, dsp[1].n)
                x2rr_ring.acquire(sy, slot)
                dma(sy, dx6[slot], x2rr[:, slot * D:(slot + 1) * D],
                    x2sp[128 * tt:128 * (tt + 1), :])
            eY(_)

            def _(t_, tt=tt, slot=slot):
                if tt == 0:
                    t_.wait_ge(v2t.h, combT_done)
                    t_.wait_ge(dsg[0].h, dsg[0].n)
                    t_.wait_ge(dsg[1].h, dsg[1].n)
                fps_ring.acquire(t_, slot)
                for dh in range(2):
                    for half in range(2):
                        mi = t_.matmul(f_ps[slot][:, 512 * dh:512 * (dh + 1)],
                                       combT[:, TQ * half + 128 * tt: TQ * half + 128 * (tt + 1)],
                                       slout[:, D * half + 512 * dh: D * half + 512 * (dh + 1)],
                                       start=(half == 0), stop=(half == 1))
                mi.then_inc(t2s.h, 1); t2s.n += 1
            eT(_)

            def _(v, slot=slot):
                t2s.wait(v)
                v.bn_stats(out=st6e[:, 12 * slot:12 * slot + 6],
                           in_=f_ps[slot][:, 0:512])
                v.bn_stats(out=st6e[:, 12 * slot + 6:12 * slot + 12],
                           in_=f_ps[slot][:, 512:1024])
                v.drain()
                v.bn_aggr(out=mse[:, 2 * slot:2 * slot + 2],
                          in_=st6e[:, 12 * slot:12 * slot + 12])
                v.drain()
                v.tensor_scalar(out=scl6[:, slot:slot + 1],
                                in0=mse[:, 2 * slot + 1:2 * slot + 2],
                                scalar1=LN_EPS, scalar2=None, op0=ALU.add).then_inc(v2s.h, 1); v2s.n += 1
            eV(_)

            vw = v2s.n
            s2v.n += 1

            def _j(s, slot=slot, vw=vw):
                s.wait_ge(v2s.h, vw)
                s.activation(out=scl6[:, slot:slot + 1], in_=scl6[:, slot:slot + 1],
                             func=ACT.Sqrt).then_inc(s2v.h, 1)
            p6bat.add(_j)

            def _(v, slot=slot):
                s2v.wait(v)
                v.reciprocal(out=scl6[:, slot:slot + 1], in_=scl6[:, slot:slot + 1])
                v.tensor_scalar(out=bcl6[:, slot:slot + 1],
                                in0=mse[:, 2 * slot:2 * slot + 1],
                                scalar1=-1.0, scalar2=None, op0=ALU.mult)
                v.drain()
                fln_ring.acquire(v, slot)
                v.tensor_scalar(out=fln[:, slot * D:(slot + 1) * D],
                                in0=f_ps[slot][:, :],
                                scalar1=bcl6[:, slot:slot + 1],
                                scalar2=scl6[:, slot:slot + 1],
                                op0=ALU.add, op1=ALU.mult).then_inc(v2g.h, 1)
                v2g.n += 1
            eV(_)
            fps_ring.release(slot, v2g)

            def _(g, slot=slot):
                v2g.wait(g)
                dx6[slot].wait(g)
                o_ring.acquire(g, slot)
                g.tensor_tensor(out=osb[:, slot * D:(slot + 1) * D],
                                in0=fln[:, slot * D:(slot + 1) * D],
                                in1=x2rr[:, slot * D:(slot + 1) * D],
                                op=ALU.add).then_inc(g2t.h, 1)
                g2t.n += 1
            eG(_)
            fln_ring.release(slot, g2t)
            x2rr_ring.release(slot, g2t)

            def _(sy, tt=tt, slot=slot):
                g2t.wait(sy)
                dma(sy, dso[slot], out[128 * tt:128 * (tt + 1), :],
                    osb[:, slot * D:(slot + 1) * D])
            eY(_)
            o_ring.release(slot, dso[slot])
        p6bat.flush(eS)

        def _(sy):
            sy.wait_ge(dso[0].h, dso[0].n)
            sy.wait_ge(dso[1].h, dso[1].n)
            sy.wait_ge(ddb.h, ddb.n)
        eY(_)

    p6.close()
    st.close()
    return nc, dbg_out


def make_tables(h):
    g0 = 2048 * h - 128
    pos = (g0 + np.arange(TKV)).astype(np.float64)
    invf = 1.0 / (10000.0 ** (np.arange(0, HD, 2, dtype=np.float64) / HD))
    hdmod = np.arange(128) % 64
    fi = hdmod % 32
    theta = pos[None, :] * invf[fi][:, None]
    cosr = np.cos(theta).astype(np.float32)
    sign = np.where(hdmod < 32, -1.0, 1.0)
    sinr = (np.sin(theta) * sign[:, None]).astype(np.float32)
    return cosr, sinr


def make_mask(h):
    m = np.zeros((NT, 128, 384), np.float32)
    for c in range(NT):
        q0 = qws(c)
        tk = 128 * c + np.arange(128)
        tq = q0 + np.arange(384)
        d = tq[None, :] - tk[:, None]
        valid = (d >= -128) & (d <= 127)
        posk = (2048 * h - 128) + tk
        valid &= ((posk >= 0) & (posk < 4096))[:, None]
        valid &= ((tq >= 128) & (tq < 2176))[None, :]
        m[c][valid] = 1.0
    return np.ascontiguousarray(m.transpose(1, 0, 2).reshape(128, NT * 384))


def chunk_rows(a):
    R, C = a.shape
    n = R // 128
    return np.ascontiguousarray(a.reshape(n, 128, C).transpose(1, 0, 2).reshape(128, n * C))


def make_inputs(inputs):
    x = np.asarray(inputs["L0_attn_x"], np.float32)
    w_qkv = np.asarray(inputs["L0_attn_w_qkv"], np.float32)
    w_out = np.asarray(inputs["L0_attn_w_out"], np.float32)
    b_out = np.asarray(inputs["L0_attn_b_out"], np.float32)
    gam = np.asarray(inputs["L0_ffn_ln_gamma"], np.float32)
    bet = np.asarray(inputs["L0_ffn_ln_beta"], np.float32)
    mu = np.asarray(inputs["L0_ffn_mu"], np.float32)
    scale = np.asarray(inputs["L0_ffn_scale"], np.float32)
    w1 = np.asarray(inputs["L0_ffn_w1"], np.float32)
    w2 = np.asarray(inputs["L0_ffn_w2"], np.float32)
    b1 = np.asarray(inputs["L0_ffn_b1"], np.float32)
    b2 = np.asarray(inputs["L0_ffn_b2"], np.float32)

    wq_l = chunk_rows(np.ascontiguousarray(w_qkv.T))
    wo_l = chunk_rows(np.ascontiguousarray(w_out.T))
    muh_l = chunk_rows(np.ascontiguousarray(mu.reshape(D, ES)))
    w1_l = chunk_rows(w1.reshape(E * D, HID))
    w2_l = chunk_rows(w2.reshape(E * 2 * 128, D))
    b1_l = np.ascontiguousarray(b1.reshape(1, E * HID))
    b2_l = np.ascontiguousarray(b2.reshape(1, E * D))
    bor_l = np.ascontiguousarray(b_out.reshape(1, D))
    gbc_l = np.ascontiguousarray(np.broadcast_to(gam[None, :], (128, D)))
    bbc_l = np.ascontiguousarray(np.broadcast_to(bet[None, :], (128, D)))
    scbc_l = np.ascontiguousarray(np.broadcast_to(scale.reshape(1, 1), (128, 1)))
    identb = np.eye(128, dtype=ml_dtypes.bfloat16)
    pm = np.zeros((128, 128), dtype=ml_dtypes.bfloat16)
    for mcol in range(128):
        src = mcol + 32 if (mcol % 64) < 32 else mcol - 32
        pm[src, mcol] = 1
    onesc_l = np.ones((128, 1), dtype=ml_dtypes.bfloat16)
    onesr_l = np.ones((1, 128), dtype=ml_dtypes.bfloat16)

    in_maps = []
    for c in range(8):
        b, h = c // 2, c % 2
        g0 = 2048 * h - 128
        xpad = np.zeros((TKV, D), np.float32)
        lo = max(0, -g0)
        hi = TKV - max(0, (g0 + TKV) - 4096)
        xpad[lo:hi] = x[b, g0 + lo: g0 + hi]
        cosr_l, sinr_l = make_tables(h)
        in_maps.append({
            "xp": xpad, "wq": wq_l, "wo": wo_l, "muh": muh_l,
            "w1h": w1_l, "w2h": w2_l, "b1r": b1_l, "b2r": b2_l, "bor": bor_l,
            "gbc": gbc_l, "bbc": bbc_l, "scbc": scbc_l,
            "cosr": cosr_l, "sinr": sinr_l, "maskh": make_mask(h),
            "identb": identb, "permb": pm, "onesc": onesc_l, "onesr": onesr_l,
            "onesrf": np.ones((1, 128), np.float32),
        })
    return in_maps


_NC_CACHE = {}


def kernel(**inputs):
    key = (STAGE, tuple(DBG))
    if key not in _NC_CACHE:
        _NC_CACHE[key] = build_program()
    nc, _ = _NC_CACHE[key]
    in_maps = make_inputs(inputs)
    res = run_bass_kernel_spmd(nc, in_maps, core_ids=list(range(8)))
    outp = np.empty((4, 4096, D), np.float32)
    for c in range(8):
        b, h = c // 2, c % 2
        outp[b, 2048 * h:2048 * (h + 1), :] = res.results[c]["out"]
    return outp
